# version 31
# baseline (speedup 1.0000x reference)
import numpy as np
import ml_dtypes
from contextlib import ExitStack

import concourse.bass as bass
import concourse.mybir as mybir
from concourse.bass_utils import run_bass_kernel_spmd

F32 = mybir.dt.float32
BF16 = mybir.dt.bfloat16
I32 = mybir.dt.int32
AF = mybir.ActivationFunctionType
ALU = mybir.AluOpType
AXX = mybir.AxisListType.X

D = 1024
NT = 16
NH = 16
DH = 64
TPC = 128 * NT
NCORES = 8
MAGIC = 12582912.0
CLIP = float(np.float32(128.0 - 1e-5))
INV_D = 1.0 / D
EPS12 = 1e-5
EPS3 = 1e-3
DEN_EPS = 1e-6
BETA_EPS = 1e-5
ACT_CONSTS = [EPS12, EPS3, -1.0, -0.5, 0.0, 1.0]


class Sem:
    def __init__(self, h):
        self.h = h
        self.n = 0


class Prog:
    def __init__(self):
        self.nc = bass.Bass("TRN2", target_bir_lowering=False)
        self.q = {k: [] for k in ("sync", "tensor", "vector", "scalar", "gpsimd")}
        self.es = ExitStack()

    def dram(self, name, shape, dt, kind):
        return self.nc.dram_tensor(name, shape, dt, kind=kind).ap()

    def dram_shared(self, name, shape, dt):
        return self.nc.dram_tensor(name, shape, dt, kind="Internal",
                                   addr_space="Shared")

    def sb(self, name, shape, dt):
        return self.es.enter_context(self.nc.sbuf_tensor(name, shape, dt))

    def psum(self, name):
        return self.es.enter_context(self.nc.psum_tensor(name, [128, 512], F32))

    def sem(self, name):
        return Sem(self.es.enter_context(self.nc.semaphore(name)))

    def set_engine_sems(self, **kw):
        self.eng_sem = kw

    def emit(self, eng, fn, waits=(), inc=None, n=1, auto=True):
        wl = [(s, v) for (s, v) in waits if s is not None and v]
        es = getattr(self, "eng_sem", {}).get(eng)
        if es is not None and auto:
            if es.n > 0:
                wl.append((es, es.n))
            if inc is None:
                inc = es
                n = 1

        def g(e, wl=wl, fn=fn, inc=inc, n=n):
            for s, v in wl:
                e.wait_ge(s.h, v)
            inst = fn(e)
            if inc is not None:
                inst.then_inc(inc.h, n)

        self.q[eng].append(g)
        if inc is not None:
            inc.n += n
            return inc.n
        return None

    def finalize(self):
        nc = self.nc
        with nc.Block() as blk:
            for eng in ("sync", "tensor", "vector", "scalar", "gpsimd"):
                ops = self.q[eng]
                if not ops:
                    continue
                getattr(blk, eng)(lambda e, ops=ops: [f(e) for f in ops])
        self.es.close()
        return nc


def register_consts(p, cstb):
    for j, val in enumerate(ACT_CONSTS):
        p.nc.const_aps.register(F32, float(val), cstb[:, j : j + 1])


def host_consts():
    cst = np.zeros((128, 8), np.float32)
    for j, val in enumerate(ACT_CONSTS):
        cst[:, j] = val
    return cst



def emit_ln_stats(p, t, xt, st, s_dve, s_act, s_pool, x_waits, eps,
                  sums_ready=None, cw=None):
    c = slice(t, t + 1)
    p.emit("vector", lambda e: e.scalar_tensor_tensor(
        st["sq"][:], xt, 1.0, xt, ALU.mult, ALU.mult, accum_out=st["sumsq"][:, c]),
        waits=x_waits, inc=s_dve)
    if sums_ready is None:
        p.emit("vector", lambda e: e.reduce_sum(st["sums"][:, c], xt, axis=AXX),
               waits=x_waits, inc=s_dve)
        sums_ready = (s_dve, s_dve.n)
    z = slice(0, 1)
    p.emit("vector", lambda e: e.tensor_scalar(
        st["ex2"][:, z], st["sumsq"][:, c], INV_D, None, ALU.mult), inc=s_dve)
    p.emit("vector", lambda e: e.tensor_scalar(
        st["mean"][:, c], st["sums"][:, c], INV_D, None, ALU.mult),
        waits=[sums_ready], inc=s_dve)
    p.emit("vector", lambda e: e.tensor_mul(
        st["nvar"][:, z], st["mean"][:, c], st["mean"][:, c]), inc=s_dve)
    m_nvar = p.emit("vector", lambda e: e.tensor_sub(
        st["nvar"][:, z], st["nvar"][:, z], st["ex2"][:, z]), inc=s_dve)
    lnw = [(s_dve, m_nvar)]
    if cw is not None:
        lnw.append(cw)
    p.emit("scalar", lambda e: e.activation(
        st["lnv"][:, z], st["nvar"][:, z], AF.Ln, bias=float(eps), scale=-1.0),
        waits=lnw, inc=s_act)
    m_rstd = p.emit("scalar", lambda e: e.activation(
        st["rstd"][:, c], st["lnv"][:, z], AF.Exp, bias=0.0, scale=-0.5),
        inc=s_act)
    m_nmr = p.emit("vector", lambda e: e.scalar_tensor_tensor(
        st["nmr"][:, c], st["mean"][:, c], -1.0, st["rstd"][:, c],
        ALU.mult, ALU.mult), waits=[(s_act, m_rstd)], inc=s_dve)
    return m_nmr


def make_stats(p, tag, n=NT, sq=None):
    st = {}
    for k in ("sums", "sumsq", "mean", "rstd", "nmr"):
        st[k] = p.sb(f"{tag}_{k}", [128, n], F32)
    for k in ("ex2", "nvar", "lnv"):
        st[k] = p.sb(f"{tag}_{k}", [128, 4], F32)
    st["sq"] = p.sb(f"{tag}_sq", [128, D], F32) if sq is None else sq
    return st


def emit_tile_sums(p, t, xt, st, s_dve, x_waits):
    c = slice(t, t + 1)
    p.emit("vector", lambda e: e.scalar_tensor_tensor(
        st["sq"][:], xt, 1.0, xt, ALU.mult, ALU.mult,
        accum_out=st["sumsq"][:, c]), waits=x_waits, inc=s_dve)
    return p.emit("vector", lambda e: e.reduce_sum(
        st["sums"][:, c], xt, axis=AXX), waits=x_waits, inc=s_dve)


def emit_group_stats(p, c, st, s_dve, s_act, eps, cw=None, sums_w=None):
    zc = slice(0, c.stop - c.start)
    w = [sums_w] if sums_w else []
    p.emit("vector", lambda e: e.tensor_scalar(
        st["ex2"][:, zc], st["sumsq"][:, c], INV_D, None, ALU.mult),
        waits=w, inc=s_dve)
    p.emit("vector", lambda e: e.tensor_scalar(
        st["mean"][:, c], st["sums"][:, c], INV_D, None, ALU.mult), inc=s_dve)
    p.emit("vector", lambda e: e.tensor_mul(
        st["nvar"][:, zc], st["mean"][:, c], st["mean"][:, c]), inc=s_dve)
    m_nvar = p.emit("vector", lambda e: e.tensor_sub(
        st["nvar"][:, zc], st["nvar"][:, zc], st["ex2"][:, zc]), inc=s_dve)
    lnw = [(s_dve, m_nvar)]
    if cw is not None:
        lnw.append(cw)
    p.emit("scalar", lambda e: e.activation(
        st["lnv"][:, zc], st["nvar"][:, zc], AF.Ln, bias=float(eps),
        scale=-1.0), waits=lnw, inc=s_act)
    m_rstd = p.emit("scalar", lambda e: e.activation(
        st["rstd"][:, c], st["lnv"][:, zc], AF.Exp, bias=0.0, scale=-0.5),
        inc=s_act)
    return p.emit("vector", lambda e: e.scalar_tensor_tensor(
        st["nmr"][:, c], st["mean"][:, c], -1.0, st["rstd"][:, c],
        ALU.mult, ALU.mult), waits=[(s_act, m_rstd)], inc=s_dve)


def emit_transposes(p, src_bf, tb, identb, waits, s_pe):
    m = None
    for cidx in range(8):
        bank = tb[cidx // 4]
        col = (cidx % 4) * 128
        m = p.emit("tensor", lambda e, cidx=cidx, bank=bank, col=col: e.matmul(
            bank[:, col : col + 128], src_bf[:, cidx * 128 : (cidx + 1) * 128],
            identb[:], start=True, stop=True),
            waits=waits if cidx == 0 else [],
            inc=s_pe if cidx == 7 else None)
    return m


def emit_evac_T(p, tb, dst3, slot_off, waits, s_dve):
    m = None
    for b in range(2):
        src = tb[b][:].rearrange("p (c x) -> p c x", c=4)
        dst = dst3[:, 4 * b : 4 * b + 4, slot_off : slot_off + 128]
        m = p.emit("vector", lambda e, src=src, dst=dst: e.tensor_copy(dst, src),
                   waits=waits if b == 0 else [], inc=s_dve)
    return m


def emit_eluplus1(p, jobs, scr_min, scr_exp, first_waits, s_dve, s_act,
                  prev_stt=None):
    marks = []
    for i, (bank, dst, so, jw) in enumerate(jobs):
        m_r = p.emit("scalar", lambda e, bank=bank, so=so: e.activation(
            scr_min[:, so : so + 512], bank[:],
            AF.Relu, bias=0.0, scale=-1.0), waits=jw, inc=s_act)
        m_exp = p.emit("scalar", lambda e, so=so: e.activation(
            scr_exp[:, so : so + 512], scr_min[:, so : so + 512],
            AF.Exp, bias=0.0, scale=-1.0), inc=s_act)
        marks.append(p.emit("vector", lambda e, bank=bank, dst=dst, so=so: e.scalar_tensor_tensor(
            dst, bank[:], 0.0, scr_exp[:, so : so + 512], ALU.max, ALU.add),
            waits=[(s_act, m_exp)], inc=s_dve))
    return marks


def emit_k1_broadcast_row(p, onesrow, src_row, dst, bank, first_waits, s_pe, s_dve):
    n = dst.shape[1]
    m_dve = None
    for off in range(0, n, 512):
        w = min(512, n - off)
        m_pe = p.emit("tensor", lambda e, off=off, w=w: e.matmul(
            bank[:, 0:w], onesrow, src_row[:, off : off + w],
            start=True, stop=True),
            waits=(first_waits if off == 0 else [(s_dve, m_dve)]), inc=s_pe)
        m_dve = p.emit("vector", lambda e, off=off, w=w: e.tensor_copy(
            dst[:, off : off + w], bank[:, 0:w]), waits=[(s_pe, m_pe)],
            inc=s_dve)
    return m_dve



def build_L1(dbg=False):
    p = Prog()
    x_in = p.dram("x", [TPC, D], F32, "ExternalInput")
    wq_in = p.dram("wqT", [D, D], F32, "ExternalInput")
    wk_in = p.dram("wkT", [D, D], F32, "ExternalInput")
    wv_in = p.dram("wvT", [D, D], F32, "ExternalInput")
    id_in = p.dram("identb", [128, 128], BF16, "ExternalInput")
    cst_in = p.dram("cst", [128, 8], F32, "ExternalInput")
    kvp_out = p.dram("kvp", [128, 520], F32, "ExternalOutput")
    qf_out = p.dram("qf", [128, 8 * TPC], BF16, "ExternalOutput")

    cstb = p.sb("cstb", [128, 8], F32)
    register_consts(p, cstb)
    identb = p.sb("identb_sb", [128, 128], BF16)
    x_sb = p.sb("x_sb", [128, NT, D], F32)
    wk_sb = p.sb("wk_sb", [128, 8, D], BF16)
    wv_sb = p.sb("wv_sb", [128, 8, D], BF16)
    wq_sb = p.sb("wq_sb", [128, 8, D], BF16)
    xlnT = p.sb("xlnT", [128, 8, 512], BF16)
    xln_bf = [p.sb(f"xlnbf{i}", [128, D], BF16) for i in range(2)]
    kf = [p.sb(f"kf{i}", [128, D], BF16) for i in range(2)]
    v_ext = [p.sb(f"vext{i}", [128, NH * 65], BF16) for i in range(2)]
    qf_sb = p.sb("qf_sb", [128, 8, TPC], BF16)
    kv_sb = p.sb("kv_sb", [128, 520], F32)
    st = make_stats(p, "l1")
    scr_min = p.sb("scr_min", [128, D], F32)
    scr_exp = p.sb("scr_exp", [128, D], F32)

    T0, T1 = p.psum("T0"), p.psum("T1")
    KA = (p.psum("KA0"), p.psum("KA1"))
    VA = (p.psum("VA0"), p.psum("VA1"))
    KV = (p.psum("KV0"), p.psum("KV1"))
    qbanks = [KA[0], KA[1], VA[0], VA[1]]

    s_cst, s_id = p.sem("s_cst"), p.sem("s_id")
    s_x = [p.sem(f"s_x{i}") for i in range(4)]
    s_wk, s_wv, s_wq = p.sem("s_wk"), p.sem("s_wv"), p.sem("s_wq")
    s_pe, s_dve, s_act, s_pool = (p.sem("s_pe"), p.sem("s_dve"),
                                  p.sem("s_act"), p.sem("s_pool"))
    s_out = p.sem("s_out")
    p.set_engine_sems(vector=s_dve, scalar=s_act, gpsimd=s_pool)

    p.emit("sync", lambda e: e.dma_start(out=cstb[:], in_=cst_in), inc=s_cst, n=16)
    p.emit("sync", lambda e: e.dma_start(out=identb[:], in_=id_in), inc=s_id, n=16)
    for g in range(4):
        src = x_in[512 * g : 512 * (g + 1), :].rearrange("(a p) d -> p a d", p=128)
        p.emit("sync", lambda e, g=g, src=src: e.dma_start(
            out=x_sb[:, 4 * g : 4 * g + 4, :], in_=src), inc=s_x[g], n=16)
    for w_in, w_sb, w_sem in ((wk_in, wk_sb, s_wk), (wv_in, wv_sb, s_wv),
                              (wq_in, wq_sb, s_wq)):
        src = w_in.rearrange("(c p) j -> p c j", p=128)
        p.emit("gpsimd", lambda e, w_sb=w_sb, src=src: e.dma_start(
            out=w_sb[:], in_=src), inc=w_sem, n=16, auto=False)
    for i in range(2):
        ones_ap = v_ext[i][:].rearrange("p (h e) -> p h e", h=NH)[:, :, 64:65]
        p.emit("gpsimd", lambda e, ones_ap=ones_ap: e.memset(ones_ap, 1.0),
               inc=s_pool)
    const_w = (s_cst, 16)

    mT1 = [None] * NT
    mEvT = [None] * NT
    mKstt = [None] * NT
    mVev = [None] * NT
    mKv = [None] * NT
    mQstt = [None] * 8
    mQdone = [None]

    mAdd = [None] * NT

    def emit_kv(t):
        par = t % 2
        for h in range(NH):
            pr = h // 2
            bank = KV[pr // 4]
            rb = (h % 2) * 64
            cb = (pr % 4) * 65
            w = []
            if h == 0:
                w = [(s_dve, mKstt[t]), (s_act, mVev[t]), (s_pool, 2)]
                if t >= 1:
                    w.append((s_dve, mAdd[t - 1]))
            p.emit("tensor", lambda e, h=h, bank=bank, rb=rb, cb=cb, par=par: e.matmul(
                bank[rb : rb + 64, cb : cb + 65],
                kf[par][:, h * DH : (h + 1) * DH],
                v_ext[par][:, h * 65 : (h + 1) * 65],
                start=True, stop=True),
                waits=w, inc=s_pe if h == NH - 1 else None)
        mKv[t] = s_pe.n
        for b in range(2):
            if t == 0:
                mAdd[t] = p.emit("vector", lambda e, b=b: e.tensor_copy(
                    kv_sb[:, b * 260 : (b + 1) * 260], KV[b][:, 0:260]),
                    waits=[(s_pe, mKv[t])] if b == 0 else [], inc=s_dve)
            else:
                mAdd[t] = p.emit("vector", lambda e, b=b: e.tensor_add(
                    kv_sb[:, b * 260 : (b + 1) * 260],
                    KV[b][:, 0:260], kv_sb[:, b * 260 : (b + 1) * 260]),
                    waits=[(s_pe, mKv[t])] if b == 0 else [], inc=s_dve)

    for g in range(4):
        for slot in range(4):
            t = 4 * g + slot
            par = t % 2
            xt = x_sb[:, t, :]
            x_wait = [(s_x[g], 16)]
            m_nmr = emit_ln_stats(p, t, xt, st, s_dve, s_act, s_pool,
                                  x_wait, EPS12, cw=const_w)
            xw = [(s_dve, m_nmr), const_w]
            if t >= 2:
                xw.append((s_pe, mT1[t - 2]))
            mXln = p.emit("scalar", lambda e, t=t, par=par, xt=xt: e.activation(
                xln_bf[par][:], xt, AF.Identity,
                bias=st["nmr"][:, t : t + 1], scale=st["rstd"][:, t : t + 1]),
                waits=xw, inc=s_act)
            tw = [(s_act, mXln)]
            if t == 0:
                tw.append((s_id, 16))
            if t >= 1:
                tw.append((s_dve, mEvT[t - 1]))
            mT1[t] = emit_transposes(p, xln_bf[par], (T0, T1), identb, tw, s_pe)
            if t >= 1:
                emit_kv(t - 1)
            ew = [(s_pe, mT1[t])]
            if slot == 0 and g >= 1:
                ew.append((s_pe, mQdone[0]))
            mEvT[t] = emit_evac_T(p, (T0, T1), xlnT, slot * 128, ew, s_dve)
            mK = mV = None
            for proj, (w_sb, banks) in enumerate(((wk_sb, KA), (wv_sb, VA))):
                for half in range(2):
                    bank = banks[half]
                    for cidx in range(8):
                        w = []
                        if half == 0 and cidx == 0:
                            w = [(s_dve, mEvT[t]), (s_wk if proj == 0 else s_wv, 16)]
                            if t >= 1:
                                w.append((s_dve, mKstt[t - 1]) if proj == 0
                                         else (s_act, mVev[t - 1]))
                        p.emit("tensor", lambda e, bank=bank, cidx=cidx, half=half,
                               slot=slot, w_sb=w_sb: e.matmul(
                            bank[:],
                            xlnT[:, cidx, slot * 128 : slot * 128 + 128],
                            w_sb[:, cidx, half * 512 : half * 512 + 512],
                            start=(cidx == 0), stop=(cidx == 7)),
                            waits=w,
                            inc=s_pe if (half == 1 and cidx == 7) else None)
                if proj == 0:
                    mK = s_pe.n
                else:
                    mV = s_pe.n
            kw = [(s_pe, mK)]
            if t >= 2:
                kw.append((s_pe, mKv[t - 2]))
            mKstt[t] = emit_eluplus1(
                p, [(KA[0], kf[par][:, 0:512], 0), (KA[1], kf[par][:, 512:1024], 512)],
                scr_min, scr_exp, kw, s_dve, s_act)
            for half in range(2):
                dst = v_ext[par][:].rearrange("p (h e) -> p h e", h=NH)[
                    :, 8 * half : 8 * half + 8, 0:64]
                src = VA[half][:].rearrange("p (h e) -> p h e", h=8)
                w = []
                if half == 0:
                    w = [(s_pe, mV), const_w]
                    if t >= 2:
                        w.append((s_pe, mKv[t - 2]))
                mVev[t] = p.emit("scalar", lambda e, dst=dst, src=src: e.activation(
                    dst, src, AF.Identity, bias=0.0, scale=1.0),
                    waits=w, inc=s_act)
        t_last = 4 * g + 3
        if g == 3:
            emit_kv(t_last)
        for co in range(8):
            bank = qbanks[co % 4]
            w = []
            if co == 0:
                w = [(s_wq, 16), (s_dve, mKstt[t_last])]
            elif co == 2:
                w = [(s_act, mVev[t_last])]
            elif co >= 4:
                w = [(s_dve, mQstt[co - 4])]
            for cidx in range(8):
                p.emit("tensor", lambda e, bank=bank, cidx=cidx, co=co: e.matmul(
                    bank[:],
                    wq_sb[:, cidx, co * 128 : co * 128 + 128],
                    xlnT[:, cidx, :],
                    start=(cidx == 0), stop=(cidx == 7)),
                    waits=w if cidx == 0 else [],
                    inc=s_pe if cidx == 7 else None)
            mQ = s_pe.n
            mQstt[co] = emit_eluplus1(
                p, [(bank, qf_sb[:, co, g * 512 : g * 512 + 512], 0)],
                scr_min, scr_exp, [(s_pe, mQ)], s_dve, s_act)
        mQdone[0] = s_pe.n

    p.emit("sync", lambda e: e.dma_start(out=kvp_out, in_=kv_sb[:]),
           waits=[(s_dve, mAdd[NT - 1])], inc=s_out, n=16)
    p.emit("sync", lambda e: e.dma_start(
        out=qf_out, in_=qf_sb[:].rearrange("p c t -> p (c t)")),
        waits=[(s_dve, mQstt[7])], inc=s_out, n=16)
    nw = 32
    if dbg:
        dbg_o = p.dram("dbg", [128, 6 * NT], F32, "ExternalOutput")
        dbgT_o = p.dram("dbgT", [128, 8 * 512], BF16, "ExternalOutput")
        for j, key in enumerate(("sums", "sumsq", "mean", "nvar", "rstd", "nmr")):
            p.emit("sync", lambda e, j=j, key=key: e.dma_start(
                out=dbg_o[:, j * NT : (j + 1) * NT], in_=st[key][:]),
                inc=s_out, n=16)
        p.emit("sync", lambda e: e.dma_start(
            out=dbgT_o, in_=xlnT[:].rearrange("p c t -> p (c t)")),
            inc=s_out, n=16)
        dbg2_o = p.dram("dbg2", [128, 128], BF16, "ExternalOutput")
        dbg3_o = p.dram("dbg3", [128, 8 * D], BF16, "ExternalOutput")
        dbg4_o = p.dram("dbg4", [128, D], F32, "ExternalOutput")
        p.emit("sync", lambda e: e.dma_start(out=dbg2_o, in_=identb[:]),
               inc=s_out, n=16)
        p.emit("sync", lambda e: e.dma_start(
            out=dbg3_o, in_=wk_sb[:].rearrange("p c j -> p (c j)")),
            inc=s_out, n=16)
        p.emit("sync", lambda e: e.dma_start(out=dbg4_o, in_=x_sb[:, 15, :]),
               inc=s_out, n=16)
        nw += 10 * 16
    p.emit("sync", lambda e: e.wait_ge(s_out.h, nw))
    return p



def build_L2(dbg=False):
    p = Prog()
    x_in = p.dram("x", [TPC, D], F32, "ExternalInput")
    qf_in = p.dram("qf", [128, 8 * TPC], BF16, "ExternalInput")
    kvall_in = p.dram("kvall", [NCORES * 128, 520], F32, "ExternalInput")
    wo_in = p.dram("woT", [D, D], F32, "ExternalInput")
    vecs_in = p.dram("vecs", [1, 4 * D + 128], F32, "ExternalInput")
    id_in = p.dram("identb", [128, 128], BF16, "ExternalInput")
    idf_in = p.dram("identf", [128, 128], F32, "ExternalInput")
    cst_in = p.dram("cst", [128, 8], F32, "ExternalInput")
    msk_in = p.dram("msk", [128, 2], F32, "ExternalInput")
    x2_out = p.dram("x2", [TPC, D], F32, "ExternalOutput")
    ab_out = p.dram("ab", [128, 2 * NT], F32, "ExternalOutput")
    gp_out = p.dram("gp", [1, 8], F32, "ExternalOutput")

    cstb = p.sb("cstb", [128, 8], F32)
    register_consts(p, cstb)
    identb = p.sb("identb_sb", [128, 128], BF16)
    identf = p.sb("identf_sb", [128, 128], F32)
    mskb = p.sb("mskb", [128, 2], F32)
    x_sb = p.sb("x_sb", [128, NT, D], F32)
    qf_sb = p.sb("qf_sb", [128, 8, TPC], BF16)
    kvall = p.sb("kvall_sb", [128, 8, 520], F32)
    kvsel = p.sb("kvsel", [128, 520], F32)
    kvb = p.sb("kvb", [128, 520], F32)
    R_bf = p.sb("R_bf", [128, 8 * 130], BF16)
    wo_sb = p.sb("wo_sb", [128, 8, D], BF16)
    vecs = p.sb("vecs_sb", [1, 4 * D + 128], F32)
    bo_full = p.sb("bo_full", [128, D], F32)
    g_full = p.sb("g_full", [128, D], F32)
    b_full = p.sb("b_full", [128, D], F32)
    den = p.sb("den", [128, 16], F32)
    dinv = p.sb("dinv", [128, 16], F32)
    oh_bf = [p.sb(f"oh{i}", [128, D], BF16) for i in range(2)]
    ohT = [p.sb(f"ohT{i}", [128, 8, 128], BF16) for i in range(2)]
    tmp = p.sb("tmp", [128, D], F32)
    xln2 = p.sb("xln2", [128, D], F32)
    st = make_stats(p, "l2")
    l3sum = p.sb("l3sum", [128, NT], F32)
    l3ex2 = p.sb("l3ex2", [128, NT], F32)
    l3sumsq = p.sb("l3sumsq", [128, NT], F32)
    mean3 = p.sb("mean3", [128, NT], F32)
    nvar3 = p.sb("nvar3", [128, NT], F32)
    lnv3 = p.sb("lnv3", [128, NT], F32)
    rstd3 = p.sb("rstd3", [128, NT], F32)
    nm3r = p.sb("nm3r", [128, NT], F32)
    u3 = p.sb("u3", [128, NT], F32)
    ab = p.sb("ab_b", [128, 2 * NT], F32)
    xnb = p.sb("xnb", [128, D], F32)
    gm = p.sb("gm", [128, NT], F32)
    gmx = p.sb("gmx", [128, 1], F32)
    gp_sb = p.sb("gp_sb", [1, 8], F32)

    QA, QB, QC = p.psum("QA"), p.psum("QB"), p.psum("QC")
    T0, T1 = p.psum("T0"), p.psum("T1")
    WA, WB = p.psum("WA"), p.psum("WB")

    s_cst, s_id, s_idf = p.sem("s_cst"), p.sem("s_id"), p.sem("s_idf")
    s_msk, s_vec = p.sem("s_msk"), p.sem("s_vec")
    s_x = [p.sem(f"s_x{i}") for i in range(4)]
    s_qf, s_kv, s_wo = p.sem("s_qf"), p.sem("s_kv"), p.sem("s_wo")
    s_pe, s_dve, s_act, s_pool = (p.sem("s_pe"), p.sem("s_dve"),
                                  p.sem("s_act"), p.sem("s_pool"))
    s_out = p.sem("s_out")
    p.set_engine_sems(vector=s_dve, scalar=s_act, gpsimd=s_pool)

    for dst, src, sm in ((cstb, cst_in, s_cst), (identb, id_in, s_id),
                         (identf, idf_in, s_idf), (mskb, msk_in, s_msk),
                         (vecs, vecs_in, s_vec)):
        p.emit("sync", lambda e, dst=dst, src=src: e.dma_start(out=dst[:], in_=src),
               inc=sm, n=16)
    for g in range(4):
        src = x_in[512 * g : 512 * (g + 1), :].rearrange("(a p) d -> p a d", p=128)
        p.emit("sync", lambda e, src=src, g=g: e.dma_start(
            out=x_sb[:, 4 * g : 4 * g + 4, :], in_=src), inc=s_x[g], n=16)
    p.emit("sync", lambda e: e.dma_start(
        out=qf_sb[:].rearrange("p c t -> p (c t)"), in_=qf_in), inc=s_qf, n=16)
    p.emit("sync", lambda e: e.dma_start(
        out=kvall[:], in_=kvall_in.rearrange("(r p) c -> p r c", p=128)),
        inc=s_kv, n=16)
    p.emit("gpsimd", lambda e: e.dma_start(
        out=wo_sb[:], in_=wo_in.rearrange("(c p) j -> p c j", p=128)),
        inc=s_wo, n=16, auto=False)
    const_w = (s_cst, 16)

    p.emit("vector", lambda e: e.tensor_add(
        kvsel[:], kvall[:, 0, :], kvall[:, 1, :]), waits=[(s_kv, 16)],
        inc=s_dve)
    for r in (2, 3):
        p.emit("vector", lambda e, r=r: e.tensor_add(
            kvsel[:], kvsel[:], kvall[:, r, :]), inc=s_dve)
    p.emit("vector", lambda e: e.tensor_add(
        kvb[:], kvall[:, 4, :], kvall[:, 5, :]), inc=s_dve)
    for r in (6, 7):
        p.emit("vector", lambda e, r=r: e.tensor_add(
            kvb[:], kvb[:], kvall[:, r, :]), inc=s_dve)
    p.emit("vector", lambda e: e.tensor_scalar(
        kvsel[:], kvsel[:], mskb[:, 0:1], None, ALU.mult),
        waits=[(s_msk, 16)], inc=s_dve)
    m_sel = p.emit("vector", lambda e: e.scalar_tensor_tensor(
        kvsel[:], kvb[:], mskb[:, 1:2], kvsel[:], ALU.mult, ALU.add), inc=s_dve)
    m_gz = p.emit("gpsimd", lambda e: e.memset(gp_sb[:], 0.0), inc=s_pool)
    m_z = p.emit("gpsimd", lambda e: e.memset(R_bf, 0.0), inc=s_pool)
    mR = None
    for cc in range(8):
        for pr in range(2):
            rs = slice(pr * 64, pr * 64 + 64)
            dcol = 130 * cc + pr * 65
            scol = 65 * cc
            mR = p.emit("gpsimd", lambda e, rs=rs, dcol=dcol, scol=scol: e.tensor_copy(
                R_bf[rs, dcol : dcol + 65], kvsel[rs, scol : scol + 65]),
                waits=[(s_dve, m_sel), (s_pool, m_z)] if (cc == 0 and pr == 0) else [],
                inc=s_pool)

    onesrow = vecs[0:1, 4 * D : 4 * D + 128]
    m_bo = emit_k1_broadcast_row(p, onesrow, vecs[0:1, 0:D], bo_full, QA,
                                 [(s_vec, 16)], s_pe, s_dve)
    m_bc1 = emit_k1_broadcast_row(p, onesrow, vecs[0:1, D : 2 * D], g_full, QA,
                                  [(s_dve, m_bo)], s_pe, s_dve)
    m_bc2 = emit_k1_broadcast_row(p, onesrow, vecs[0:1, 2 * D : 3 * D], b_full,
                                  QA, [(s_dve, m_bc1)], s_pe, s_dve)

    qk_banks = [(QA, 0), (QA, 130), (QA, 260), (QB, 0), (QB, 130), (QB, 260),
                (QC, 0), (QC, 130)]
    mDiv = [None] * NT
    mT2 = [None] * NT
    mEvT2 = [None] * NT
    mWo = [None] * NT
    mRes = [None] * NT
    m_sq3_prev = None
    m_gm_prev = None

    for t in range(NT):
        par = t % 2
        xt = x_sb[:, t, :]
        for cc in range(8):
            bank, coff = qk_banks[cc]
            w = []
            if cc == 0:
                if t == 0:
                    w = [(s_pool, mR), (s_qf, 16), (s_dve, m_bc2)]
                else:
                    w = [(s_dve, mDiv[t - 1])]
            p.emit("tensor", lambda e, bank=bank, coff=coff, cc=cc, t=t: e.matmul(
                bank[:, coff : coff + 130],
                qf_sb[:, cc, t * 128 : t * 128 + 128],
                R_bf[:, 130 * cc : 130 * cc + 130],
                start=True, stop=True),
                waits=w, inc=s_pe if cc == 7 else None)
        mQkv = s_pe.n

        for gi, (bank, nch, c0) in enumerate(((QA, 3, 0), (QB, 3, 3), (QC, 2, 6))):
            srcg = bank[:, 0 : 130 * nch].rearrange(
                "p (c q e) -> p c q e", c=nch, q=2)[:, :, :, 64:65]
            dstg = den[:, 2 * c0 : 2 * (c0 + nch)].rearrange(
                "p (c q) -> p c q", c=nch).unsqueeze(3)
            p.emit("vector", lambda e, srcg=srcg, dstg=dstg: e.tensor_scalar(
                dstg, srcg, DEN_EPS, None, ALU.add),
                waits=[(s_pe, mQkv)] if gi == 0 else [], inc=s_dve)
        p.emit("vector", lambda e: e.reciprocal(dinv[:], den[:]), inc=s_dve)

        ow = []
        if t >= 2:
            ow = [(s_pe, mT2[t - 2])]
        for i, (bank, nch, c0) in enumerate(((QA, 3, 0), (QB, 3, 3), (QC, 2, 6))):
            src = bank[:, 0 : 130 * nch].rearrange(
                "p (c q e) -> p c q e", c=nch, q=2)[:, :, :, 0:64]
            dst = oh_bf[par][:, 128 * c0 : 128 * (c0 + nch)].rearrange(
                "p (c q e) -> p c q e", c=nch, q=2)
            di = dinv[:, 2 * c0 : 2 * (c0 + nch)].rearrange(
                "p (c q) -> p c q", c=nch).unsqueeze(3).broadcast_to(
                [128, nch, 2, 64])
            mDiv[t] = p.emit("vector", lambda e, src=src, dst=dst, di=di: e.tensor_tensor(
                dst, src, di, ALU.mult), waits=ow if i == 0 else [], inc=s_dve)

        tw = [(s_dve, mDiv[t])]
        if t == 0:
            tw.append((s_id, 16))
        if t >= 1:
            tw.append((s_dve, mEvT2[t - 1]))
        mT2[t] = emit_transposes(p, oh_bf[par], (T0, T1), identb, tw, s_pe)
        ew = [(s_pe, mT2[t])]
        if t >= 2:
            ew.append((s_pe, mWo[t - 2]))
        mEvT2[t] = emit_evac_T(p, (T0, T1), ohT[par], 0, ew, s_dve)
        for half in range(2):
            bank = (WA, WB)[half]
            for cidx in range(8):
                w = []
                if half == 0 and cidx == 0:
                    w = [(s_dve, mEvT2[t]), (s_wo, 16)]
                    if t >= 1:
                        w.append((s_dve, mRes[t - 1]))
                p.emit("tensor", lambda e, bank=bank, cidx=cidx, half=half, par=par: e.matmul(
                    bank[:], ohT[par][:, cidx, :],
                    wo_sb[:, cidx, half * 512 : half * 512 + 512],
                    start=(cidx == 0), stop=(cidx == 7)),
                    waits=w, inc=s_pe if (half == 1 and cidx == 7) else None)
        mWo[t] = s_pe.n
        for half in range(2):
            bank = (WA, WB)[half]
            p.emit("vector", lambda e, bank=bank, half=half: e.tensor_add(
                tmp[:, half * 512 : half * 512 + 512], bank[:],
                bo_full[:, half * 512 : half * 512 + 512]),
                waits=[(s_pe, mWo[t])] if half == 0 else [], inc=s_dve)
        mRes[t] = p.emit("vector", lambda e, xt=xt, t=t: e.scalar_tensor_tensor(
            xt, tmp[:], 1.0, xt, ALU.mult, ALU.add,
            accum_out=st["sums"][:, t : t + 1]),
            waits=[(s_x[t // 4], 16)], inc=s_dve)
        m_nmr = emit_ln_stats(p, t, xt, st, s_dve, s_act, s_pool,
                              [(s_dve, mRes[t])], EPS12,
                              sums_ready=(s_dve, mRes[t]), cw=const_w)
        c = slice(t, t + 1)
        xw = [(s_dve, m_nmr)]
        if m_sq3_prev is not None:
            xw.append((s_dve, m_sq3_prev))
        m_x2t = p.emit("scalar", lambda e, xt=xt, c=c: e.activation(
            xln2[:], xt, AF.Identity, bias=st["nmr"][:, c],
            scale=st["rstd"][:, c], accum_out=l3sum[:, c]),
            waits=xw, inc=s_act)
        m_sq3 = p.emit("vector", lambda e, c=c: e.scalar_tensor_tensor(
            st["sq"][:], xln2[:], 1.0, xln2[:], ALU.mult, ALU.mult,
            accum_out=l3sumsq[:, c]), waits=[(s_act, m_x2t)], inc=s_dve)
        m_sq3_prev = m_sq3
        p.emit("vector", lambda e, c=c: e.tensor_scalar(
            l3ex2[:, c], l3sumsq[:, c], INV_D, None, ALU.mult), inc=s_dve)
        p.emit("vector", lambda e, c=c: e.tensor_scalar(
            mean3[:, c], l3sum[:, c], INV_D, None, ALU.mult), inc=s_dve)
        p.emit("vector", lambda e, c=c: e.tensor_mul(
            nvar3[:, c], mean3[:, c], mean3[:, c]), inc=s_dve)
        m_nv3 = p.emit("vector", lambda e, c=c: e.tensor_sub(
            nvar3[:, c], nvar3[:, c], l3ex2[:, c]), inc=s_dve)
        p.emit("scalar", lambda e, c=c: e.activation(
            lnv3[:, c], nvar3[:, c], AF.Ln, bias=EPS3, scale=-1.0),
            waits=[(s_dve, m_nv3)], inc=s_act)
        m_r3 = p.emit("scalar", lambda e, c=c: e.activation(
            rstd3[:, c], lnv3[:, c], AF.Exp, bias=0.0, scale=-0.5), inc=s_act)
        m_nm3 = p.emit("vector", lambda e, c=c: e.scalar_tensor_tensor(
            nm3r[:, c], mean3[:, c], -1.0, rstd3[:, c], ALU.mult, ALU.mult),
            waits=[(s_act, m_r3)], inc=s_dve)
        p.emit("vector", lambda e, c=c, t=t: e.tensor_mul(
            ab[:, 2 * t : 2 * t + 1], st["rstd"][:, c], rstd3[:, c]), inc=s_dve)
        p.emit("vector", lambda e, c=c: e.scalar_tensor_tensor(
            u3[:, c], mean3[:, c], -1.0, st["nmr"][:, c], ALU.mult, ALU.add),
            inc=s_dve)
        p.emit("vector", lambda e, c=c, t=t: e.tensor_mul(
            ab[:, 2 * t + 1 : 2 * t + 2], u3[:, c], rstd3[:, c]), inc=s_dve)
        m_sq3_prev = p.emit("vector", lambda e, c=c: e.tensor_scalar(
            tmp[:], xln2[:], rstd3[:, c], nm3r[:, c], ALU.mult, ALU.add),
            waits=[(s_dve, m_nm3)], inc=s_dve)
        xgw = [(s_dve, m_bc1)]
        m_xg = p.emit("vector", lambda e: e.scalar_tensor_tensor(
            xnb[:], tmp[:], 1.0, g_full[:], ALU.mult, ALU.mult),
            waits=xgw, inc=s_dve)
        m_add = p.emit("gpsimd", lambda e: e.tensor_add(xnb[:], xnb[:], b_full[:]),
               waits=[(s_dve, m_xg)], inc=s_pool)
        m_gm_prev = p.emit("vector", lambda e, c=c: e.reduce_max(
            gm[:, c], xnb[:], axis=AXX, apply_absolute_value=True),
            waits=[(s_pool, m_add)], inc=s_dve)

    m_gmx = p.emit("vector", lambda e: e.reduce_max(gmx[:], gm[:], axis=AXX),
                   inc=s_dve)
    m_tp = p.emit("tensor", lambda e: e.matmul(
        T0[0:1, 0:128], gmx[:], identf[:], start=True, stop=True),
        waits=[(s_dve, s_dve.n), (s_idf, 16)], inc=s_pe)
    m_gp = p.emit("vector", lambda e: e.reduce_max(
        gp_sb[0:1, 0:1], T0[0:1, 0:128], axis=AXX), waits=[(s_pe, m_tp)],
        inc=s_dve)

    for g in range(4):
        dst = x2_out[512 * g : 512 * (g + 1), :].rearrange("(a p) d -> p a d", p=128)
        p.emit("sync", lambda e, dst=dst, g=g: e.dma_start(
            out=dst, in_=x_sb[:, 4 * g : 4 * g + 4, :]),
            waits=[(s_dve, mRes[NT - 1])] if g == 0 else [], inc=s_out, n=16)
    p.emit("sync", lambda e: e.dma_start(out=ab_out, in_=ab[:]),
           waits=[(s_dve, s_dve.n)], inc=s_out, n=16)
    p.emit("sync", lambda e: e.dma_start(out=gp_out, in_=gp_sb[:]),
           waits=[(s_dve, m_gp), (s_pool, m_gz)], inc=s_out, n=16)
    nw = 96
    if dbg:
        d1 = p.dram("d_kvsel", [128, 520], F32, "ExternalOutput")
        d2 = p.dram("d_R", [128, 8 * 130], BF16, "ExternalOutput")
        d3 = p.dram("d_den", [128, 32], F32, "ExternalOutput")
        p.emit("sync", lambda e: e.dma_start(out=d1, in_=kvsel[:]), inc=s_out, n=16)
        p.emit("sync", lambda e: e.dma_start(out=d2, in_=R_bf[:]), inc=s_out, n=16)
        p.emit("sync", lambda e: e.dma_start(out=d3[:, 0:16], in_=den[:]), inc=s_out, n=16)
        p.emit("sync", lambda e: e.dma_start(out=d3[:, 16:32], in_=dinv[:]), inc=s_out, n=16)
        nw += 64
    p.emit("sync", lambda e: e.wait_ge(s_out.h, nw))
    return p



def build_L3(dbg=False):
    p = Prog()
    x2_in = p.dram("x2", [TPC, D], F32, "ExternalInput")
    ab_in = p.dram("ab", [128, 2 * NT], F32, "ExternalInput")
    gp_in = p.dram("gps", [1, 8], F32, "ExternalInput")
    wf_in = p.dram("wf", [D, D], F32, "ExternalInput")
    vecs_in = p.dram("vecs", [1, 4 * D + 128], F32, "ExternalInput")
    id_in = p.dram("identb", [128, 128], BF16, "ExternalInput")
    cst_in = p.dram("cst", [128, 8], F32, "ExternalInput")
    out_o = p.dram("out", [TPC, D], F32, "ExternalOutput")

    cstb = p.sb("cstb", [128, 8], F32)
    register_consts(p, cstb)
    identb = p.sb("identb_sb", [128, 128], BF16)
    x_sb = p.sb("x_sb", [128, NT, D], F32)
    ab_sb = p.sb("ab_sb", [128, 2 * NT], F32)
    gp_sb = p.sb("gp_sb", [1, 8], F32)
    wf_sb = p.sb("wf_sb", [128, 8, D], F32)
    wq_sb = p.sb("wq_sb", [128, 8, D], BF16)
    vecs = p.sb("vecs_sb", [1, 4 * D + 128], F32)
    g_full = p.sb("g_full", [128, D], F32)
    b_full = p.sb("b_full", [128, D], F32)
    scal = p.sb("scal", [1, 8], F32)
    sxq_b = p.sb("sxq_b", [128, 1], F32)
    sbe_b = p.sb("sbe_b", [128, 1], F32)
    sy_b = p.sb("sy_b", [128, 1], F32)
    bsum = p.sb("bsum", [128, 8], F32)
    bsum1 = p.sb("bsum1", [128, 1], F32)
    wscr = p.sb("wscr", [128, D], F32)
    tmp = p.sb("tmp", [128, D], F32)
    tmp2 = p.sb("tmp2", [128, D], F32)
    xq_bf = [p.sb(f"xq{i}", [128, D], BF16) for i in range(2)]
    xqT = [p.sb(f"xqT{i}", [128, 8, 128], BF16) for i in range(2)]
    ytmp = p.sb("ytmp", [128, D], F32)

    T0, T1 = p.psum("T0"), p.psum("T1")
    FA, FB = p.psum("FA"), p.psum("FB")
    BC = p.psum("BC")

    s_cst, s_id, s_ab2 = p.sem("s_cst"), p.sem("s_id"), p.sem("s_ab2")
    s_gp, s_vec = p.sem("s_gp"), p.sem("s_vec")
    s_x = [p.sem(f"s_x{i}") for i in range(4)]
    s_wf = p.sem("s_wf")
    s_pe, s_dve, s_act, s_pool = (p.sem("s_pe"), p.sem("s_dve"),
                                  p.sem("s_act"), p.sem("s_pool"))
    s_out = p.sem("s_out")
    p.set_engine_sems(vector=s_dve, scalar=s_act, gpsimd=s_pool)

    for dst, src, sm in ((cstb, cst_in, s_cst), (identb, id_in, s_id),
                         (ab_sb, ab_in, s_ab2), (gp_sb, gp_in, s_gp),
                         (vecs, vecs_in, s_vec)):
        p.emit("sync", lambda e, dst=dst, src=src: e.dma_start(out=dst[:], in_=src),
               inc=sm, n=16)
    for g in range(4):
        src = x2_in[512 * g : 512 * (g + 1), :].rearrange("(a p) d -> p a d", p=128)
        p.emit("sync", lambda e, src=src, g=g: e.dma_start(
            out=x_sb[:, 4 * g : 4 * g + 4, :], in_=src), inc=s_x[g], n=16)
    m_wf = p.emit("sync", lambda e: e.dma_start(
        out=wf_sb[:], in_=wf_in.rearrange("(c p) j -> p c j", p=128)),
        inc=s_wf, n=16)
    const_w = (s_cst, 16)

    onesrow = vecs[0:1, 4 * D : 4 * D + 128]
    p.emit("vector", lambda e: e.reduce_max(
        scal[0:1, 0:1], gp_sb[0:1, :], axis=AXX), waits=[(s_gp, 16)], inc=s_dve)
    p.emit("vector", lambda e: e.reciprocal(scal[0:1, 1:2], scal[0:1, 0:1]),
           inc=s_dve)
    p.emit("vector", lambda e: e.tensor_scalar(
        scal[0:1, 1:2], scal[0:1, 1:2], 128.0, None, ALU.mult), inc=s_dve)
    for cidx in range(8):
        p.emit("vector", lambda e, cidx=cidx: e.reduce_sum(
            bsum[:, cidx : cidx + 1], wf_sb[:, cidx, :], axis=AXX,
            apply_absolute_value=True),
            waits=[(s_wf, 16)] if cidx == 0 else [], inc=s_dve)
    m_bs1 = p.emit("vector", lambda e: e.reduce_sum(bsum1[:], bsum[:], axis=AXX),
                   inc=s_dve)
    m_bmm = p.emit("tensor", lambda e: e.matmul(
        BC[0:1, 0:1], bsum1[:], cstb[:, 5:6], start=True, stop=True),
        waits=[(s_dve, m_bs1), const_w], inc=s_pe)
    p.emit("vector", lambda e: e.tensor_scalar(
        scal[0:1, 2:3], BC[0:1, 0:1], 1.0 / (D * D), None, ALU.mult),
        waits=[(s_pe, m_bmm)], inc=s_dve)
    p.emit("vector", lambda e: e.tensor_scalar(
        scal[0:1, 3:4], scal[0:1, 2:3], BETA_EPS, None, ALU.add), inc=s_dve)
    p.emit("vector", lambda e: e.reciprocal(scal[0:1, 3:4], scal[0:1, 3:4]),
           inc=s_dve)
    p.emit("vector", lambda e: e.tensor_mul(
        scal[0:1, 4:5], scal[0:1, 0:1], scal[0:1, 2:3]), inc=s_dve)
    m_sy = p.emit("vector", lambda e: e.tensor_scalar(
        scal[0:1, 4:5], scal[0:1, 4:5], 1.0 / 128.0, None, ALU.mult), inc=s_dve)
    m_prev = m_sy
    for col, dstb in ((1, sxq_b), (3, sbe_b), (4, sy_b)):
        m_pe = p.emit("tensor", lambda e, col=col: e.matmul(
            BC[:, 0:1], onesrow, scal[0:1, col : col + 1],
            start=True, stop=True),
            waits=[(s_dve, m_prev), (s_vec, 16)], inc=s_pe)
        m_prev = p.emit("vector", lambda e, dstb=dstb: e.tensor_copy(
            dstb[:], BC[:, 0:1]), waits=[(s_pe, m_pe)], inc=s_dve)
    m_bcasts = m_prev
    m_wq = None
    for cidx in range(8):
        p.emit("gpsimd", lambda e, cidx=cidx: e.tensor_scalar(
            wscr[:], wf_sb[:, cidx, :], sbe_b[:], MAGIC, ALU.mult, ALU.add),
            waits=[(s_dve, m_bcasts)] if cidx == 0 else [], inc=s_pool)
        p.emit("gpsimd", lambda e: e.tensor_scalar(
            wscr[:], wscr[:], MAGIC, 1.0, ALU.subtract, ALU.min), inc=s_pool)
        m_wq = p.emit("gpsimd", lambda e, cidx=cidx: e.tensor_scalar(
            wq_sb[:, cidx, :], wscr[:], -1.0, None, ALU.max), inc=s_pool)
    m_bc1 = emit_k1_broadcast_row(p, onesrow, vecs[0:1, D : 2 * D], g_full, BC,
                                  [(s_vec, 16), (s_dve, m_bcasts)], s_pe, s_dve)
    m_bc2 = emit_k1_broadcast_row(p, onesrow, vecs[0:1, 2 * D : 3 * D], b_full,
                                  BC, [(s_dve, m_bc1)], s_pe, s_dve)

    mT3 = [None] * NT
    mEvT3 = [None] * NT
    mF = [None] * NT
    mRes = [None] * NT
    m_xq_prev = None
    m_y_prev = None
    for t in range(NT):
        par = t % 2
        xt = x_sb[:, t, :]
        w0 = [(s_x[t // 4], 16), (s_ab2, 16), (s_dve, m_bc2)]
        if m_xq_prev is not None:
            w0.append((s_pool, m_xq_prev))
        p.emit("vector", lambda e, xt=xt, t=t: e.tensor_scalar(
            tmp[:], xt, ab_sb[:, 2 * t : 2 * t + 1],
            ab_sb[:, 2 * t + 1 : 2 * t + 2], ALU.mult, ALU.add),
            waits=w0, inc=s_dve)
        p.emit("vector", lambda e: e.scalar_tensor_tensor(
            tmp[:], tmp[:], 1.0, g_full[:], ALU.mult, ALU.mult), inc=s_dve)
        p.emit("vector", lambda e: e.tensor_add(tmp[:], tmp[:], b_full[:]),
               inc=s_dve)
        m_clip = p.emit("vector", lambda e: e.tensor_scalar(
            tmp2[:], tmp[:], sxq_b[:], -CLIP, ALU.mult, ALU.max), inc=s_dve)
        xqw = [(s_dve, m_clip)]
        if t >= 2:
            xqw.append((s_pe, mT3[t - 2]))
        m_xq_prev = p.emit("gpsimd", lambda e, par=par: e.tensor_scalar(
            xq_bf[par][:], tmp2[:], CLIP, None, ALU.min), waits=xqw, inc=s_pool)
        tw = [(s_pool, m_xq_prev)]
        if t == 0:
            tw.append((s_id, 16))
        if t >= 1:
            tw.append((s_dve, mEvT3[t - 1]))
        mT3[t] = emit_transposes(p, xq_bf[par], (T0, T1), identb, tw, s_pe)
        ew = [(s_pe, mT3[t])]
        if t >= 2:
            ew.append((s_pe, mF[t - 2]))
        mEvT3[t] = emit_evac_T(p, (T0, T1), xqT[par], 0, ew, s_dve)
        for half in range(2):
            bank = (FA, FB)[half]
            for cidx in range(8):
                w = []
                if half == 0 and cidx == 0:
                    w = [(s_dve, mEvT3[t]), (s_pool, m_wq)]
                    if m_y_prev is not None:
                        w.append((s_act, m_y_prev))
                p.emit("tensor", lambda e, bank=bank, cidx=cidx, half=half, par=par: e.matmul(
                    bank[:], xqT[par][:, cidx, :],
                    wq_sb[:, cidx, half * 512 : half * 512 + 512],
                    start=(cidx == 0), stop=(cidx == 7)),
                    waits=w, inc=s_pe if (half == 1 and cidx == 7) else None)
        mF[t] = s_pe.n
        for half in range(2):
            bank = (FA, FB)[half]
            w = [(s_pe, mF[t])] if half == 0 else []
            if half == 0 and t == 0:
                w.append((s_cst, 16))
            if half == 0 and t >= 1:
                w.append((s_dve, mRes[t - 1]))
            m_y_prev = p.emit("scalar", lambda e, bank=bank, half=half: e.activation(
                ytmp[:, half * 512 : half * 512 + 512], bank[:], AF.Identity,
                bias=0.0, scale=sy_b[:]), waits=w, inc=s_act)
        mRes[t] = p.emit("vector", lambda e, xt=xt: e.tensor_add(
            xt, ytmp[:], xt), waits=[(s_act, m_y_prev)], inc=s_dve)
        dst = out_o[128 * t : 128 * (t + 1), :]
        p.emit("sync", lambda e, dst=dst, t=t: e.dma_start(
            out=dst, in_=x_sb[:, t, :]), waits=[(s_dve, mRes[t])],
            inc=s_out, n=16)
    nw = 16 * NT
    if dbg:
        d1 = p.dram("d_scal", [1, 8], F32, "ExternalOutput")
        d2 = p.dram("d_wq", [128, 8 * D], BF16, "ExternalOutput")
        d3 = p.dram("d_sb", [128, 4], F32, "ExternalOutput")
        d3s = p.sb("d3s", [128, 4], F32)
        p.emit("sync", lambda e: e.dma_start(out=d1, in_=scal[:]), inc=s_out, n=16)
        p.emit("sync", lambda e: e.dma_start(
            out=d2, in_=wq_sb[:].rearrange("p c j -> p (c j)")), inc=s_out, n=16)
        p.emit("vector", lambda e: e.tensor_copy(d3s[:, 0:1], sxq_b[:]))
        p.emit("vector", lambda e: e.tensor_copy(d3s[:, 1:2], sbe_b[:]))
        p.emit("vector", lambda e: e.tensor_copy(d3s[:, 2:3], sy_b[:]))
        p.emit("vector", lambda e: e.tensor_copy(d3s[:, 3:4], sy_b[:]))
        mneed = s_dve.n
        p.emit("sync", lambda e: e.dma_start(out=d3, in_=d3s[:]),
               waits=[(s_dve, mneed)], inc=s_out, n=16)
        nw += 48
    p.emit("sync", lambda e: e.wait_ge(s_out.h, nw))
    return p




def build_F(sim_local_exchange=False, dbg=False):
    p = Prog()
    x_in = p.dram("x", [TPC, D], F32, "ExternalInput")
    xe_in = p.dram("xe", [TPC, D], F32, "ExternalInput")
    wq_in = p.dram("wqT", [D, D], F32, "ExternalInput")
    wk_in = p.dram("wkT", [D, D], F32, "ExternalInput")
    wv_in = p.dram("wvT", [D, D], F32, "ExternalInput")
    wo_in = p.dram("woT", [D, D], F32, "ExternalInput")
    wf_in = p.dram("wf", [D, D], F32, "ExternalInput")
    bo_in = p.dram("bo_full", [128, D], F32, "ExternalInput")
    gf_in = p.dram("g_full", [128, D], F32, "ExternalInput")
    bf_in = p.dram("b_full", [128, D], F32, "ExternalInput")
    id_in = p.dram("identb", [128, 128], BF16, "ExternalInput")
    idf_in = p.dram("identf", [128, 128], F32, "ExternalInput")
    ones_in = p.dram("ones128", [1, 128], F32, "ExternalInput")
    cst_in = p.dram("cst", [128, 8], F32, "ExternalInput")
    selm_in = p.dram("selm", [32, 512], F32, "ExternalInput")
    nz_in = p.dram("nz", [1, 4], I32, "ExternalInput")
    out_o = p.dram("out", [TPC, D], F32, "ExternalOutput")
    if dbg:
        x2_dbg = p.dram("x2_dbg", [TPC, D], F32, "ExternalOutput")
        scr_dbg = p.dram("scr_dbg", [128, D], F32, "ExternalOutput")
        st_dbg = p.dram("st_dbg", [128, 160], F32, "ExternalOutput")
        wq_dbg = p.dram("wq_dbg", [128, 8192], BF16, "ExternalOutput")
    kvx = p.dram_shared("kvx", [2 * 128, 520], F32)
    kvf = p.dram_shared("kvf", [2, 4], I32)

    cstb = p.sb("cstb", [128, 8], F32)
    register_consts(p, cstb)
    identb = p.sb("identb_sb", [128, 128], BF16)
    identf = p.sb("identf_sb", [128, 128], F32)
    onesr = None
    snz = p.sb("snz", [1, 4], I32)
    sfl = p.sb("sfl", [1, 4], I32)
    x_sb = p.sb("x_sb", [128, NT, D], F32)
    xe_sb = p.sb("xe_sb", [128, 2, D], F32)
    wstage = p.sb("wstage", [128, 2, 1024], F32)
    big_a = p.sb("big_a", [128, 16384], BF16)
    wq_wo = p.sb("wq_wo", [128, 8192], BF16)
    xlnT = p.sb("xlnT", [128, 8, TPC], BF16)

    xln_bf = [p.sb(f"f_xlnbf{i}", [128, D], BF16) for i in range(2)]
    kf = [p.sb(f"f_kf{i}", [128, D], BF16) for i in range(2)]
    v_ext = [p.sb(f"f_vext{i}", [128, NH * 65], BF16) for i in range(2)]
    kv_sb = p.sb("f_kv_sb", [128, 520], F32)
    kvo_sb = xe_sb[:, 0, 0:520]
    R_bf = xe_sb[:, 1, 0:520].bitcast(BF16)
    bo_full = p.sb("f_bo_full", [128, D], F32)
    g_full = p.sb("f_g_full", [128, D], F32)
    b_full = p.sb("f_b_full", [128, D], F32)
    oh_bf = [p.sb(f"f_oh{i}", [128, D], BF16) for i in range(2)]
    xlnT_x = oh_bf[0][:].rearrange("p (c x) -> p c x", c=8)
    ohT = [xlnT[:, i, :].rearrange("p (c x) -> p c x", c=8) for i in range(2)]
    tmp32 = kf[0][:].bitcast(F32)[:, 0:32]
    dvT_sb = kf[1][:].bitcast(F32)[0:16, 0:256]
    selm_sb = [xln_bf[i][:].bitcast(F32)[0:16, 0:512] for i in range(2)]
    num_sb = p.sb("f_numsb", [128, 1024], F32)
    S0 = p.sb("f_S0", [128, D], F32)
    S1 = p.sb("f_S1", [128, D], F32)
    S2 = p.sb("f_S2", [128, D], F32)
    ytmp = S0
    st = make_stats(p, "f", n=32, sq=S0)
    l3sum = xe_sb[:, 0, 568:584]
    l3sumsq = xe_sb[:, 0, 584:600]
    mean3 = xe_sb[:, 0, 781:797]
    rstd3 = xe_sb[:, 0, 797:813]
    nm3r = xe_sb[:, 0, 813:829]
    u3 = xe_sb[:, 0, 829:833]
    ab = xe_sb[:, 0, 608:640]
    gm = xe_sb[:, 0, 552:568]
    gmx = xe_sb[:, 0, 776:777]
    scal = xe_sb[0:1, 0, 768:776]
    sxq_b = xe_sb[:, 0, 778:779]
    sbe_b = xe_sb[:, 0, 779:780]
    sy_b = xe_sb[:, 0, 780:781]
    bsum = xe_sb[:, 0, 600:608]
    sums_b = xe_sb[:, 0, 848:864]
    bsum1 = xe_sb[:, 0, 777:778]
    onesr = xe_sb[0:1, 0, 640:768]

    T0, T1 = p.psum("T0"), p.psum("T1")
    KA = (p.psum("KA0"), p.psum("KA1"))
    VA = (p.psum("VA0"), p.psum("VA1"))
    KV = (p.psum("KV0"), p.psum("KV1"))
    qbanks = [KA[0], KA[1], VA[0], VA[1]]
    WB2 = (KV[0], KV[1])
    BC = VA[1]

    s_cst, s_id, s_idf = p.sem("s_cst"), p.sem("s_id"), p.sem("s_idf")
    s_ones, s_nz = p.sem("s_ones"), p.sem("s_nz")
    s_msk = p.sem("s_msk")
    s_bo, s_gf, s_bf = p.sem("s_bo"), p.sem("s_gf"), p.sem("s_bf")
    s_x = [p.sem(f"s_x{i}") for i in range(4)]
    s_xe = [p.sem(f"s_xe{i}") for i in range(8)]
    s_wc = [p.sem(f"s_wc{i}") for i in range(8)]
    s_kvs = p.sem("s_kvs")
    s_pl = p.sem("s_pl")
    s_kvr = p.sem("s_kvr")
    s_pe, s_dve, s_act, s_pool = (p.sem("s_pe"), p.sem("s_dve"),
                                  p.sem("s_act"), p.sem("s_pool"))
    s_out = p.sem("s_out")
    p.set_engine_sems(vector=s_dve, scalar=s_act, gpsimd=s_pool)
    const_w = (s_cst, 16)

    for dst, srcp, sm in ((cstb, cst_in, s_cst), (identb, id_in, s_id),
                          (identf, idf_in, s_idf),
                          (snz, nz_in, s_nz), (bo_full, bo_in, s_bo),
                          (g_full, gf_in, s_gf), (b_full, bf_in, s_bf)):
        p.emit("sync", lambda e, dst=dst, srcp=srcp: e.dma_start(
            out=dst[:], in_=srcp), inc=sm, n=16)
    def emit_x_group(g):
        srcp = x_in[512 * g : 512 * (g + 1), :].rearrange(
            "(a p) d -> p a d", p=128)
        p.emit("sync", lambda e, srcp=srcp, g=g: e.dma_start(
            out=x_sb[:, 4 * g : 4 * g + 4, :], in_=srcp), inc=s_x[g], n=16)

    emit_x_group(0)

    wslot_free = [[], []]
    wstream_idx = [0]

    def stream_weight(w_in, consume):
        widx = wstream_idx[0]
        wstream_idx[0] += 1
        done = {s_act: 0, s_pool: 0, s_dve: 0}
        for c in range(8):
            srcp = w_in[128 * c : 128 * (c + 1), :]
            p.emit("sync", lambda e, srcp=srcp, c=c: e.dma_start(
                out=wstage[:, c % 2, :], in_=srcp),
                waits=list(wslot_free[c % 2]), inc=s_wc[c], n=16)
            marks = consume(c, wstage[:, c % 2, :], [(s_wc[c], 16 * (widx + 1))])
            wslot_free[c % 2] = marks
            for sm, mk in marks:
                done[sm] = max(done[sm], mk)
        return done

    def cast_consumer(dst_of, extra=None):
        def consume(c, src_ap, waits):
            if c % 2 == 0:
                w = waits + [const_w] + (extra if (extra and c == 0) else [])
                m = p.emit("scalar", lambda e, src_ap=src_ap, c=c: e.activation(
                    dst_of(c), src_ap, AF.Identity, bias=0.0, scale=1.0),
                    waits=w)
                return [(s_act, m)]
            w = waits + (extra if (extra and c == 1) else [])
            m = p.emit("gpsimd", lambda e, src_ap=src_ap, c=c: e.tensor_copy(
                dst_of(c), src_ap), waits=w)
            return [(s_pool, m)]
        return consume

    for i in range(2):
        ones_ap = v_ext[i][:].rearrange("p (h e) -> p h e", h=NH)[:, :, 64:65]
        p.emit("gpsimd", lambda e, ones_ap=ones_ap: e.memset(ones_ap, 1.0))
    m_ones_vext = s_pool.n

    dk = stream_weight(wk_in, cast_consumer(
        lambda c: big_a[:, c * 1024 : (c + 1) * 1024]))
    dv = stream_weight(wv_in, cast_consumer(
        lambda c: big_a[:, 8192 + c * 1024 : 8192 + (c + 1) * 1024]))
    emit_x_group(1)
    emit_x_group(2)
    dq = stream_weight(wq_in, cast_consumer(
        lambda c: wq_wo[:, c * 1024 : (c + 1) * 1024]))
    emit_x_group(3)
    m_wk_done = (dk[s_act], dk[s_pool])
    m_wv_done = (dv[s_act], dv[s_pool])
    m_wq_done = (dq[s_act], dq[s_pool])

    NKV = 2 * NT
    mT1 = [None] * NKV
    mEvT = [None] * NKV
    mKstt = [None] * NKV
    mKsttH = [None] * NKV
    mVevH = [None] * NKV
    mVev = [None] * NKV
    mKv = [None] * NKV
    mAdd = [None] * NKV
    mXdoneA = [None] * NKV
    mXdoneV = [None] * NKV

    for b in range(2):
        p.emit("vector", lambda e, b=b: e.memset(KV[b][:, 0:260], 0.0),
               inc=s_dve)

    def emit_kv(t):
        par = t % 2
        for h in range(NH):
            pr2 = h // 2
            bank = KV[pr2 // 4]
            rb = (h % 2) * 64
            cb = (pr2 % 4) * 65
            w = []
            if h == 0:
                w = [(s_dve, mKsttH[t]), (s_act, mVevH[t]),
                     (s_pool, m_ones_vext)]
            if h == 8:
                w = [(s_dve, mKstt[t]), (s_act, mVev[t])]
                if sim_local_exchange and t >= 1:
                    w.append((s_dve, mAdd[t - 1]))
            p.emit("tensor", lambda e, h=h, bank=bank, rb=rb, cb=cb, par=par,
                   t=t: e.matmul(
                bank[rb : rb + 64, cb : cb + 65],
                kf[par][:, h * DH : (h + 1) * DH],
                v_ext[par][:, h * 65 : (h + 1) * 65],
                start=sim_local_exchange,
                stop=(sim_local_exchange or t == NKV - 1)),
                waits=w, inc=s_pe if h == NH - 1 else None)
        mKv[t] = s_pe.n
        if sim_local_exchange:
            for b in range(2):
                if t == 0:
                    mAdd[t] = p.emit("vector", lambda e, b=b: e.tensor_copy(
                        kv_sb[:, b * 260 : (b + 1) * 260], KV[b][:, 0:260]),
                        waits=[(s_pe, mKv[t])] if b == 0 else [], inc=s_dve)
                else:
                    mAdd[t] = p.emit("vector", lambda e, b=b: e.tensor_add(
                        kv_sb[:, b * 260 : (b + 1) * 260],
                        KV[b][:, 0:260], kv_sb[:, b * 260 : (b + 1) * 260]),
                        waits=[(s_pe, mKv[t])] if b == 0 else [], inc=s_dve)
        elif t == NKV - 1:
            for b in range(2):
                mAdd[t] = p.emit("vector", lambda e, b=b: e.tensor_copy(
                    kv_sb[:, b * 260 : (b + 1) * 260], KV[b][:, 0:260]),
                    waits=[(s_pe, mKv[t])] if b == 0 else [], inc=s_dve)

    def emit_xln(t, xt, m_stats):
        par = t % 2
        xw = [(s_dve, m_stats)]
        if t >= 2:
            xw.append((s_pe, mT1[t - 2]))
        mXdoneA[t] = p.emit("scalar", lambda e, t=t, par=par, xt=xt: e.activation(
            xln_bf[par][:], xt, AF.Identity,
            bias=st["nmr"][:, t : t + 1], scale=st["rstd"][:, t : t + 1]),
            waits=xw, inc=s_act)

    def kv_tile_A(t, dstT, slot_off):
        par = t % 2
        tw = [(s_act, mXdoneA[t])]
        if t == 0:
            tw.append((s_id, 16))
        if t >= 1:
            tw.append((s_act, mEvT[t - 1]))
        mTh = [None, None]
        for cidx in range(8):
            bank = (T0, T1)[cidx // 4]
            col = (cidx % 4) * 128
            p.emit("tensor", lambda e, cidx=cidx, bank=bank, col=col, par=par: e.matmul(
                bank[:, col : col + 128],
                xln_bf[par][:, cidx * 128 : (cidx + 1) * 128],
                identb[:], start=True, stop=True),
                waits=tw if cidx == 0 else [],
                inc=s_pe if cidx % 4 == 3 else None)
            if cidx % 4 == 3:
                mTh[cidx // 4] = s_pe.n
        mT1[t] = mTh[1]
        if t >= 1:
            emit_kv(t - 1)
        mEvH = [None, None]
        for b in range(2):
            src = (T0, T1)[b][:].rearrange("p (c x) -> p c x", c=4)
            dst = dstT[:, 4 * b : 4 * b + 4, slot_off : slot_off + 128]
            mEvH[b] = p.emit("scalar", lambda e, src=src, dst=dst: e.activation(
                dst, src, AF.Identity, bias=0.0, scale=1.0),
                waits=[(s_pe, mTh[b])], inc=s_act)
        mEvT[t] = mEvH[1]
        return mEvH

    def kv_tile_B(t, dstT, slot_off, mEvH):
        par = t % 2
        for proj, (wcols, banks, wdone, wsem_last) in enumerate((
                (0, KA, m_wk_done, None), (8192, VA, m_wv_done, None))):
            for half in range(2):
                bank = banks[half]
                for cidx in range(8):
                    w = []
                    if half == 0 and cidx == 0:
                        w = [(s_act, mEvH[0]),
                             (s_act, wdone[0]), (s_pool, wdone[1])]
                        if t >= 1:
                            w.append((s_dve, mKstt[t - 1]) if proj == 0
                                     else (s_act, mVev[t - 1]))
                    if proj == 0 and half == 0 and cidx == 4:
                        w = [(s_act, mEvH[1])]
                    p.emit("tensor", lambda e, bank=bank, cidx=cidx, half=half,
                           wcols=wcols, dstT=dstT, slot_off=slot_off: e.matmul(
                        bank[:],
                        dstT[:, cidx, slot_off : slot_off + 128],
                        big_a[:, wcols + cidx * 1024 + half * 512 :
                              wcols + cidx * 1024 + half * 512 + 512],
                        start=(cidx == 0), stop=(cidx == 7)),
                        waits=w,
                        inc=s_pe if cidx == 7 else None)
                if half == 0:
                    if proj == 0:
                        mKh0 = s_pe.n
                    else:
                        mVh0 = s_pe.n
            if proj == 0:
                mK = s_pe.n
            else:
                mV = s_pe.n
        kw0 = [(s_pe, mKh0)]
        if t >= 2:
            kw0.append((s_pe, mKv[t - 2]))
        mKsttH[t], mKstt[t] = emit_eluplus1(
            p, [(KA[0], kf[par][:, 0:512], 0, kw0),
                (KA[1], kf[par][:, 512:1024], 512, [(s_pe, mK)])],
            S1, S2, None, s_dve, s_act)
        for half in range(2):
            dst = v_ext[par][:].rearrange("p (h e) -> p h e", h=NH)[
                :, 8 * half : 8 * half + 8, 0:64]
            srcv = VA[half][:].rearrange("p (h e) -> p h e", h=8)
            w = [(s_pe, mVh0 if half == 0 else mV)]
            if half == 0:
                w.append(const_w)
                if t >= 2:
                    w.append((s_pe, mKv[t - 2]))
            m_v = p.emit("scalar", lambda e, dst=dst, srcv=srcv: e.activation(
                dst, srcv, AF.Identity, bias=0.0, scale=1.0),
                waits=w, inc=s_act)
            if half == 0:
                mVevH[t] = m_v
            else:
                mVev[t] = m_v
        mXdoneV[t] = mKstt[t]

    def tile_info(t):
        if t < NT:
            return x_sb[:, t, :], xlnT, t * 128
        pr, k = divmod(t - NT, 2)
        buf = xe_sb if pr % 2 == 0 else wstage
        return buf[:, k, :], xlnT_x, 0

    def tile_prep(t):
        if t < NT:
            g = t // 4
            for slot in range(4):
                tt = 4 * g + slot
                emit_tile_sums(p, tt, x_sb[:, tt, :], st, s_dve,
                               [(s_x[g], 16)])
            return emit_group_stats(p, slice(4 * g, 4 * g + 4), st, s_dve,
                                    s_act, EPS12, cw=const_w)
        pr = (t - NT) // 2
        buf = xe_sb if pr % 2 == 0 else wstage
        srcp = xe_in[256 * pr : 256 * (pr + 1), :].rearrange(
            "(a p) d -> p a d", p=128)
        w = []
        if pr == 1:
            w = [(s_act, m_wq_done[0]), (s_pool, m_wq_done[1])]
        if pr >= 2:
            jb = NT + 2 * (pr - 2) + 1
            w = [(s_act, mXdoneA[jb] or 0), (s_dve, mXdoneV[jb] or 0)]
        p.emit("sync", lambda e, srcp=srcp, buf=buf: e.dma_start(
            out=buf[:], in_=srcp), waits=w, inc=s_xe[pr], n=16)
        for k in range(2):
            emit_tile_sums(p, NT + 2 * pr + k, buf[:, k, :], st, s_dve,
                           [(s_xe[pr], 16)])
        return emit_group_stats(p, slice(NT + 2 * pr, NT + 2 * pr + 2), st,
                                s_dve, s_act, EPS12, cw=const_w)

    m_gs = tile_prep(0)
    emit_xln(0, tile_info(0)[0], m_gs)
    m_boa = [None] * NT
    m_boa = [None] * NT
    for t in range(NKV):
        xt, dstT, slot_off = tile_info(t)
        mEvH = kv_tile_A(t, dstT, slot_off)
        if t + 1 < NKV:
            nxt = t + 1
            new_group = (nxt < NT and nxt % 4 == 0) or (
                nxt >= NT and (nxt - NT) % 2 == 0)
            if new_group:
                m_gs = tile_prep(nxt)
            emit_xln(nxt, tile_info(nxt)[0], m_gs)
        kv_tile_B(t, dstT, slot_off, mEvH)
        if t < NT:
            m_boa[t] = p.emit("vector", lambda e, xt=xt: e.tensor_add(
                xt, xt, bo_full[:]),
                waits=[(s_act, mXdoneA[t]), (s_bo, 16)], inc=s_dve)
    wslot_free[0] = [(s_act, mXdoneA[NKV - 1]), (s_dve, mXdoneV[NKV - 1])]
    wslot_free[1] = list(wslot_free[0])
    emit_kv(NKV - 1)

    def exch_local(e):
        pid = e.partition_id()
        slot = pid % 2
        e.wait_ge(s_nz.h, 16)
        e.wait_ge(s_dve.h, mAdd[NKV - 1])
        e.dma_start(out=kvx.ap()[bass.ts(slot, 128), :], in_=kv_sb[:]).then_inc(
            s_kvs.h, 16)
        e.wait_ge(s_kvs.h, 16)
        return e.dma_start(out=kvo_sb, in_=kvx.ap()[bass.ts(slot, 128), :])

    def exch(e):
        pid = e.partition_id()
        slot = pid % 2
        other = 1 - slot
        e.wait_ge(s_nz.h, 16)
        e.wait_ge(s_dve.h, mAdd[NKV - 1])
        e.dma_start(out=kvx.ap()[bass.ts(slot, 128), :], in_=kv_sb[:]).then_inc(
            s_kvs.h, 16)
        e.wait_ge(s_kvs.h, 16)
        e.dma_start(out=kvf.ap()[bass.ts(slot, 1), :], in_=snz[:]).then_inc(
            s_kvs.h, 16)
        e.wait_ge(s_kvs.h, 32)
        with (e.register("r_it") as r_it, e.register("r_tgt") as r_tgt):
            e.reg_mov(r_it, 0)
            e.reg_mov(r_tgt, 0)

            def cond():
                e.dma_start(out=sfl[:], in_=kvf.ap()[bass.ts(other, 1), :]
                            ).then_inc(s_pl.h, 16)
                e.reg_add(r_tgt, r_tgt, 16)
                e.wait_ge(s_pl.h, r_tgt)
                with (e.register("r_c") as r_c, e.register("r_v") as r_v,
                      e.register("r_q") as r_q, e.register("r_nd") as r_nd,
                      e.register("r_un") as r_un, e.register("r_go") as r_go):
                    e.reg_mov(r_c, 0)
                    for i in range(4):
                        e.load(r_v, sfl[0:1, i : i + 1])
                        e.load(r_q, snz[0:1, i : i + 1])
                        e.reg_alu(r_q, r_v, r_q, ALU.is_equal)
                        e.reg_alu(r_c, r_c, r_q, ALU.add)
                    e.reg_add(r_it, r_it, 1)
                    e.reg_alu(r_nd, r_c, 4, ALU.is_lt)
                    e.reg_alu(r_un, r_it, 200000, ALU.is_lt)
                    e.reg_alu(r_go, r_nd, r_un, ALU.logical_and)
                    return r_go

            with e.While(cond):
                pass
        return e.dma_start(out=kvo_sb, in_=kvx.ap()[bass.ts(other, 128), :])

    qf_sb = big_a[:].rearrange("p (c t) -> p c t", c=8)
    mQstt = [[None] * 8 for _ in range(4)]
    for g in range(4):
        for co in range(8):
            bank = qbanks[co % 4]
            w = []
            if g == 0 and co == 0:
                w = [(s_act, m_wq_done[0]), (s_pool, m_wq_done[1]),
                     (s_dve, mKstt[NKV - 1]), (s_act, mVev[NKV - 1])]
            gco = g * 8 + co
            if gco >= 4:
                pg, pc = divmod(gco - 4, 8)
                w = w + [(s_dve, mQstt[pg][pc])]
            for cidx in range(8):
                p.emit("tensor", lambda e, bank=bank, cidx=cidx, co=co, g=g: e.matmul(
                    bank[:],
                    wq_wo[:, cidx * 1024 + co * 128 : cidx * 1024 + co * 128 + 128],
                    xlnT[:, cidx, g * 512 : (g + 1) * 512],
                    start=(cidx == 0), stop=(cidx == 7)),
                    waits=w if cidx == 0 else [],
                    inc=s_pe if cidx == 7 else None)
            mQ = s_pe.n
            gco = g * 8 + co
            qw = [(s_pe, mQ)]
            if gco == 0:
                qw.append((s_dve, mKstt[NKV - 1]))
            else:
                qw.append((s_dve, mQstt[(gco - 1) // 8][(gco - 1) % 8]))
            mQstt[g][co] = emit_eluplus1(
                p, [(bank, qf_sb[:, co, g * 512 : g * 512 + 512], 0, qw)],
                S1, S2, None, s_dve, s_act)[-1]
    mQlast_pe = s_pe.n
    mQlast_dve = mQstt[3][7]

    do = stream_weight(wo_in, cast_consumer(
        lambda c: wq_wo[:, c * 1024 : (c + 1) * 1024],
        extra=[(s_pe, mQlast_pe)]))
    m_wo_done = (do[s_act], do[s_pool])

    def beta_consumer(c, src_ap, waits):
        m = p.emit("vector", lambda e, src_ap=src_ap, c=c: e.reduce_sum(
            bsum[:, c : c + 1], src_ap, axis=AXX, apply_absolute_value=True),
            waits=waits, inc=s_dve)
        return [(s_dve, m)]
    df1 = stream_weight(wf_in, beta_consumer)
    m_beta_parts = df1[s_dve]

    p.emit("sync", lambda e: e.dma_start(out=onesr, in_=ones_in),
           waits=[(s_act, mXdoneA[NKV - 1]), (s_dve, mXdoneV[NKV - 1])],
           inc=s_ones, n=16)
    p.emit("sync", exch_local if sim_local_exchange else exch,
           inc=s_kvr, n=16)
    m_tot = p.emit("vector", lambda e: e.tensor_add(
        kv_sb[:], kv_sb[:], kvo_sb), waits=[(s_kvr, 16)], inc=s_dve)
    m_z = p.emit("gpsimd", lambda e: e.memset(R_bf, 0.0),
                 waits=[(s_xe[7], 16), (s_act, mXdoneA[NKV - 1]),
                        (s_dve, mXdoneV[NKV - 1])])
    mR = None
    for cc in range(8):
        for pr in range(2):
            rs = slice(pr * 64, pr * 64 + 64)
            dcol = 128 * cc + pr * 64
            scol = 65 * cc
            p.emit("gpsimd", lambda e, rs=rs, dcol=dcol, scol=scol: e.tensor_copy(
                R_bf[rs, dcol : dcol + 64], kv_sb[rs, scol : scol + 64]),
                waits=[(s_dve, m_tot)] if (cc == 0 and pr == 0) else [])
            mR = p.emit("gpsimd", lambda e, rs=rs, cc=cc, pr=pr, scol=scol: e.tensor_copy(
                R_bf[rs, 1024 + 2 * cc + pr : 1024 + 2 * cc + pr + 1],
                kv_sb[rs, scol + 64 : scol + 65]))
    for i in range(2):
        p.emit("sync", lambda e, i=i: e.dma_start(
            out=selm_sb[i], in_=selm_in[16 * i : 16 * i + 16, :]),
            waits=[(s_pe, mKv[NKV - 1])] if i == 0 else [], inc=s_msk, n=16)
    m_msk = s_pool.n

    def xn_store(t):
        if t < 12:
            return xlnT[:, 2 + t // 2, (t % 2) * 1024 : (t % 2) * 1024 + 1024]
        if t < 14:
            return v_ext[t - 12][:, 0:1024]
        return oh_bf[t - 14][:]

    NPAIR = NT // 2
    mNum = [None] * NPAIR
    mDiv1 = [None] * NPAIR
    mDiv2 = [None] * NPAIR
    mWo = [None] * NT
    mRes = [None] * NT
    mEvW = [None] * NT
    mNe_r2_prev = [None]
    mSq2 = [None] * NT
    st3 = {"sums": l3sum, "sumsq": l3sumsq, "ex2": st["ex2"],
           "nvar": st["nvar"], "lnv": st["lnv"], "mean": mean3,
           "rstd": rstd3, "nmr": nm3r}
    m_xg_prev = None
    m_ga_prev = [None, None]
    num_banks = (T0, KA[0], T1, KA[1])

    def attn_pair(pp):
        buf = pp % 2
        t0 = 2 * pp
        g = pp // 2
        for tk in range(2):
            for ch in range(8):
                w = []
                if tk == 0 and ch == 0:
                    if pp == 0:
                        w = [(s_pool, mR), (s_pool, m_msk),
                             (s_dve, mQlast_dve)]
                    else:
                        w = [(s_dve, mDiv1[pp - 1])]
                    w.append((s_dve, mQstt[g][7]))
                p.emit("tensor", lambda e, tk=tk, ch=ch, t0=t0: e.matmul(
                    VA[0][:, 16 * tk + 2 * ch : 16 * tk + 2 * ch + 2],
                    qf_sb[:, ch, (t0 + tk) * 128 : (t0 + tk) * 128 + 128],
                    R_bf[:, 1024 + 2 * ch : 1024 + 2 * ch + 2],
                    start=True, stop=True),
                    waits=w,
                    inc=s_pe if (tk == 1 and ch == 7) else None)
        mDen = s_pe.n
        for cc in range(8):
            bank = num_banks[cc // 2]
            coff = (cc % 2) * 256
            w = []
            if cc == 0 and pp >= 1:
                w = [(s_dve, mDiv2[pp - 1])]
            if cc == 4 and pp >= 1:
                w = [(s_act, mNe_r2_prev[0])]
            p.emit("tensor", lambda e, bank=bank, coff=coff, cc=cc, t0=t0: e.matmul(
                bank[:, coff : coff + 256],
                R_bf[:, 128 * cc : 128 * cc + 128],
                qf_sb[:, cc, t0 * 128 : t0 * 128 + 256],
                start=True, stop=True),
                waits=w, inc=s_pe if cc == 7 else None)
        mNum[pp] = s_pe.n
        p.emit("vector", lambda e: e.tensor_scalar(
            tmp32, VA[0][:, 0:32], DEN_EPS, None, ALU.add),
            waits=[(s_pe, mDen)], inc=s_dve)
        m_rcs = p.emit("vector", lambda e: e.reciprocal(tmp32, tmp32),
                       inc=s_dve)
        if pp >= 1:
            wo_tile(pp - 1, 1)
        for tk in range(2):
            p.emit("tensor", lambda e, tk=tk: e.matmul(
                VA[1][0:16, 128 * tk : 128 * tk + 128],
                tmp32[:, 16 * tk : 16 * tk + 16], identf[:],
                start=True, stop=True),
                waits=[(s_dve, m_rcs), (s_idf, 16)] if tk == 0 else [],
                inc=s_pe if tk == 1 else None)
        m_tpd = s_pe.n
        m_cpT = p.emit("vector", lambda e: e.tensor_copy(
            dvT_sb, VA[1][0:16, 0:256]), waits=[(s_pe, m_tpd)], inc=s_dve)
        mNe_r1 = None
        for rnd in range(2):
            for j in range(2):
                nb = num_banks[2 * rnd + j]
                ew = []
                if j == 0:
                    ew = ([(s_pe, mNum[pp])] if rnd == 0
                          else [(s_dve, mDiv1[pp])])
                mNe = p.emit("scalar", lambda e, nb=nb, j=j: e.activation(
                    num_sb[:, j * 512 : j * 512 + 512], nb[:], AF.Identity,
                    bias=0.0, scale=1.0), waits=ew, inc=s_act)
            if rnd == 0:
                mNe_r1 = mNe
            else:
                mNe_r2_prev[0] = mNe
            for cc in range(4):
                ch = 4 * rnd + cc
                sv = selm_sb[ch // 4][0:16, 128 * (ch % 4) : 128 * (ch % 4) + 128]
                dbank = (VA[cc // 2] if rnd == 0 else num_banks[cc // 2])
                w = []
                if cc == 0:
                    w = [(s_dve, m_cpT), (s_msk, 32)]
                    if rnd == 1:
                        w.append((s_act, mNe_r1))
                p.emit("tensor", lambda e, ch=ch, cc=cc, sv=sv, dbank=dbank: e.matmul(
                    dbank[:, 256 * (ch % 2) : 256 * (ch % 2) + 256],
                    sv, dvT_sb,
                    start=True, stop=True),
                    waits=w, inc=s_pe if cc == 3 else None)
            mB = s_pe.n
            for j in range(2):
                dst = ohT[buf][:, 4 * rnd + 2 * j : 4 * rnd + 2 * j + 2,
                               :].rearrange("p c x -> p (c x)")
                db = (VA[j] if rnd == 0 else num_banks[j])
                dw = [(s_act, mNe)] if j == 0 else []
                if j == 0:
                    dw.append((s_pe, mB))
                if rnd == 0 and j == 0:
                    dw.append((s_pe, mQlast_pe) if pp <= 1
                              else (s_pe, mWo[2 * (pp - 2) + 1]))
                m = p.emit("vector", lambda e, dst=dst, j=j, db=db: e.tensor_tensor(
                    dst, num_sb[:, j * 512 : j * 512 + 512], db[:, 0:512],
                    ALU.mult), waits=dw, inc=s_dve)
            if rnd == 0:
                mDiv1[pp] = m
            else:
                mDiv2[pp] = m
        wo_tile(pp, 0)

    def wo_tile(pp, tk):
        buf = pp % 2
        t = 2 * pp + tk
        xt = x_sb[:, t, :]
        for half in range(2):
            bank = WB2[half]
            for cidx in range(8):
                w = []
                if half == 0 and cidx == 0:
                    w = [(s_act, m_wo_done[0]), (s_pool, m_wo_done[1])]
                    if tk == 0:
                        w.append((s_dve, mDiv1[pp]))
                    if t == 0:
                        w.append((s_dve, mAdd[NKV - 1]))
                    if t >= 1:
                        w.append((s_dve, mEvW[t - 1]))
                if half == 0 and cidx == 4 and tk == 0:
                    w = [(s_dve, mDiv2[pp])]
                p.emit("tensor", lambda e, bank=bank, cidx=cidx, half=half,
                       buf=buf, tk=tk: e.matmul(
                    bank[:], ohT[buf][:, cidx, tk * 128 : tk * 128 + 128],
                    wq_wo[:, cidx * 1024 + half * 512 :
                          cidx * 1024 + half * 512 + 512],
                    start=(cidx == 0), stop=(cidx == 7)),
                    waits=w, inc=s_pe if cidx == 7 else None)
            if half == 0:
                mWoH0 = s_pe.n
        mWo[t] = s_pe.n
        for half in range(2):
            bank = WB2[half]
            w = [(s_pe, mWoH0 if half == 0 else mWo[t])]
            if half == 0:
                w.append((s_dve, m_boa[t]))
            acc = (st["sums"] if half == 0 else sums_b)[:, t : t + 1]
            mRes[t] = p.emit("vector", lambda e, bank=bank, half=half, xt=xt,
                             acc=acc: e.scalar_tensor_tensor(
                xt[:, half * 512 : half * 512 + 512], bank[:], 1.0,
                xt[:, half * 512 : half * 512 + 512], ALU.mult, ALU.add,
                accum_out=acc), waits=w, inc=s_dve)
        mEvW[t] = mRes[t]
        mSq2[t] = p.emit("scalar", lambda e, xt=xt, t=t: e.activation(
            S0[:], xt, AF.Square, bias=0.0, scale=1.0,
            accum_out=st["sumsq"][:, t : t + 1]),
            waits=[(s_dve, mRes[t])], inc=s_act)

    def group_chain(g):
        nonlocal m_xg_prev
        c4 = slice(4 * g, 4 * g + 4)
        p.emit("vector", lambda e, c4=c4: e.tensor_add(
            st["sums"][:, c4], st["sums"][:, c4], sums_b[:, c4]), inc=s_dve)
        m_gs2 = emit_group_stats(p, c4, st, s_dve, s_act, EPS12, cw=const_w,
                                 sums_w=(s_act, mSq2[4 * g + 3]))
        m_x2t_last = None
        for slot in range(4):
            t = 4 * g + slot
            c = slice(t, t + 1)
            xw = [(s_dve, m_gs2)]
            if m_xg_prev is not None:
                xw.append((s_dve, m_xg_prev))
            m_x2t = p.emit("scalar", lambda e, t=t, c=c: e.activation(
                S2[:], x_sb[:, t, :], AF.Identity, bias=st["nmr"][:, c],
                scale=st["rstd"][:, c], accum_out=l3sum[:, c]),
                waits=xw, inc=s_act)
            m_x2t_last = p.emit("scalar", lambda e, c=c: e.activation(
                S2[:], S2[:], AF.Square, bias=0.0, scale=1.0,
                accum_out=l3sumsq[:, c]), inc=s_act)
        m_gs3 = emit_group_stats(p, c4, st3, s_dve, s_act, EPS3,
                                 sums_w=(s_act, m_x2t_last))
        p.emit("vector", lambda e, c4=c4: e.tensor_mul(
            ab[:, c4], st["rstd"][:, c4], rstd3[:, c4]),
            waits=[(s_dve, m_gs3)], inc=s_dve)
        p.emit("vector", lambda e, c4=c4: e.tensor_sub(
            u3[:, 0:4], st["nmr"][:, c4], mean3[:, c4]), inc=s_dve)
        m_ab = p.emit("vector", lambda e, c4=c4, g=g: e.tensor_mul(
            ab[:, NT + 4 * g : NT + 4 * g + 4], u3[:, 0:4], rstd3[:, c4]),
            inc=s_dve)
        return m_ab

    def group_gamma(g, m_ab):
        nonlocal m_xg_prev
        for slot in range(4):
            t = 4 * g + slot
            p.emit("vector", lambda e, t=t: e.tensor_scalar(
                S2[:], x_sb[:, t, :], ab[:, t : t + 1],
                ab[:, NT + t : NT + t + 1], ALU.mult, ALU.add),
                waits=[(s_dve, m_ab), (s_gf, 16)], inc=s_dve)
            p.emit("vector", lambda e: e.scalar_tensor_tensor(
                S2[:], S2[:], 1.0, g_full[:], ALU.mult, ALU.mult), inc=s_dve)
            p.emit("vector", lambda e, t=t: e.tensor_add(
                xn_store(t), S2[:], b_full[:]), waits=[(s_bf, 16)], inc=s_dve)
            m_xg_prev = p.emit("vector", lambda e, t=t: e.reduce_max(
                gm[:, t : t + 1], xn_store(t), axis=AXX,
                apply_absolute_value=True), inc=s_dve)

    m_abs = {}
    for step in (("p", 0), ("p", 1), ("p", 2), ("a", 0), ("p", 3), ("b", 0),
                 ("p", 4), ("a", 1), ("p", 5), ("b", 1), ("p", 6), ("a", 2),
                 ("p", 7), ("b", 2)):
        if step[0] == "p":
            attn_pair(step[1])
        elif step[0] == "a":
            m_abs[step[1]] = group_chain(step[1])
        else:
            group_gamma(step[1], m_abs[step[1]])
    wo_tile(NPAIR - 1, 1)

    m_bs1 = p.emit("vector", lambda e: e.reduce_sum(bsum1, bsum, axis=AXX),
                   waits=[(s_dve, m_beta_parts)], inc=s_dve)
    m_bmm = p.emit("tensor", lambda e: e.matmul(
        BC[0:1, 0:1], bsum1, cstb[:, 5:6], start=True, stop=True),
        waits=[(s_dve, m_bs1), const_w, (s_dve, mDiv1[NPAIR - 1])], inc=s_pe)
    p.emit("vector", lambda e: e.tensor_scalar(
        scal[0:1, 2:3], BC[0:1, 0:1], 1.0 / (D * D), None, ALU.mult),
        waits=[(s_pe, m_bmm)], inc=s_dve)
    p.emit("vector", lambda e: e.tensor_scalar(
        scal[0:1, 3:4], scal[0:1, 2:3], BETA_EPS, None, ALU.add), inc=s_dve)
    m_sbe_sc = p.emit("vector", lambda e: e.reciprocal(
        scal[0:1, 3:4], scal[0:1, 3:4]), inc=s_dve)
    m_sbe_mm = p.emit("tensor", lambda e: e.matmul(
        BC[:, 0:1], onesr, scal[0:1, 3:4], start=True, stop=True),
        waits=[(s_dve, m_sbe_sc), (s_ones, 16)], inc=s_pe)
    m_sbe_cp = p.emit("vector", lambda e: e.tensor_copy(
        sbe_b, BC[:, 0:1]), waits=[(s_pe, m_sbe_mm)], inc=s_dve)

    nsb_i = num_sb[:].bitcast(I32)

    q_prev = {}

    def quant_consumer(c, src_ap, waits):
        w = list(waits)
        if c == 0:
            w += [(s_dve, m_sbe_cp), (s_pe, mNum[NPAIR - 1]),
                  (s_dve, mDiv2[NPAIR - 1])]
        else:
            w += [(s_dve, q_prev["dve"]), (s_pool, q_prev["pool"])]
        m_sc = p.emit("scalar", lambda e, src_ap=src_ap: e.activation(
            num_sb[:], src_ap, AF.Identity, bias=0.0, scale=sbe_b),
            waits=w, inc=s_act)
        p.emit("vector", lambda e: e.tensor_copy(nsb_i, num_sb[:]),
               waits=[(s_act, m_sc)], inc=s_dve)
        m_cv2 = p.emit("vector", lambda e: e.tensor_copy(num_sb[:], nsb_i),
                       inc=s_dve)
        p.emit("gpsimd", lambda e, c=c: e.tensor_scalar(
            big_a[:, c * 1024 : (c + 1) * 1024], num_sb[:],
            1.0, -1.0, ALU.min, ALU.max), waits=[(s_dve, m_cv2)])
        q_prev["dve"] = m_cv2
        q_prev["pool"] = s_pool.n
        return [(s_pool, s_pool.n)]
    df2 = stream_weight(wf_in, quant_consumer)
    m_wq2 = df2[s_pool]

    group_gamma(3, group_chain(3))
    m_gm_all = s_dve.n


    p.emit("vector", lambda e: e.reduce_max(gmx, gm, axis=AXX), inc=s_dve)
    m_tp = p.emit("tensor", lambda e: e.matmul(
        T0[0:1, 0:128], gmx, identf[:], start=True, stop=True),
        waits=[(s_dve, s_dve.n), (s_idf, 16)],
        inc=s_pe)
    p.emit("vector", lambda e: e.reduce_max(
        scal[0:1, 0:1], T0[0:1, 0:128], axis=AXX), waits=[(s_pe, m_tp)],
        inc=s_dve)
    p.emit("vector", lambda e: e.reciprocal(scal[0:1, 1:2], scal[0:1, 0:1]),
           inc=s_dve)
    p.emit("vector", lambda e: e.tensor_scalar(
        scal[0:1, 1:2], scal[0:1, 1:2], 128.0, None, ALU.mult), inc=s_dve)
    p.emit("vector", lambda e: e.tensor_mul(
        scal[0:1, 4:5], scal[0:1, 0:1], scal[0:1, 2:3]), inc=s_dve)
    m_sy = p.emit("vector", lambda e: e.tensor_scalar(
        scal[0:1, 4:5], scal[0:1, 4:5], 1.0 / 128.0, None, ALU.mult), inc=s_dve)
    m_prev = m_sy
    for col, dstb in ((1, sxq_b), (4, sy_b)):
        m_pe = p.emit("tensor", lambda e, col=col: e.matmul(
            BC[:, 0:1], onesr, scal[0:1, col : col + 1],
            start=True, stop=True),
            waits=[(s_dve, m_prev), (s_ones, 16)], inc=s_pe)
        m_prev = p.emit("vector", lambda e, dstb=dstb: e.tensor_copy(
            dstb, BC[:, 0:1]), waits=[(s_pe, m_pe)], inc=s_dve)
    m_bcasts = m_prev

    if dbg:
        s_dbg = p.sem("s_dbg")
        for t in range(NT):
            p.emit("sync", lambda e, t=t: e.dma_start(
                out=x2_dbg[128 * t : 128 * (t + 1), :], in_=x_sb[:, t, :]),
                waits=[(s_dve, m_gm_all)] if t == 0 else [], inc=s_dbg, n=16)
        p.emit("sync", lambda e: e.dma_start(
            out=scr_dbg, in_=xe_sb[:, 0, :]), waits=[(s_dve, m_bcasts)],
            inc=s_dbg, n=16)
        for j, kk in enumerate(("sums", "sumsq", "mean", "rstd", "nmr")):
            p.emit("sync", lambda e, j=j, kk=kk: e.dma_start(
                out=st_dbg[:, 32 * j : 32 * (j + 1)], in_=st[kk][:]),
                waits=[(s_dve, m_bcasts)], inc=s_dbg, n=16)
        p.emit("sync", lambda e: e.dma_start(
            out=wq_dbg, in_=big_a[:, 0:8192]), waits=[(s_pool, m_wq2)],
            inc=s_dbg, n=16)

    w_q = big_a[:, 0:8192]
    mT3 = [None] * NT
    mEvT3 = [None] * NT
    mF = [None] * NT
    mOut = [None] * NT
    m_xq = [None] * NT
    for t in range(NT):
        w0 = [(s_dve, m_bcasts)]
        if t == 0 and dbg:
            w0.append((s_dbg, 16 * NT))
        m_clip = p.emit("vector", lambda e, t=t: e.tensor_scalar(
            S2[:], xn_store(t), sxq_b, -CLIP, ALU.mult, ALU.max),
            waits=w0 if t == 0 else [], inc=s_dve)
        m_xq[t] = p.emit("vector", lambda e, t=t: e.tensor_scalar(
            xn_store(t), S2[:], CLIP, None, ALU.min),
            waits=[(s_dve, m_clip)], inc=s_dve)
    for t in range(NT):
        par = t % 2
        xt = x_sb[:, t, :]
        tb3 = (T0, T1) if par == 0 else (KA[0], KA[1])
        fb3 = (KV[0], KV[1]) if par == 0 else (VA[0], VA[1])
        tw = [(s_dve, m_xq[t])]
        if t == 0:
            tw.append((s_pe, m_tp))
        if t >= 2:
            tw.append((s_act, mEvT3[t - 2]))
        mTh3 = [None, None]
        for cidx in range(8):
            bank = tb3[cidx // 4]
            col = (cidx % 4) * 128
            p.emit("tensor", lambda e, cidx=cidx, bank=bank, col=col, t=t: e.matmul(
                bank[:, col : col + 128],
                xn_store(t)[:, cidx * 128 : (cidx + 1) * 128],
                identb[:], start=True, stop=True),
                waits=tw if cidx == 0 else [],
                inc=s_pe if cidx % 4 == 3 else None)
            if cidx % 4 == 3:
                mTh3[cidx // 4] = s_pe.n
        mT3[t] = mTh3[1]
        mEv3H = [None, None]
        for b in range(2):
            src = tb3[b][:].rearrange("p (c x) -> p c x", c=4)
            dst = ohT[par][:, 4 * b : 4 * b + 4, 0:128]
            ew = [(s_pe, mTh3[b])]
            if b == 0 and t >= 2:
                ew.append((s_pe, mF[t - 2]))
            mEv3H[b] = p.emit("scalar", lambda e, src=src, dst=dst: e.activation(
                dst, src, AF.Identity, bias=0.0, scale=1.0),
                waits=ew, inc=s_act)
        mEvT3[t] = mEv3H[1]
        for half in range(2):
            bank = fb3[half]
            for cidx in range(8):
                w = []
                if half == 0 and cidx == 0:
                    w = [(s_act, mEv3H[0]), (s_pool, m_wq2)]
                    if t >= 2:
                        w.append((s_dve, mOut[t - 2]))
                if half == 0 and cidx == 4:
                    w = [(s_act, mEv3H[1])]
                p.emit("tensor", lambda e, bank=bank, cidx=cidx, half=half, par=par: e.matmul(
                    bank[:], ohT[par][:, cidx, 0:128],
                    w_q[:, cidx * 1024 + half * 512 :
                        cidx * 1024 + half * 512 + 512],
                    start=(cidx == 0), stop=(cidx == 7)),
                    waits=w, inc=s_pe if cidx == 7 else None)
            if half == 0:
                mFh0 = s_pe.n
        mF[t] = s_pe.n
        for half in range(2):
            bank = fb3[half]
            w = [(s_pe, mFh0 if half == 0 else mF[t])]
            mOut[t] = p.emit("vector", lambda e, bank=bank, half=half, xt=xt: e.scalar_tensor_tensor(
                xt[:, half * 512 : half * 512 + 512], bank[:], sy_b,
                xt[:, half * 512 : half * 512 + 512], ALU.mult, ALU.add),
                waits=w, inc=s_dve)
        dst = out_o[128 * t : 128 * (t + 1), :]
        p.emit("sync", lambda e, dst=dst, t=t: e.dma_start(
            out=dst, in_=x_sb[:, t, :]), waits=[(s_dve, mOut[t])],
            inc=s_out, n=16)
    p.emit("sync", lambda e: e.wait_ge(s_out.h, 16 * NT))
    if dbg:
        p.emit("sync", lambda e: e.wait_ge(s_dbg.h, 16 * (NT + 7)))
    return p



def _np(a, dt=np.float32):
    return np.ascontiguousarray(np.asarray(a, dtype=dt))


def kernel(L0_attn_x, L0_attn_wq, L0_attn_wk, L0_attn_wv, L0_attn_wo,
           L0_attn_bo, L0_ffn_weight, L0_ffn_ln_gamma, L0_ffn_ln_beta):
    x = _np(L0_attn_x).reshape(NCORES * TPC, D)
    wqT = _np(np.asarray(L0_attn_wq).T)
    wkT = _np(np.asarray(L0_attn_wk).T)
    wvT = _np(np.asarray(L0_attn_wv).T)
    woT = _np(np.asarray(L0_attn_wo).T)
    wf = _np(L0_ffn_weight)
    bo_full = np.ascontiguousarray(np.broadcast_to(_np(L0_attn_bo), (128, D)))
    g_full = np.ascontiguousarray(np.broadcast_to(_np(L0_ffn_ln_gamma), (128, D)))
    b_full = np.ascontiguousarray(np.broadcast_to(_np(L0_ffn_ln_beta), (128, D)))
    identb = np.eye(128, dtype=ml_dtypes.bfloat16)
    identf = np.eye(128, dtype=np.float32)
    ones128 = np.ones((1, 128), np.float32)
    cst = host_consts()
    nz = np.random.default_rng().integers(1, 2**31 - 1, size=(1, 4),
                                          dtype=np.int32)
    selm = np.zeros((32, 512), np.float32)
    for ch in range(8):
        for q in range(2):
            selm[16 * (ch // 4) + 2 * ch + q,
                 128 * (ch % 4) + 64 * q : 128 * (ch % 4) + 64 * q + 64] = 1.0
    xs = [np.ascontiguousarray(x[i * TPC : (i + 1) * TPC]) for i in range(NCORES)]

    nc = build_F().finalize()
    maps = []
    for i in range(NCORES):
        b = i // 4
        j = 4 * b + (i - 4 * b + 2) % 4
        maps.append(dict(x=xs[i], xe=xs[j], wqT=wqT, wkT=wkT, wvT=wvT,
                         woT=woT, wf=wf, bo_full=bo_full, g_full=g_full,
                         b_full=b_full, identb=identb, identf=identf,
                         ones128=ones128, cst=cst, selm=selm, nz=nz))
    res = run_bass_kernel_spmd(nc, maps, core_ids=list(range(NCORES))).results
    out = np.concatenate([res[i]["out"] for i in range(NCORES)], axis=0)
    return np.ascontiguousarray(out.reshape(2, NCORES * TPC // 2, D),
                                dtype=np.float32)


# version 32
# speedup vs baseline: 1.6227x; 1.6227x over previous
import numpy as np
import ml_dtypes
from contextlib import ExitStack

import concourse.bass as bass
import concourse.mybir as mybir
from concourse.bass_utils import run_bass_kernel_spmd

F32 = mybir.dt.float32
BF16 = mybir.dt.bfloat16
I32 = mybir.dt.int32
AF = mybir.ActivationFunctionType
ALU = mybir.AluOpType
AXX = mybir.AxisListType.X

D = 1024
NT = 16
NH = 16
DH = 64
TPC = 128 * NT
NCORES = 8
MAGIC = 12582912.0
CLIP = float(np.float32(128.0 - 1e-5))
INV_D = 1.0 / D
EPS12 = 1e-5
EPS3 = 1e-3
DEN_EPS = 1e-6
BETA_EPS = 1e-5
ACT_CONSTS = [EPS12, EPS3, -1.0, -0.5, 0.0, 1.0]


class Sem:
    def __init__(self, h):
        self.h = h
        self.n = 0


class Prog:
    def __init__(self):
        self.nc = bass.Bass("TRN2", target_bir_lowering=False)
        self.q = {k: [] for k in ("sync", "tensor", "vector", "scalar", "gpsimd")}
        self.es = ExitStack()

    def dram(self, name, shape, dt, kind):
        return self.nc.dram_tensor(name, shape, dt, kind=kind).ap()

    def dram_shared(self, name, shape, dt):
        return self.nc.dram_tensor(name, shape, dt, kind="Internal",
                                   addr_space="Shared")

    def sb(self, name, shape, dt):
        return self.es.enter_context(self.nc.sbuf_tensor(name, shape, dt))

    def psum(self, name):
        return self.es.enter_context(self.nc.psum_tensor(name, [128, 512], F32))

    def sem(self, name):
        return Sem(self.es.enter_context(self.nc.semaphore(name)))

    def set_engine_sems(self, **kw):
        self.eng_sem = kw

    def emit(self, eng, fn, waits=(), inc=None, n=1, auto=True):
        wl = [(s, v) for (s, v) in waits if s is not None and v]
        es = getattr(self, "eng_sem", {}).get(eng)
        if es is not None and auto:
            if es.n > 0:
                wl.append((es, es.n))
            if inc is None:
                inc = es
                n = 1

        def g(e, wl=wl, fn=fn, inc=inc, n=n):
            for s, v in wl:
                e.wait_ge(s.h, v)
            inst = fn(e)
            if inc is not None:
                inst.then_inc(inc.h, n)

        self.q[eng].append(g)
        if inc is not None:
            inc.n += n
            return inc.n
        return None

    def finalize(self):
        nc = self.nc
        with nc.Block() as blk:
            for eng in ("sync", "tensor", "vector", "scalar", "gpsimd"):
                ops = self.q[eng]
                if not ops:
                    continue
                getattr(blk, eng)(lambda e, ops=ops: [f(e) for f in ops])
        self.es.close()
        return nc


def register_consts(p, cstb):
    for j, val in enumerate(ACT_CONSTS):
        p.nc.const_aps.register(F32, float(val), cstb[:, j : j + 1])


def host_consts():
    cst = np.zeros((128, 8), np.float32)
    for j, val in enumerate(ACT_CONSTS):
        cst[:, j] = val
    return cst



def emit_ln_stats(p, t, xt, st, s_dve, s_act, s_pool, x_waits, eps,
                  sums_ready=None, cw=None):
    c = slice(t, t + 1)
    p.emit("vector", lambda e: e.scalar_tensor_tensor(
        st["sq"][:], xt, 1.0, xt, ALU.mult, ALU.mult, accum_out=st["sumsq"][:, c]),
        waits=x_waits, inc=s_dve)
    if sums_ready is None:
        p.emit("vector", lambda e: e.reduce_sum(st["sums"][:, c], xt, axis=AXX),
               waits=x_waits, inc=s_dve)
        sums_ready = (s_dve, s_dve.n)
    z = slice(0, 1)
    p.emit("vector", lambda e: e.tensor_scalar(
        st["ex2"][:, z], st["sumsq"][:, c], INV_D, None, ALU.mult), inc=s_dve)
    p.emit("vector", lambda e: e.tensor_scalar(
        st["mean"][:, c], st["sums"][:, c], INV_D, None, ALU.mult),
        waits=[sums_ready], inc=s_dve)
    p.emit("vector", lambda e: e.tensor_mul(
        st["nvar"][:, z], st["mean"][:, c], st["mean"][:, c]), inc=s_dve)
    m_nvar = p.emit("vector", lambda e: e.tensor_sub(
        st["nvar"][:, z], st["nvar"][:, z], st["ex2"][:, z]), inc=s_dve)
    lnw = [(s_dve, m_nvar)]
    if cw is not None:
        lnw.append(cw)
    p.emit("scalar", lambda e: e.activation(
        st["lnv"][:, z], st["nvar"][:, z], AF.Ln, bias=float(eps), scale=-1.0),
        waits=lnw, inc=s_act)
    m_rstd = p.emit("scalar", lambda e: e.activation(
        st["rstd"][:, c], st["lnv"][:, z], AF.Exp, bias=0.0, scale=-0.5),
        inc=s_act)
    m_nmr = p.emit("vector", lambda e: e.scalar_tensor_tensor(
        st["nmr"][:, c], st["mean"][:, c], -1.0, st["rstd"][:, c],
        ALU.mult, ALU.mult), waits=[(s_act, m_rstd)], inc=s_dve)
    return m_nmr


def make_stats(p, tag, n=NT, sq=None):
    st = {}
    for k in ("sums", "sumsq", "mean", "rstd", "nmr"):
        st[k] = p.sb(f"{tag}_{k}", [128, n], F32)
    for k in ("ex2", "nvar", "lnv"):
        st[k] = p.sb(f"{tag}_{k}", [128, 4], F32)
    st["sq"] = p.sb(f"{tag}_sq", [128, D], F32) if sq is None else sq
    return st


def emit_tile_sums(p, t, xt, st, s_dve, x_waits):
    c = slice(t, t + 1)
    p.emit("vector", lambda e: e.scalar_tensor_tensor(
        st["sq"][:], xt, 1.0, xt, ALU.mult, ALU.mult,
        accum_out=st["sumsq"][:, c]), waits=x_waits, inc=s_dve)
    return p.emit("vector", lambda e: e.reduce_sum(
        st["sums"][:, c], xt, axis=AXX), waits=x_waits, inc=s_dve)


def emit_group_stats(p, c, st, s_dve, s_act, eps, cw=None, sums_w=None):
    zc = slice(0, c.stop - c.start)
    w = [sums_w] if sums_w else []
    p.emit("vector", lambda e: e.tensor_scalar(
        st["ex2"][:, zc], st["sumsq"][:, c], INV_D, None, ALU.mult),
        waits=w, inc=s_dve)
    p.emit("vector", lambda e: e.tensor_scalar(
        st["mean"][:, c], st["sums"][:, c], INV_D, None, ALU.mult), inc=s_dve)
    p.emit("vector", lambda e: e.tensor_mul(
        st["nvar"][:, zc], st["mean"][:, c], st["mean"][:, c]), inc=s_dve)
    m_nvar = p.emit("vector", lambda e: e.tensor_sub(
        st["nvar"][:, zc], st["nvar"][:, zc], st["ex2"][:, zc]), inc=s_dve)
    lnw = [(s_dve, m_nvar)]
    if cw is not None:
        lnw.append(cw)
    p.emit("scalar", lambda e: e.activation(
        st["lnv"][:, zc], st["nvar"][:, zc], AF.Ln, bias=float(eps),
        scale=-1.0), waits=lnw, inc=s_act)
    m_rstd = p.emit("scalar", lambda e: e.activation(
        st["rstd"][:, c], st["lnv"][:, zc], AF.Exp, bias=0.0, scale=-0.5),
        inc=s_act)
    return p.emit("vector", lambda e: e.scalar_tensor_tensor(
        st["nmr"][:, c], st["mean"][:, c], -1.0, st["rstd"][:, c],
        ALU.mult, ALU.mult), waits=[(s_act, m_rstd)], inc=s_dve)


def emit_transposes(p, src_bf, tb, identb, waits, s_pe):
    m = None
    for cidx in range(8):
        bank = tb[cidx // 4]
        col = (cidx % 4) * 128
        m = p.emit("tensor", lambda e, cidx=cidx, bank=bank, col=col: e.matmul(
            bank[:, col : col + 128], src_bf[:, cidx * 128 : (cidx + 1) * 128],
            identb[:], start=True, stop=True),
            waits=waits if cidx == 0 else [],
            inc=s_pe if cidx == 7 else None)
    return m


def emit_evac_T(p, tb, dst3, slot_off, waits, s_dve):
    m = None
    for b in range(2):
        src = tb[b][:].rearrange("p (c x) -> p c x", c=4)
        dst = dst3[:, 4 * b : 4 * b + 4, slot_off : slot_off + 128]
        m = p.emit("vector", lambda e, src=src, dst=dst: e.tensor_copy(dst, src),
                   waits=waits if b == 0 else [], inc=s_dve)
    return m


def emit_eluplus1(p, jobs, scr_min, scr_exp, first_waits, s_dve, s_act,
                  prev_stt=None):
    marks = []
    for i, (bank, dst, so, jw) in enumerate(jobs):
        m_r = p.emit("scalar", lambda e, bank=bank, so=so: e.activation(
            scr_min[:, so : so + 512], bank[:],
            AF.Relu, bias=0.0, scale=-1.0), waits=jw, inc=s_act)
        m_exp = p.emit("scalar", lambda e, so=so: e.activation(
            scr_exp[:, so : so + 512], scr_min[:, so : so + 512],
            AF.Exp, bias=0.0, scale=-1.0), inc=s_act)
        marks.append(p.emit("vector", lambda e, bank=bank, dst=dst, so=so: e.scalar_tensor_tensor(
            dst, bank[:], 0.0, scr_exp[:, so : so + 512], ALU.max, ALU.add),
            waits=[(s_act, m_exp)], inc=s_dve))
    return marks


def emit_k1_broadcast_row(p, onesrow, src_row, dst, bank, first_waits, s_pe, s_dve):
    n = dst.shape[1]
    m_dve = None
    for off in range(0, n, 512):
        w = min(512, n - off)
        m_pe = p.emit("tensor", lambda e, off=off, w=w: e.matmul(
            bank[:, 0:w], onesrow, src_row[:, off : off + w],
            start=True, stop=True),
            waits=(first_waits if off == 0 else [(s_dve, m_dve)]), inc=s_pe)
        m_dve = p.emit("vector", lambda e, off=off, w=w: e.tensor_copy(
            dst[:, off : off + w], bank[:, 0:w]), waits=[(s_pe, m_pe)],
            inc=s_dve)
    return m_dve



def build_L1(dbg=False):
    p = Prog()
    x_in = p.dram("x", [TPC, D], F32, "ExternalInput")
    wq_in = p.dram("wqT", [D, D], F32, "ExternalInput")
    wk_in = p.dram("wkT", [D, D], F32, "ExternalInput")
    wv_in = p.dram("wvT", [D, D], F32, "ExternalInput")
    id_in = p.dram("identb", [128, 128], BF16, "ExternalInput")
    cst_in = p.dram("cst", [128, 8], F32, "ExternalInput")
    kvp_out = p.dram("kvp", [128, 520], F32, "ExternalOutput")
    qf_out = p.dram("qf", [128, 8 * TPC], BF16, "ExternalOutput")

    cstb = p.sb("cstb", [128, 8], F32)
    register_consts(p, cstb)
    identb = p.sb("identb_sb", [128, 128], BF16)
    x_sb = p.sb("x_sb", [128, NT, D], F32)
    wk_sb = p.sb("wk_sb", [128, 8, D], BF16)
    wv_sb = p.sb("wv_sb", [128, 8, D], BF16)
    wq_sb = p.sb("wq_sb", [128, 8, D], BF16)
    xlnT = p.sb("xlnT", [128, 8, 512], BF16)
    xln_bf = [p.sb(f"xlnbf{i}", [128, D], BF16) for i in range(2)]
    kf = [p.sb(f"kf{i}", [128, D], BF16) for i in range(2)]
    v_ext = [p.sb(f"vext{i}", [128, NH * 65], BF16) for i in range(2)]
    qf_sb = p.sb("qf_sb", [128, 8, TPC], BF16)
    kv_sb = p.sb("kv_sb", [128, 520], F32)
    st = make_stats(p, "l1")
    scr_min = p.sb("scr_min", [128, D], F32)
    scr_exp = p.sb("scr_exp", [128, D], F32)

    T0, T1 = p.psum("T0"), p.psum("T1")
    KA = (p.psum("KA0"), p.psum("KA1"))
    VA = (p.psum("VA0"), p.psum("VA1"))
    KV = (p.psum("KV0"), p.psum("KV1"))
    qbanks = [KA[0], KA[1], VA[0], VA[1]]

    s_cst, s_id = p.sem("s_cst"), p.sem("s_id")
    s_x = [p.sem(f"s_x{i}") for i in range(4)]
    s_wk, s_wv, s_wq = p.sem("s_wk"), p.sem("s_wv"), p.sem("s_wq")
    s_pe, s_dve, s_act, s_pool = (p.sem("s_pe"), p.sem("s_dve"),
                                  p.sem("s_act"), p.sem("s_pool"))
    s_out = p.sem("s_out")
    p.set_engine_sems(vector=s_dve, scalar=s_act, gpsimd=s_pool)

    p.emit("sync", lambda e: e.dma_start(out=cstb[:], in_=cst_in), inc=s_cst, n=16)
    p.emit("sync", lambda e: e.dma_start(out=identb[:], in_=id_in), inc=s_id, n=16)
    for g in range(4):
        src = x_in[512 * g : 512 * (g + 1), :].rearrange("(a p) d -> p a d", p=128)
        p.emit("sync", lambda e, g=g, src=src: e.dma_start(
            out=x_sb[:, 4 * g : 4 * g + 4, :], in_=src), inc=s_x[g], n=16)
    for w_in, w_sb, w_sem in ((wk_in, wk_sb, s_wk), (wv_in, wv_sb, s_wv),
                              (wq_in, wq_sb, s_wq)):
        src = w_in.rearrange("(c p) j -> p c j", p=128)
        p.emit("gpsimd", lambda e, w_sb=w_sb, src=src: e.dma_start(
            out=w_sb[:], in_=src), inc=w_sem, n=16, auto=False)
    for i in range(2):
        ones_ap = v_ext[i][:].rearrange("p (h e) -> p h e", h=NH)[:, :, 64:65]
        p.emit("gpsimd", lambda e, ones_ap=ones_ap: e.memset(ones_ap, 1.0),
               inc=s_pool)
    const_w = (s_cst, 16)

    mT1 = [None] * NT
    mEvT = [None] * NT
    mKstt = [None] * NT
    mVev = [None] * NT
    mKv = [None] * NT
    mQstt = [None] * 8
    mQdone = [None]

    mAdd = [None] * NT

    def emit_kv(t):
        par = t % 2
        for h in range(NH):
            pr = h // 2
            bank = KV[pr // 4]
            rb = (h % 2) * 64
            cb = (pr % 4) * 65
            w = []
            if h == 0:
                w = [(s_dve, mKstt[t]), (s_act, mVev[t]), (s_pool, 2)]
                if t >= 1:
                    w.append((s_dve, mAdd[t - 1]))
            p.emit("tensor", lambda e, h=h, bank=bank, rb=rb, cb=cb, par=par: e.matmul(
                bank[rb : rb + 64, cb : cb + 65],
                kf[par][:, h * DH : (h + 1) * DH],
                v_ext[par][:, h * 65 : (h + 1) * 65],
                start=True, stop=True),
                waits=w, inc=s_pe if h == NH - 1 else None)
        mKv[t] = s_pe.n
        for b in range(2):
            if t == 0:
                mAdd[t] = p.emit("vector", lambda e, b=b: e.tensor_copy(
                    kv_sb[:, b * 260 : (b + 1) * 260], KV[b][:, 0:260]),
                    waits=[(s_pe, mKv[t])] if b == 0 else [], inc=s_dve)
            else:
                mAdd[t] = p.emit("vector", lambda e, b=b: e.tensor_add(
                    kv_sb[:, b * 260 : (b + 1) * 260],
                    KV[b][:, 0:260], kv_sb[:, b * 260 : (b + 1) * 260]),
                    waits=[(s_pe, mKv[t])] if b == 0 else [], inc=s_dve)

    for g in range(4):
        for slot in range(4):
            t = 4 * g + slot
            par = t % 2
            xt = x_sb[:, t, :]
            x_wait = [(s_x[g], 16)]
            m_nmr = emit_ln_stats(p, t, xt, st, s_dve, s_act, s_pool,
                                  x_wait, EPS12, cw=const_w)
            xw = [(s_dve, m_nmr), const_w]
            if t >= 2:
                xw.append((s_pe, mT1[t - 2]))
            mXln = p.emit("scalar", lambda e, t=t, par=par, xt=xt: e.activation(
                xln_bf[par][:], xt, AF.Identity,
                bias=st["nmr"][:, t : t + 1], scale=st["rstd"][:, t : t + 1]),
                waits=xw, inc=s_act)
            tw = [(s_act, mXln)]
            if t == 0:
                tw.append((s_id, 16))
            if t >= 1:
                tw.append((s_dve, mEvT[t - 1]))
            mT1[t] = emit_transposes(p, xln_bf[par], (T0, T1), identb, tw, s_pe)
            if t >= 1:
                emit_kv(t - 1)
            ew = [(s_pe, mT1[t])]
            if slot == 0 and g >= 1:
                ew.append((s_pe, mQdone[0]))
            mEvT[t] = emit_evac_T(p, (T0, T1), xlnT, slot * 128, ew, s_dve)
            mK = mV = None
            for proj, (w_sb, banks) in enumerate(((wk_sb, KA), (wv_sb, VA))):
                for half in range(2):
                    bank = banks[half]
                    for cidx in range(8):
                        w = []
                        if half == 0 and cidx == 0:
                            w = [(s_dve, mEvT[t]), (s_wk if proj == 0 else s_wv, 16)]
                            if t >= 1:
                                w.append((s_dve, mKstt[t - 1]) if proj == 0
                                         else (s_act, mVev[t - 1]))
                        p.emit("tensor", lambda e, bank=bank, cidx=cidx, half=half,
                               slot=slot, w_sb=w_sb: e.matmul(
                            bank[:],
                            xlnT[:, cidx, slot * 128 : slot * 128 + 128],
                            w_sb[:, cidx, half * 512 : half * 512 + 512],
                            start=(cidx == 0), stop=(cidx == 7)),
                            waits=w,
                            inc=s_pe if (half == 1 and cidx == 7) else None)
                if proj == 0:
                    mK = s_pe.n
                else:
                    mV = s_pe.n
            kw = [(s_pe, mK)]
            if t >= 2:
                kw.append((s_pe, mKv[t - 2]))
            mKstt[t] = emit_eluplus1(
                p, [(KA[0], kf[par][:, 0:512], 0), (KA[1], kf[par][:, 512:1024], 512)],
                scr_min, scr_exp, kw, s_dve, s_act)
            for half in range(2):
                dst = v_ext[par][:].rearrange("p (h e) -> p h e", h=NH)[
                    :, 8 * half : 8 * half + 8, 0:64]
                src = VA[half][:].rearrange("p (h e) -> p h e", h=8)
                w = []
                if half == 0:
                    w = [(s_pe, mV), const_w]
                    if t >= 2:
                        w.append((s_pe, mKv[t - 2]))
                mVev[t] = p.emit("scalar", lambda e, dst=dst, src=src: e.activation(
                    dst, src, AF.Identity, bias=0.0, scale=1.0),
                    waits=w, inc=s_act)
        t_last = 4 * g + 3
        if g == 3:
            emit_kv(t_last)
        for co in range(8):
            bank = qbanks[co % 4]
            w = []
            if co == 0:
                w = [(s_wq, 16), (s_dve, mKstt[t_last])]
            elif co == 2:
                w = [(s_act, mVev[t_last])]
            elif co >= 4:
                w = [(s_dve, mQstt[co - 4])]
            for cidx in range(8):
                p.emit("tensor", lambda e, bank=bank, cidx=cidx, co=co: e.matmul(
                    bank[:],
                    wq_sb[:, cidx, co * 128 : co * 128 + 128],
                    xlnT[:, cidx, :],
                    start=(cidx == 0), stop=(cidx == 7)),
                    waits=w if cidx == 0 else [],
                    inc=s_pe if cidx == 7 else None)
            mQ = s_pe.n
            mQstt[co] = emit_eluplus1(
                p, [(bank, qf_sb[:, co, g * 512 : g * 512 + 512], 0)],
                scr_min, scr_exp, [(s_pe, mQ)], s_dve, s_act)
        mQdone[0] = s_pe.n

    p.emit("sync", lambda e: e.dma_start(out=kvp_out, in_=kv_sb[:]),
           waits=[(s_dve, mAdd[NT - 1])], inc=s_out, n=16)
    p.emit("sync", lambda e: e.dma_start(
        out=qf_out, in_=qf_sb[:].rearrange("p c t -> p (c t)")),
        waits=[(s_dve, mQstt[7])], inc=s_out, n=16)
    nw = 32
    if dbg:
        dbg_o = p.dram("dbg", [128, 6 * NT], F32, "ExternalOutput")
        dbgT_o = p.dram("dbgT", [128, 8 * 512], BF16, "ExternalOutput")
        for j, key in enumerate(("sums", "sumsq", "mean", "nvar", "rstd", "nmr")):
            p.emit("sync", lambda e, j=j, key=key: e.dma_start(
                out=dbg_o[:, j * NT : (j + 1) * NT], in_=st[key][:]),
                inc=s_out, n=16)
        p.emit("sync", lambda e: e.dma_start(
            out=dbgT_o, in_=xlnT[:].rearrange("p c t -> p (c t)")),
            inc=s_out, n=16)
        dbg2_o = p.dram("dbg2", [128, 128], BF16, "ExternalOutput")
        dbg3_o = p.dram("dbg3", [128, 8 * D], BF16, "ExternalOutput")
        dbg4_o = p.dram("dbg4", [128, D], F32, "ExternalOutput")
        p.emit("sync", lambda e: e.dma_start(out=dbg2_o, in_=identb[:]),
               inc=s_out, n=16)
        p.emit("sync", lambda e: e.dma_start(
            out=dbg3_o, in_=wk_sb[:].rearrange("p c j -> p (c j)")),
            inc=s_out, n=16)
        p.emit("sync", lambda e: e.dma_start(out=dbg4_o, in_=x_sb[:, 15, :]),
               inc=s_out, n=16)
        nw += 10 * 16
    p.emit("sync", lambda e: e.wait_ge(s_out.h, nw))
    return p



def build_L2(dbg=False):
    p = Prog()
    x_in = p.dram("x", [TPC, D], F32, "ExternalInput")
    qf_in = p.dram("qf", [128, 8 * TPC], BF16, "ExternalInput")
    kvall_in = p.dram("kvall", [NCORES * 128, 520], F32, "ExternalInput")
    wo_in = p.dram("woT", [D, D], F32, "ExternalInput")
    vecs_in = p.dram("vecs", [1, 4 * D + 128], F32, "ExternalInput")
    id_in = p.dram("identb", [128, 128], BF16, "ExternalInput")
    idf_in = p.dram("identf", [128, 128], F32, "ExternalInput")
    cst_in = p.dram("cst", [128, 8], F32, "ExternalInput")
    msk_in = p.dram("msk", [128, 2], F32, "ExternalInput")
    x2_out = p.dram("x2", [TPC, D], F32, "ExternalOutput")
    ab_out = p.dram("ab", [128, 2 * NT], F32, "ExternalOutput")
    gp_out = p.dram("gp", [1, 8], F32, "ExternalOutput")

    cstb = p.sb("cstb", [128, 8], F32)
    register_consts(p, cstb)
    identb = p.sb("identb_sb", [128, 128], BF16)
    identf = p.sb("identf_sb", [128, 128], F32)
    mskb = p.sb("mskb", [128, 2], F32)
    x_sb = p.sb("x_sb", [128, NT, D], F32)
    qf_sb = p.sb("qf_sb", [128, 8, TPC], BF16)
    kvall = p.sb("kvall_sb", [128, 8, 520], F32)
    kvsel = p.sb("kvsel", [128, 520], F32)
    kvb = p.sb("kvb", [128, 520], F32)
    R_bf = p.sb("R_bf", [128, 8 * 130], BF16)
    wo_sb = p.sb("wo_sb", [128, 8, D], BF16)
    vecs = p.sb("vecs_sb", [1, 4 * D + 128], F32)
    bo_full = p.sb("bo_full", [128, D], F32)
    g_full = p.sb("g_full", [128, D], F32)
    b_full = p.sb("b_full", [128, D], F32)
    den = p.sb("den", [128, 16], F32)
    dinv = p.sb("dinv", [128, 16], F32)
    oh_bf = [p.sb(f"oh{i}", [128, D], BF16) for i in range(2)]
    ohT = [p.sb(f"ohT{i}", [128, 8, 128], BF16) for i in range(2)]
    tmp = p.sb("tmp", [128, D], F32)
    xln2 = p.sb("xln2", [128, D], F32)
    st = make_stats(p, "l2")
    l3sum = p.sb("l3sum", [128, NT], F32)
    l3ex2 = p.sb("l3ex2", [128, NT], F32)
    l3sumsq = p.sb("l3sumsq", [128, NT], F32)
    mean3 = p.sb("mean3", [128, NT], F32)
    nvar3 = p.sb("nvar3", [128, NT], F32)
    lnv3 = p.sb("lnv3", [128, NT], F32)
    rstd3 = p.sb("rstd3", [128, NT], F32)
    nm3r = p.sb("nm3r", [128, NT], F32)
    u3 = p.sb("u3", [128, NT], F32)
    ab = p.sb("ab_b", [128, 2 * NT], F32)
    xnb = p.sb("xnb", [128, D], F32)
    gm = p.sb("gm", [128, NT], F32)
    gmx = p.sb("gmx", [128, 1], F32)
    gp_sb = p.sb("gp_sb", [1, 8], F32)

    QA, QB, QC = p.psum("QA"), p.psum("QB"), p.psum("QC")
    T0, T1 = p.psum("T0"), p.psum("T1")
    WA, WB = p.psum("WA"), p.psum("WB")

    s_cst, s_id, s_idf = p.sem("s_cst"), p.sem("s_id"), p.sem("s_idf")
    s_msk, s_vec = p.sem("s_msk"), p.sem("s_vec")
    s_x = [p.sem(f"s_x{i}") for i in range(4)]
    s_qf, s_kv, s_wo = p.sem("s_qf"), p.sem("s_kv"), p.sem("s_wo")
    s_pe, s_dve, s_act, s_pool = (p.sem("s_pe"), p.sem("s_dve"),
                                  p.sem("s_act"), p.sem("s_pool"))
    s_out = p.sem("s_out")
    p.set_engine_sems(vector=s_dve, scalar=s_act, gpsimd=s_pool)

    for dst, src, sm in ((cstb, cst_in, s_cst), (identb, id_in, s_id),
                         (identf, idf_in, s_idf), (mskb, msk_in, s_msk),
                         (vecs, vecs_in, s_vec)):
        p.emit("sync", lambda e, dst=dst, src=src: e.dma_start(out=dst[:], in_=src),
               inc=sm, n=16)
    for g in range(4):
        src = x_in[512 * g : 512 * (g + 1), :].rearrange("(a p) d -> p a d", p=128)
        p.emit("sync", lambda e, src=src, g=g: e.dma_start(
            out=x_sb[:, 4 * g : 4 * g + 4, :], in_=src), inc=s_x[g], n=16)
    p.emit("sync", lambda e: e.dma_start(
        out=qf_sb[:].rearrange("p c t -> p (c t)"), in_=qf_in), inc=s_qf, n=16)
    p.emit("sync", lambda e: e.dma_start(
        out=kvall[:], in_=kvall_in.rearrange("(r p) c -> p r c", p=128)),
        inc=s_kv, n=16)
    p.emit("gpsimd", lambda e: e.dma_start(
        out=wo_sb[:], in_=wo_in.rearrange("(c p) j -> p c j", p=128)),
        inc=s_wo, n=16, auto=False)
    const_w = (s_cst, 16)

    p.emit("vector", lambda e: e.tensor_add(
        kvsel[:], kvall[:, 0, :], kvall[:, 1, :]), waits=[(s_kv, 16)],
        inc=s_dve)
    for r in (2, 3):
        p.emit("vector", lambda e, r=r: e.tensor_add(
            kvsel[:], kvsel[:], kvall[:, r, :]), inc=s_dve)
    p.emit("vector", lambda e: e.tensor_add(
        kvb[:], kvall[:, 4, :], kvall[:, 5, :]), inc=s_dve)
    for r in (6, 7):
        p.emit("vector", lambda e, r=r: e.tensor_add(
            kvb[:], kvb[:], kvall[:, r, :]), inc=s_dve)
    p.emit("vector", lambda e: e.tensor_scalar(
        kvsel[:], kvsel[:], mskb[:, 0:1], None, ALU.mult),
        waits=[(s_msk, 16)], inc=s_dve)
    m_sel = p.emit("vector", lambda e: e.scalar_tensor_tensor(
        kvsel[:], kvb[:], mskb[:, 1:2], kvsel[:], ALU.mult, ALU.add), inc=s_dve)
    m_gz = p.emit("gpsimd", lambda e: e.memset(gp_sb[:], 0.0), inc=s_pool)
    m_z = p.emit("gpsimd", lambda e: e.memset(R_bf, 0.0), inc=s_pool)
    mR = None
    for cc in range(8):
        for pr in range(2):
            rs = slice(pr * 64, pr * 64 + 64)
            dcol = 130 * cc + pr * 65
            scol = 65 * cc
            mR = p.emit("gpsimd", lambda e, rs=rs, dcol=dcol, scol=scol: e.tensor_copy(
                R_bf[rs, dcol : dcol + 65], kvsel[rs, scol : scol + 65]),
                waits=[(s_dve, m_sel), (s_pool, m_z)] if (cc == 0 and pr == 0) else [],
                inc=s_pool)

    onesrow = vecs[0:1, 4 * D : 4 * D + 128]
    m_bo = emit_k1_broadcast_row(p, onesrow, vecs[0:1, 0:D], bo_full, QA,
                                 [(s_vec, 16)], s_pe, s_dve)
    m_bc1 = emit_k1_broadcast_row(p, onesrow, vecs[0:1, D : 2 * D], g_full, QA,
                                  [(s_dve, m_bo)], s_pe, s_dve)
    m_bc2 = emit_k1_broadcast_row(p, onesrow, vecs[0:1, 2 * D : 3 * D], b_full,
                                  QA, [(s_dve, m_bc1)], s_pe, s_dve)

    qk_banks = [(QA, 0), (QA, 130), (QA, 260), (QB, 0), (QB, 130), (QB, 260),
                (QC, 0), (QC, 130)]
    mDiv = [None] * NT
    mT2 = [None] * NT
    mEvT2 = [None] * NT
    mWo = [None] * NT
    mRes = [None] * NT
    m_sq3_prev = None
    m_gm_prev = None

    for t in range(NT):
        par = t % 2
        xt = x_sb[:, t, :]
        for cc in range(8):
            bank, coff = qk_banks[cc]
            w = []
            if cc == 0:
                if t == 0:
                    w = [(s_pool, mR), (s_qf, 16), (s_dve, m_bc2)]
                else:
                    w = [(s_dve, mDiv[t - 1])]
            p.emit("tensor", lambda e, bank=bank, coff=coff, cc=cc, t=t: e.matmul(
                bank[:, coff : coff + 130],
                qf_sb[:, cc, t * 128 : t * 128 + 128],
                R_bf[:, 130 * cc : 130 * cc + 130],
                start=True, stop=True),
                waits=w, inc=s_pe if cc == 7 else None)
        mQkv = s_pe.n

        for gi, (bank, nch, c0) in enumerate(((QA, 3, 0), (QB, 3, 3), (QC, 2, 6))):
            srcg = bank[:, 0 : 130 * nch].rearrange(
                "p (c q e) -> p c q e", c=nch, q=2)[:, :, :, 64:65]
            dstg = den[:, 2 * c0 : 2 * (c0 + nch)].rearrange(
                "p (c q) -> p c q", c=nch).unsqueeze(3)
            p.emit("vector", lambda e, srcg=srcg, dstg=dstg: e.tensor_scalar(
                dstg, srcg, DEN_EPS, None, ALU.add),
                waits=[(s_pe, mQkv)] if gi == 0 else [], inc=s_dve)
        p.emit("vector", lambda e: e.reciprocal(dinv[:], den[:]), inc=s_dve)

        ow = []
        if t >= 2:
            ow = [(s_pe, mT2[t - 2])]
        for i, (bank, nch, c0) in enumerate(((QA, 3, 0), (QB, 3, 3), (QC, 2, 6))):
            src = bank[:, 0 : 130 * nch].rearrange(
                "p (c q e) -> p c q e", c=nch, q=2)[:, :, :, 0:64]
            dst = oh_bf[par][:, 128 * c0 : 128 * (c0 + nch)].rearrange(
                "p (c q e) -> p c q e", c=nch, q=2)
            di = dinv[:, 2 * c0 : 2 * (c0 + nch)].rearrange(
                "p (c q) -> p c q", c=nch).unsqueeze(3).broadcast_to(
                [128, nch, 2, 64])
            mDiv[t] = p.emit("vector", lambda e, src=src, dst=dst, di=di: e.tensor_tensor(
                dst, src, di, ALU.mult), waits=ow if i == 0 else [], inc=s_dve)

        tw = [(s_dve, mDiv[t])]
        if t == 0:
            tw.append((s_id, 16))
        if t >= 1:
            tw.append((s_dve, mEvT2[t - 1]))
        mT2[t] = emit_transposes(p, oh_bf[par], (T0, T1), identb, tw, s_pe)
        ew = [(s_pe, mT2[t])]
        if t >= 2:
            ew.append((s_pe, mWo[t - 2]))
        mEvT2[t] = emit_evac_T(p, (T0, T1), ohT[par], 0, ew, s_dve)
        for half in range(2):
            bank = (WA, WB)[half]
            for cidx in range(8):
                w = []
                if half == 0 and cidx == 0:
                    w = [(s_dve, mEvT2[t]), (s_wo, 16)]
                    if t >= 1:
                        w.append((s_dve, mRes[t - 1]))
                p.emit("tensor", lambda e, bank=bank, cidx=cidx, half=half, par=par: e.matmul(
                    bank[:], ohT[par][:, cidx, :],
                    wo_sb[:, cidx, half * 512 : half * 512 + 512],
                    start=(cidx == 0), stop=(cidx == 7)),
                    waits=w, inc=s_pe if (half == 1 and cidx == 7) else None)
        mWo[t] = s_pe.n
        for half in range(2):
            bank = (WA, WB)[half]
            p.emit("vector", lambda e, bank=bank, half=half: e.tensor_add(
                tmp[:, half * 512 : half * 512 + 512], bank[:],
                bo_full[:, half * 512 : half * 512 + 512]),
                waits=[(s_pe, mWo[t])] if half == 0 else [], inc=s_dve)
        mRes[t] = p.emit("vector", lambda e, xt=xt, t=t: e.scalar_tensor_tensor(
            xt, tmp[:], 1.0, xt, ALU.mult, ALU.add,
            accum_out=st["sums"][:, t : t + 1]),
            waits=[(s_x[t // 4], 16)], inc=s_dve)
        m_nmr = emit_ln_stats(p, t, xt, st, s_dve, s_act, s_pool,
                              [(s_dve, mRes[t])], EPS12,
                              sums_ready=(s_dve, mRes[t]), cw=const_w)
        c = slice(t, t + 1)
        xw = [(s_dve, m_nmr)]
        if m_sq3_prev is not None:
            xw.append((s_dve, m_sq3_prev))
        m_x2t = p.emit("scalar", lambda e, xt=xt, c=c: e.activation(
            xln2[:], xt, AF.Identity, bias=st["nmr"][:, c],
            scale=st["rstd"][:, c], accum_out=l3sum[:, c]),
            waits=xw, inc=s_act)
        m_sq3 = p.emit("vector", lambda e, c=c: e.scalar_tensor_tensor(
            st["sq"][:], xln2[:], 1.0, xln2[:], ALU.mult, ALU.mult,
            accum_out=l3sumsq[:, c]), waits=[(s_act, m_x2t)], inc=s_dve)
        m_sq3_prev = m_sq3
        p.emit("vector", lambda e, c=c: e.tensor_scalar(
            l3ex2[:, c], l3sumsq[:, c], INV_D, None, ALU.mult), inc=s_dve)
        p.emit("vector", lambda e, c=c: e.tensor_scalar(
            mean3[:, c], l3sum[:, c], INV_D, None, ALU.mult), inc=s_dve)
        p.emit("vector", lambda e, c=c: e.tensor_mul(
            nvar3[:, c], mean3[:, c], mean3[:, c]), inc=s_dve)
        m_nv3 = p.emit("vector", lambda e, c=c: e.tensor_sub(
            nvar3[:, c], nvar3[:, c], l3ex2[:, c]), inc=s_dve)
        p.emit("scalar", lambda e, c=c: e.activation(
            lnv3[:, c], nvar3[:, c], AF.Ln, bias=EPS3, scale=-1.0),
            waits=[(s_dve, m_nv3)], inc=s_act)
        m_r3 = p.emit("scalar", lambda e, c=c: e.activation(
            rstd3[:, c], lnv3[:, c], AF.Exp, bias=0.0, scale=-0.5), inc=s_act)
        m_nm3 = p.emit("vector", lambda e, c=c: e.scalar_tensor_tensor(
            nm3r[:, c], mean3[:, c], -1.0, rstd3[:, c], ALU.mult, ALU.mult),
            waits=[(s_act, m_r3)], inc=s_dve)
        p.emit("vector", lambda e, c=c, t=t: e.tensor_mul(
            ab[:, 2 * t : 2 * t + 1], st["rstd"][:, c], rstd3[:, c]), inc=s_dve)
        p.emit("vector", lambda e, c=c: e.scalar_tensor_tensor(
            u3[:, c], mean3[:, c], -1.0, st["nmr"][:, c], ALU.mult, ALU.add),
            inc=s_dve)
        p.emit("vector", lambda e, c=c, t=t: e.tensor_mul(
            ab[:, 2 * t + 1 : 2 * t + 2], u3[:, c], rstd3[:, c]), inc=s_dve)
        m_sq3_prev = p.emit("vector", lambda e, c=c: e.tensor_scalar(
            tmp[:], xln2[:], rstd3[:, c], nm3r[:, c], ALU.mult, ALU.add),
            waits=[(s_dve, m_nm3)], inc=s_dve)
        xgw = [(s_dve, m_bc1)]
        m_xg = p.emit("vector", lambda e: e.scalar_tensor_tensor(
            xnb[:], tmp[:], 1.0, g_full[:], ALU.mult, ALU.mult),
            waits=xgw, inc=s_dve)
        m_add = p.emit("gpsimd", lambda e: e.tensor_add(xnb[:], xnb[:], b_full[:]),
               waits=[(s_dve, m_xg)], inc=s_pool)
        m_gm_prev = p.emit("vector", lambda e, c=c: e.reduce_max(
            gm[:, c], xnb[:], axis=AXX, apply_absolute_value=True),
            waits=[(s_pool, m_add)], inc=s_dve)

    m_gmx = p.emit("vector", lambda e: e.reduce_max(gmx[:], gm[:], axis=AXX),
                   inc=s_dve)
    m_tp = p.emit("tensor", lambda e: e.matmul(
        T0[0:1, 0:128], gmx[:], identf[:], start=True, stop=True),
        waits=[(s_dve, s_dve.n), (s_idf, 16)], inc=s_pe)
    m_gp = p.emit("vector", lambda e: e.reduce_max(
        gp_sb[0:1, 0:1], T0[0:1, 0:128], axis=AXX), waits=[(s_pe, m_tp)],
        inc=s_dve)

    for g in range(4):
        dst = x2_out[512 * g : 512 * (g + 1), :].rearrange("(a p) d -> p a d", p=128)
        p.emit("sync", lambda e, dst=dst, g=g: e.dma_start(
            out=dst, in_=x_sb[:, 4 * g : 4 * g + 4, :]),
            waits=[(s_dve, mRes[NT - 1])] if g == 0 else [], inc=s_out, n=16)
    p.emit("sync", lambda e: e.dma_start(out=ab_out, in_=ab[:]),
           waits=[(s_dve, s_dve.n)], inc=s_out, n=16)
    p.emit("sync", lambda e: e.dma_start(out=gp_out, in_=gp_sb[:]),
           waits=[(s_dve, m_gp), (s_pool, m_gz)], inc=s_out, n=16)
    nw = 96
    if dbg:
        d1 = p.dram("d_kvsel", [128, 520], F32, "ExternalOutput")
        d2 = p.dram("d_R", [128, 8 * 130], BF16, "ExternalOutput")
        d3 = p.dram("d_den", [128, 32], F32, "ExternalOutput")
        p.emit("sync", lambda e: e.dma_start(out=d1, in_=kvsel[:]), inc=s_out, n=16)
        p.emit("sync", lambda e: e.dma_start(out=d2, in_=R_bf[:]), inc=s_out, n=16)
        p.emit("sync", lambda e: e.dma_start(out=d3[:, 0:16], in_=den[:]), inc=s_out, n=16)
        p.emit("sync", lambda e: e.dma_start(out=d3[:, 16:32], in_=dinv[:]), inc=s_out, n=16)
        nw += 64
    p.emit("sync", lambda e: e.wait_ge(s_out.h, nw))
    return p



def build_L3(dbg=False):
    p = Prog()
    x2_in = p.dram("x2", [TPC, D], F32, "ExternalInput")
    ab_in = p.dram("ab", [128, 2 * NT], F32, "ExternalInput")
    gp_in = p.dram("gps", [1, 8], F32, "ExternalInput")
    wf_in = p.dram("wf", [D, D], F32, "ExternalInput")
    vecs_in = p.dram("vecs", [1, 4 * D + 128], F32, "ExternalInput")
    id_in = p.dram("identb", [128, 128], BF16, "ExternalInput")
    cst_in = p.dram("cst", [128, 8], F32, "ExternalInput")
    out_o = p.dram("out", [TPC, D], F32, "ExternalOutput")

    cstb = p.sb("cstb", [128, 8], F32)
    register_consts(p, cstb)
    identb = p.sb("identb_sb", [128, 128], BF16)
    x_sb = p.sb("x_sb", [128, NT, D], F32)
    ab_sb = p.sb("ab_sb", [128, 2 * NT], F32)
    gp_sb = p.sb("gp_sb", [1, 8], F32)
    wf_sb = p.sb("wf_sb", [128, 8, D], F32)
    wq_sb = p.sb("wq_sb", [128, 8, D], BF16)
    vecs = p.sb("vecs_sb", [1, 4 * D + 128], F32)
    g_full = p.sb("g_full", [128, D], F32)
    b_full = p.sb("b_full", [128, D], F32)
    scal = p.sb("scal", [1, 8], F32)
    sxq_b = p.sb("sxq_b", [128, 1], F32)
    sbe_b = p.sb("sbe_b", [128, 1], F32)
    sy_b = p.sb("sy_b", [128, 1], F32)
    bsum = p.sb("bsum", [128, 8], F32)
    bsum1 = p.sb("bsum1", [128, 1], F32)
    wscr = p.sb("wscr", [128, D], F32)
    tmp = p.sb("tmp", [128, D], F32)
    tmp2 = p.sb("tmp2", [128, D], F32)
    xq_bf = [p.sb(f"xq{i}", [128, D], BF16) for i in range(2)]
    xqT = [p.sb(f"xqT{i}", [128, 8, 128], BF16) for i in range(2)]
    ytmp = p.sb("ytmp", [128, D], F32)

    T0, T1 = p.psum("T0"), p.psum("T1")
    FA, FB = p.psum("FA"), p.psum("FB")
    BC = p.psum("BC")

    s_cst, s_id, s_ab2 = p.sem("s_cst"), p.sem("s_id"), p.sem("s_ab2")
    s_gp, s_vec = p.sem("s_gp"), p.sem("s_vec")
    s_x = [p.sem(f"s_x{i}") for i in range(4)]
    s_wf = p.sem("s_wf")
    s_pe, s_dve, s_act, s_pool = (p.sem("s_pe"), p.sem("s_dve"),
                                  p.sem("s_act"), p.sem("s_pool"))
    s_out = p.sem("s_out")
    p.set_engine_sems(vector=s_dve, scalar=s_act, gpsimd=s_pool)

    for dst, src, sm in ((cstb, cst_in, s_cst), (identb, id_in, s_id),
                         (ab_sb, ab_in, s_ab2), (gp_sb, gp_in, s_gp),
                         (vecs, vecs_in, s_vec)):
        p.emit("sync", lambda e, dst=dst, src=src: e.dma_start(out=dst[:], in_=src),
               inc=sm, n=16)
    for g in range(4):
        src = x2_in[512 * g : 512 * (g + 1), :].rearrange("(a p) d -> p a d", p=128)
        p.emit("sync", lambda e, src=src, g=g: e.dma_start(
            out=x_sb[:, 4 * g : 4 * g + 4, :], in_=src), inc=s_x[g], n=16)
    m_wf = p.emit("sync", lambda e: e.dma_start(
        out=wf_sb[:], in_=wf_in.rearrange("(c p) j -> p c j", p=128)),
        inc=s_wf, n=16)
    const_w = (s_cst, 16)

    onesrow = vecs[0:1, 4 * D : 4 * D + 128]
    p.emit("vector", lambda e: e.reduce_max(
        scal[0:1, 0:1], gp_sb[0:1, :], axis=AXX), waits=[(s_gp, 16)], inc=s_dve)
    p.emit("vector", lambda e: e.reciprocal(scal[0:1, 1:2], scal[0:1, 0:1]),
           inc=s_dve)
    p.emit("vector", lambda e: e.tensor_scalar(
        scal[0:1, 1:2], scal[0:1, 1:2], 128.0, None, ALU.mult), inc=s_dve)
    for cidx in range(8):
        p.emit("vector", lambda e, cidx=cidx: e.reduce_sum(
            bsum[:, cidx : cidx + 1], wf_sb[:, cidx, :], axis=AXX,
            apply_absolute_value=True),
            waits=[(s_wf, 16)] if cidx == 0 else [], inc=s_dve)
    m_bs1 = p.emit("vector", lambda e: e.reduce_sum(bsum1[:], bsum[:], axis=AXX),
                   inc=s_dve)
    m_bmm = p.emit("tensor", lambda e: e.matmul(
        BC[0:1, 0:1], bsum1[:], cstb[:, 5:6], start=True, stop=True),
        waits=[(s_dve, m_bs1), const_w], inc=s_pe)
    p.emit("vector", lambda e: e.tensor_scalar(
        scal[0:1, 2:3], BC[0:1, 0:1], 1.0 / (D * D), None, ALU.mult),
        waits=[(s_pe, m_bmm)], inc=s_dve)
    p.emit("vector", lambda e: e.tensor_scalar(
        scal[0:1, 3:4], scal[0:1, 2:3], BETA_EPS, None, ALU.add), inc=s_dve)
    p.emit("vector", lambda e: e.reciprocal(scal[0:1, 3:4], scal[0:1, 3:4]),
           inc=s_dve)
    p.emit("vector", lambda e: e.tensor_mul(
        scal[0:1, 4:5], scal[0:1, 0:1], scal[0:1, 2:3]), inc=s_dve)
    m_sy = p.emit("vector", lambda e: e.tensor_scalar(
        scal[0:1, 4:5], scal[0:1, 4:5], 1.0 / 128.0, None, ALU.mult), inc=s_dve)
    m_prev = m_sy
    for col, dstb in ((1, sxq_b), (3, sbe_b), (4, sy_b)):
        m_pe = p.emit("tensor", lambda e, col=col: e.matmul(
            BC[:, 0:1], onesrow, scal[0:1, col : col + 1],
            start=True, stop=True),
            waits=[(s_dve, m_prev), (s_vec, 16)], inc=s_pe)
        m_prev = p.emit("vector", lambda e, dstb=dstb: e.tensor_copy(
            dstb[:], BC[:, 0:1]), waits=[(s_pe, m_pe)], inc=s_dve)
    m_bcasts = m_prev
    m_wq = None
    for cidx in range(8):
        p.emit("gpsimd", lambda e, cidx=cidx: e.tensor_scalar(
            wscr[:], wf_sb[:, cidx, :], sbe_b[:], MAGIC, ALU.mult, ALU.add),
            waits=[(s_dve, m_bcasts)] if cidx == 0 else [], inc=s_pool)
        p.emit("gpsimd", lambda e: e.tensor_scalar(
            wscr[:], wscr[:], MAGIC, 1.0, ALU.subtract, ALU.min), inc=s_pool)
        m_wq = p.emit("gpsimd", lambda e, cidx=cidx: e.tensor_scalar(
            wq_sb[:, cidx, :], wscr[:], -1.0, None, ALU.max), inc=s_pool)
    m_bc1 = emit_k1_broadcast_row(p, onesrow, vecs[0:1, D : 2 * D], g_full, BC,
                                  [(s_vec, 16), (s_dve, m_bcasts)], s_pe, s_dve)
    m_bc2 = emit_k1_broadcast_row(p, onesrow, vecs[0:1, 2 * D : 3 * D], b_full,
                                  BC, [(s_dve, m_bc1)], s_pe, s_dve)

    mT3 = [None] * NT
    mEvT3 = [None] * NT
    mF = [None] * NT
    mRes = [None] * NT
    m_xq_prev = None
    m_y_prev = None
    for t in range(NT):
        par = t % 2
        xt = x_sb[:, t, :]
        w0 = [(s_x[t // 4], 16), (s_ab2, 16), (s_dve, m_bc2)]
        if m_xq_prev is not None:
            w0.append((s_pool, m_xq_prev))
        p.emit("vector", lambda e, xt=xt, t=t: e.tensor_scalar(
            tmp[:], xt, ab_sb[:, 2 * t : 2 * t + 1],
            ab_sb[:, 2 * t + 1 : 2 * t + 2], ALU.mult, ALU.add),
            waits=w0, inc=s_dve)
        p.emit("vector", lambda e: e.scalar_tensor_tensor(
            tmp[:], tmp[:], 1.0, g_full[:], ALU.mult, ALU.mult), inc=s_dve)
        p.emit("vector", lambda e: e.tensor_add(tmp[:], tmp[:], b_full[:]),
               inc=s_dve)
        m_clip = p.emit("vector", lambda e: e.tensor_scalar(
            tmp2[:], tmp[:], sxq_b[:], -CLIP, ALU.mult, ALU.max), inc=s_dve)
        xqw = [(s_dve, m_clip)]
        if t >= 2:
            xqw.append((s_pe, mT3[t - 2]))
        m_xq_prev = p.emit("gpsimd", lambda e, par=par: e.tensor_scalar(
            xq_bf[par][:], tmp2[:], CLIP, None, ALU.min), waits=xqw, inc=s_pool)
        tw = [(s_pool, m_xq_prev)]
        if t == 0:
            tw.append((s_id, 16))
        if t >= 1:
            tw.append((s_dve, mEvT3[t - 1]))
        mT3[t] = emit_transposes(p, xq_bf[par], (T0, T1), identb, tw, s_pe)
        ew = [(s_pe, mT3[t])]
        if t >= 2:
            ew.append((s_pe, mF[t - 2]))
        mEvT3[t] = emit_evac_T(p, (T0, T1), xqT[par], 0, ew, s_dve)
        for half in range(2):
            bank = (FA, FB)[half]
            for cidx in range(8):
                w = []
                if half == 0 and cidx == 0:
                    w = [(s_dve, mEvT3[t]), (s_pool, m_wq)]
                    if m_y_prev is not None:
                        w.append((s_act, m_y_prev))
                p.emit("tensor", lambda e, bank=bank, cidx=cidx, half=half, par=par: e.matmul(
                    bank[:], xqT[par][:, cidx, :],
                    wq_sb[:, cidx, half * 512 : half * 512 + 512],
                    start=(cidx == 0), stop=(cidx == 7)),
                    waits=w, inc=s_pe if (half == 1 and cidx == 7) else None)
        mF[t] = s_pe.n
        for half in range(2):
            bank = (FA, FB)[half]
            w = [(s_pe, mF[t])] if half == 0 else []
            if half == 0 and t == 0:
                w.append((s_cst, 16))
            if half == 0 and t >= 1:
                w.append((s_dve, mRes[t - 1]))
            m_y_prev = p.emit("scalar", lambda e, bank=bank, half=half: e.activation(
                ytmp[:, half * 512 : half * 512 + 512], bank[:], AF.Identity,
                bias=0.0, scale=sy_b[:]), waits=w, inc=s_act)
        mRes[t] = p.emit("vector", lambda e, xt=xt: e.tensor_add(
            xt, ytmp[:], xt), waits=[(s_act, m_y_prev)], inc=s_dve)
        dst = out_o[128 * t : 128 * (t + 1), :]
        p.emit("sync", lambda e, dst=dst, t=t: e.dma_start(
            out=dst, in_=x_sb[:, t, :]), waits=[(s_dve, mRes[t])],
            inc=s_out, n=16)
    nw = 16 * NT
    if dbg:
        d1 = p.dram("d_scal", [1, 8], F32, "ExternalOutput")
        d2 = p.dram("d_wq", [128, 8 * D], BF16, "ExternalOutput")
        d3 = p.dram("d_sb", [128, 4], F32, "ExternalOutput")
        d3s = p.sb("d3s", [128, 4], F32)
        p.emit("sync", lambda e: e.dma_start(out=d1, in_=scal[:]), inc=s_out, n=16)
        p.emit("sync", lambda e: e.dma_start(
            out=d2, in_=wq_sb[:].rearrange("p c j -> p (c j)")), inc=s_out, n=16)
        p.emit("vector", lambda e: e.tensor_copy(d3s[:, 0:1], sxq_b[:]))
        p.emit("vector", lambda e: e.tensor_copy(d3s[:, 1:2], sbe_b[:]))
        p.emit("vector", lambda e: e.tensor_copy(d3s[:, 2:3], sy_b[:]))
        p.emit("vector", lambda e: e.tensor_copy(d3s[:, 3:4], sy_b[:]))
        mneed = s_dve.n
        p.emit("sync", lambda e: e.dma_start(out=d3, in_=d3s[:]),
               waits=[(s_dve, mneed)], inc=s_out, n=16)
        nw += 48
    p.emit("sync", lambda e: e.wait_ge(s_out.h, nw))
    return p




def build_F(sim_local_exchange=False, dbg=False):
    p = Prog()
    x_in = p.dram("x", [TPC, D], F32, "ExternalInput")
    xe_in = p.dram("xe", [TPC, D], F32, "ExternalInput")
    wq_in = p.dram("wqT", [D, D], F32, "ExternalInput")
    wk_in = p.dram("wkT", [D, D], F32, "ExternalInput")
    wv_in = p.dram("wvT", [D, D], F32, "ExternalInput")
    wo_in = p.dram("woT", [D, D], F32, "ExternalInput")
    wf_in = p.dram("wf", [D, D], F32, "ExternalInput")
    bo_in = p.dram("bo_full", [128, D], F32, "ExternalInput")
    gf_in = p.dram("g_full", [128, D], F32, "ExternalInput")
    bf_in = p.dram("b_full", [128, D], F32, "ExternalInput")
    id_in = p.dram("identb", [128, 128], BF16, "ExternalInput")
    idf_in = p.dram("identf", [128, 128], F32, "ExternalInput")
    ones_in = p.dram("ones128", [1, 128], F32, "ExternalInput")
    cst_in = p.dram("cst", [128, 8], F32, "ExternalInput")
    selm_in = p.dram("selm", [32, 512], F32, "ExternalInput")
    nz_in = p.dram("nz", [1, 4], I32, "ExternalInput")
    out_o = p.dram("out", [TPC, D], F32, "ExternalOutput")
    if dbg:
        x2_dbg = p.dram("x2_dbg", [TPC, D], F32, "ExternalOutput")
        scr_dbg = p.dram("scr_dbg", [128, D], F32, "ExternalOutput")
        st_dbg = p.dram("st_dbg", [128, 160], F32, "ExternalOutput")
        wq_dbg = p.dram("wq_dbg", [128, 8192], BF16, "ExternalOutput")
    kvx = p.dram_shared("kvx", [2 * 128, 520], F32)
    kvf = p.dram_shared("kvf", [2, 4], I32)

    cstb = p.sb("cstb", [128, 8], F32)
    register_consts(p, cstb)
    identb = p.sb("identb_sb", [128, 128], BF16)
    identf = p.sb("identf_sb", [128, 128], F32)
    onesr = None
    snz = p.sb("snz", [1, 4], I32)
    sfl = p.sb("sfl", [1, 4], I32)
    x_sb = p.sb("x_sb", [128, NT, D], F32)
    xe_sb = p.sb("xe_sb", [128, 2, D], F32)
    wstage = p.sb("wstage", [128, 2, 1024], F32)
    big_a = p.sb("big_a", [128, 16384], BF16)
    wq_wo = p.sb("wq_wo", [128, 8192], BF16)
    xlnT = p.sb("xlnT", [128, 8, TPC], BF16)

    xln_bf = [p.sb(f"f_xlnbf{i}", [128, D], BF16) for i in range(2)]
    kf = [p.sb(f"f_kf{i}", [128, D], BF16) for i in range(2)]
    v_ext = [p.sb(f"f_vext{i}", [128, NH * 65], BF16) for i in range(2)]
    kv_sb = p.sb("f_kv_sb", [128, 520], F32)
    kvo_sb = xe_sb[:, 0, 0:520]
    R_bf = xe_sb[:, 1, 0:520].bitcast(BF16)
    bo_full = p.sb("f_bo_full", [128, D], F32)
    g_full = p.sb("f_g_full", [128, D], F32)
    b_full = p.sb("f_b_full", [128, D], F32)
    oh_bf = [p.sb(f"f_oh{i}", [128, D], BF16) for i in range(2)]
    xlnT_x = oh_bf[0][:].rearrange("p (c x) -> p c x", c=8)
    ohT = [xlnT[:, i, :].rearrange("p (c x) -> p c x", c=8) for i in range(2)]
    tmp32 = kf[0][:].bitcast(F32)[:, 0:32]
    dvT_sb = kf[1][:].bitcast(F32)[0:16, 0:256]
    selm_sb = [xln_bf[i][:].bitcast(F32)[0:16, 0:512] for i in range(2)]
    num_sb = p.sb("f_numsb", [128, 1024], F32)
    S0 = p.sb("f_S0", [128, D], F32)
    S1 = p.sb("f_S1", [128, D], F32)
    S2 = p.sb("f_S2", [128, D], F32)
    ytmp = S0
    st = make_stats(p, "f", n=32, sq=S0)
    l3sum = xe_sb[:, 0, 568:584]
    l3sumsq = xe_sb[:, 0, 584:600]
    mean3 = xe_sb[:, 0, 781:797]
    rstd3 = xe_sb[:, 0, 797:813]
    nm3r = xe_sb[:, 0, 813:829]
    u3 = xe_sb[:, 0, 829:833]
    ab = xe_sb[:, 0, 608:640]
    gm = xe_sb[:, 0, 552:568]
    gmx = xe_sb[:, 0, 776:777]
    scal = xe_sb[0:1, 0, 768:776]
    sxq_b = xe_sb[:, 0, 778:779]
    sbe_b = xe_sb[:, 0, 779:780]
    sy_b = xe_sb[:, 0, 780:781]
    bsum = xe_sb[:, 0, 600:608]
    sums_b = xe_sb[:, 0, 848:864]
    bsum1 = xe_sb[:, 0, 777:778]
    onesr = xe_sb[0:1, 0, 640:768]

    T0, T1 = p.psum("T0"), p.psum("T1")
    KA = (p.psum("KA0"), p.psum("KA1"))
    VA = (p.psum("VA0"), p.psum("VA1"))
    KV = (p.psum("KV0"), p.psum("KV1"))
    qbanks = [KA[0], KA[1], VA[0], VA[1]]
    WB2 = (KV[0], KV[1])
    BC = VA[1]

    s_cst, s_id, s_idf = p.sem("s_cst"), p.sem("s_id"), p.sem("s_idf")
    s_ones, s_nz = p.sem("s_ones"), p.sem("s_nz")
    s_msk = p.sem("s_msk")
    s_bo, s_gf, s_bf = p.sem("s_bo"), p.sem("s_gf"), p.sem("s_bf")
    s_x = [p.sem(f"s_x{i}") for i in range(4)]
    s_xe = [p.sem(f"s_xe{i}") for i in range(8)]
    s_wc = [p.sem(f"s_wc{i}") for i in range(8)]
    s_kvs = p.sem("s_kvs")
    s_pl = p.sem("s_pl")
    s_kvr = p.sem("s_kvr")
    s_pe, s_dve, s_act, s_pool = (p.sem("s_pe"), p.sem("s_dve"),
                                  p.sem("s_act"), p.sem("s_pool"))
    s_out = p.sem("s_out")
    p.set_engine_sems(vector=s_dve, scalar=s_act, gpsimd=s_pool)
    const_w = (s_cst, 16)

    for dst, srcp, sm in ((cstb, cst_in, s_cst), (identb, id_in, s_id),
                          (identf, idf_in, s_idf),
                          (snz, nz_in, s_nz), (bo_full, bo_in, s_bo),
                          (g_full, gf_in, s_gf), (b_full, bf_in, s_bf)):
        p.emit("sync", lambda e, dst=dst, srcp=srcp: e.dma_start(
            out=dst[:], in_=srcp), inc=sm, n=16)
    def emit_x_group(g):
        srcp = x_in[512 * g : 512 * (g + 1), :].rearrange(
            "(a p) d -> p a d", p=128)
        p.emit("sync", lambda e, srcp=srcp, g=g: e.dma_start(
            out=x_sb[:, 4 * g : 4 * g + 4, :], in_=srcp), inc=s_x[g], n=16)

    emit_x_group(0)

    wslot_free = [[], []]
    wstream_idx = [0]

    def stream_weight(w_in, consume):
        widx = wstream_idx[0]
        wstream_idx[0] += 1
        done = {s_act: 0, s_pool: 0, s_dve: 0}
        for c in range(8):
            srcp = w_in[128 * c : 128 * (c + 1), :]
            p.emit("sync", lambda e, srcp=srcp, c=c: e.dma_start(
                out=wstage[:, c % 2, :], in_=srcp),
                waits=list(wslot_free[c % 2]), inc=s_wc[c], n=16)
            marks = consume(c, wstage[:, c % 2, :], [(s_wc[c], 16 * (widx + 1))])
            wslot_free[c % 2] = marks
            for sm, mk in marks:
                done[sm] = max(done[sm], mk)
        return done

    def cast_consumer(dst_of, extra=None):
        def consume(c, src_ap, waits):
            if c % 2 == 0:
                w = waits + [const_w] + (extra if (extra and c == 0) else [])
                m = p.emit("scalar", lambda e, src_ap=src_ap, c=c: e.activation(
                    dst_of(c), src_ap, AF.Identity, bias=0.0, scale=1.0),
                    waits=w)
                return [(s_act, m)]
            w = waits + (extra if (extra and c == 1) else [])
            m = p.emit("gpsimd", lambda e, src_ap=src_ap, c=c: e.tensor_copy(
                dst_of(c), src_ap), waits=w)
            return [(s_pool, m)]
        return consume

    for i in range(2):
        ones_ap = v_ext[i][:].rearrange("p (h e) -> p h e", h=NH)[:, :, 64:65]
        p.emit("gpsimd", lambda e, ones_ap=ones_ap: e.memset(ones_ap, 1.0))
    m_ones_vext = s_pool.n

    dk = stream_weight(wk_in, cast_consumer(
        lambda c: big_a[:, c * 1024 : (c + 1) * 1024]))
    dv = stream_weight(wv_in, cast_consumer(
        lambda c: big_a[:, 8192 + c * 1024 : 8192 + (c + 1) * 1024]))
    emit_x_group(1)
    emit_x_group(2)
    dq = stream_weight(wq_in, cast_consumer(
        lambda c: wq_wo[:, c * 1024 : (c + 1) * 1024]))
    emit_x_group(3)
    m_wk_done = (dk[s_act], dk[s_pool])
    m_wv_done = (dv[s_act], dv[s_pool])
    m_wq_done = (dq[s_act], dq[s_pool])

    NKV = 2 * NT
    mT1 = [None] * NKV
    mEvT = [None] * NKV
    mKstt = [None] * NKV
    mKsttH = [None] * NKV
    mVevH = [None] * NKV
    mVev = [None] * NKV
    mKv = [None] * NKV
    mAdd = [None] * NKV
    mXdoneA = [None] * NKV
    mXdoneV = [None] * NKV

    for b in range(2):
        p.emit("vector", lambda e, b=b: e.memset(KV[b][:, 0:260], 0.0),
               inc=s_dve)

    def emit_kv(t):
        par = t % 2
        for h in range(NH):
            pr2 = h // 2
            bank = KV[pr2 // 4]
            rb = (h % 2) * 64
            cb = (pr2 % 4) * 65
            w = []
            if h == 0:
                w = [(s_dve, mKsttH[t]), (s_act, mVevH[t]),
                     (s_pool, m_ones_vext)]
            if h == 8:
                w = [(s_dve, mKstt[t]), (s_act, mVev[t])]
                if sim_local_exchange and t >= 1:
                    w.append((s_dve, mAdd[t - 1]))
            p.emit("tensor", lambda e, h=h, bank=bank, rb=rb, cb=cb, par=par,
                   t=t: e.matmul(
                bank[rb : rb + 64, cb : cb + 65],
                kf[par][:, h * DH : (h + 1) * DH],
                v_ext[par][:, h * 65 : (h + 1) * 65],
                start=sim_local_exchange,
                stop=(sim_local_exchange or t == NKV - 1)),
                waits=w, inc=s_pe if h == NH - 1 else None)
        mKv[t] = s_pe.n
        if sim_local_exchange:
            for b in range(2):
                if t == 0:
                    mAdd[t] = p.emit("vector", lambda e, b=b: e.tensor_copy(
                        kv_sb[:, b * 260 : (b + 1) * 260], KV[b][:, 0:260]),
                        waits=[(s_pe, mKv[t])] if b == 0 else [], inc=s_dve)
                else:
                    mAdd[t] = p.emit("vector", lambda e, b=b: e.tensor_add(
                        kv_sb[:, b * 260 : (b + 1) * 260],
                        KV[b][:, 0:260], kv_sb[:, b * 260 : (b + 1) * 260]),
                        waits=[(s_pe, mKv[t])] if b == 0 else [], inc=s_dve)
        elif t == NKV - 1:
            for b in range(2):
                mAdd[t] = p.emit("vector", lambda e, b=b: e.tensor_copy(
                    kv_sb[:, b * 260 : (b + 1) * 260], KV[b][:, 0:260]),
                    waits=[(s_pe, mKv[t])] if b == 0 else [], inc=s_dve)

    def emit_xln(t, xt, m_stats):
        par = t % 2
        xw = [(s_dve, m_stats)]
        if t >= 2:
            xw.append((s_pe, mT1[t - 2]))
        mXdoneA[t] = p.emit("vector", lambda e, t=t, par=par, xt=xt: e.tensor_scalar(
            xln_bf[par][:], xt, st["rstd"][:, t : t + 1],
            st["nmr"][:, t : t + 1], ALU.mult, ALU.add),
            waits=xw, inc=s_dve)

    def kv_tile_A(t, dstT, slot_off):
        par = t % 2
        tw = [(s_dve, mXdoneA[t])]
        if t == 0:
            tw.append((s_id, 16))
        if t >= 1:
            tw.append((s_act, mEvT[t - 1]))
        mTh = [None, None]
        for cidx in range(8):
            bank = (T0, T1)[cidx // 4]
            col = (cidx % 4) * 128
            p.emit("tensor", lambda e, cidx=cidx, bank=bank, col=col, par=par: e.matmul(
                bank[:, col : col + 128],
                xln_bf[par][:, cidx * 128 : (cidx + 1) * 128],
                identb[:], start=True, stop=True),
                waits=tw if cidx == 0 else [],
                inc=s_pe if cidx % 4 == 3 else None)
            if cidx % 4 == 3:
                mTh[cidx // 4] = s_pe.n
        mT1[t] = mTh[1]
        if t >= 1:
            emit_kv(t - 1)
        mEvH = [None, None]
        for b in range(2):
            src = (T0, T1)[b][:].rearrange("p (c x) -> p c x", c=4)
            dst = dstT[:, 4 * b : 4 * b + 4, slot_off : slot_off + 128]
            mEvH[b] = p.emit("scalar", lambda e, src=src, dst=dst: e.activation(
                dst, src, AF.Identity, bias=0.0, scale=1.0),
                waits=[(s_pe, mTh[b])], inc=s_act)
        mEvT[t] = mEvH[1]
        return mEvH

    def kv_tile_B(t, dstT, slot_off, mEvH):
        par = t % 2
        for proj, (wcols, banks, wdone, wsem_last) in enumerate((
                (0, KA, m_wk_done, None), (8192, VA, m_wv_done, None))):
            for half in range(2):
                bank = banks[half]
                for cidx in range(8):
                    w = []
                    if half == 0 and cidx == 0:
                        w = [(s_act, mEvH[0]),
                             (s_act, wdone[0]), (s_pool, wdone[1])]
                        if t >= 1:
                            w.append((s_dve, mKstt[t - 1]) if proj == 0
                                     else (s_act, mVev[t - 1]))
                    if proj == 0 and half == 0 and cidx == 4:
                        w = [(s_act, mEvH[1])]
                    p.emit("tensor", lambda e, bank=bank, cidx=cidx, half=half,
                           wcols=wcols, dstT=dstT, slot_off=slot_off: e.matmul(
                        bank[:],
                        dstT[:, cidx, slot_off : slot_off + 128],
                        big_a[:, wcols + cidx * 1024 + half * 512 :
                              wcols + cidx * 1024 + half * 512 + 512],
                        start=(cidx == 0), stop=(cidx == 7)),
                        waits=w,
                        inc=s_pe if cidx == 7 else None)
                if half == 0:
                    if proj == 0:
                        mKh0 = s_pe.n
                    else:
                        mVh0 = s_pe.n
            if proj == 0:
                mK = s_pe.n
            else:
                mV = s_pe.n
        kw0 = [(s_pe, mKh0)]
        if t >= 2:
            kw0.append((s_pe, mKv[t - 2]))
        mKsttH[t], mKstt[t] = emit_eluplus1(
            p, [(KA[0], kf[par][:, 0:512], 0, kw0),
                (KA[1], kf[par][:, 512:1024], 512, [(s_pe, mK)])],
            S1, S2, None, s_dve, s_act)
        for half in range(2):
            dst = v_ext[par][:].rearrange("p (h e) -> p h e", h=NH)[
                :, 8 * half : 8 * half + 8, 0:64]
            srcv = VA[half][:].rearrange("p (h e) -> p h e", h=8)
            w = [(s_pe, mVh0 if half == 0 else mV)]
            if half == 0:
                w.append(const_w)
                if t >= 2:
                    w.append((s_pe, mKv[t - 2]))
            m_v = p.emit("scalar", lambda e, dst=dst, srcv=srcv: e.activation(
                dst, srcv, AF.Identity, bias=0.0, scale=1.0),
                waits=w, inc=s_act)
            if half == 0:
                mVevH[t] = m_v
            else:
                mVev[t] = m_v
        mXdoneV[t] = mKstt[t]

    def tile_info(t):
        if t < NT:
            return x_sb[:, t, :], xlnT, t * 128
        pr, k = divmod(t - NT, 2)
        buf = xe_sb if pr % 2 == 0 else wstage
        return buf[:, k, :], xlnT_x, 0

    def tile_prep(t):
        if t < NT:
            g = t // 4
            for slot in range(4):
                tt = 4 * g + slot
                emit_tile_sums(p, tt, x_sb[:, tt, :], st, s_dve,
                               [(s_x[g], 16)])
            return emit_group_stats(p, slice(4 * g, 4 * g + 4), st, s_dve,
                                    s_act, EPS12, cw=const_w)
        pr = (t - NT) // 2
        buf = xe_sb if pr % 2 == 0 else wstage
        srcp = xe_in[256 * pr : 256 * (pr + 1), :].rearrange(
            "(a p) d -> p a d", p=128)
        w = []
        if pr == 1:
            w = [(s_act, m_wq_done[0]), (s_pool, m_wq_done[1])]
        if pr >= 2:
            jb = NT + 2 * (pr - 2) + 1
            w = [(s_dve, mXdoneA[jb] or 0), (s_dve, mXdoneV[jb] or 0)]
        p.emit("sync", lambda e, srcp=srcp, buf=buf: e.dma_start(
            out=buf[:], in_=srcp), waits=w, inc=s_xe[pr], n=16)
        for k in range(2):
            emit_tile_sums(p, NT + 2 * pr + k, buf[:, k, :], st, s_dve,
                           [(s_xe[pr], 16)])
        return emit_group_stats(p, slice(NT + 2 * pr, NT + 2 * pr + 2), st,
                                s_dve, s_act, EPS12, cw=const_w)

    m_gs = tile_prep(0)
    emit_xln(0, tile_info(0)[0], m_gs)
    m_boa = [None] * NT
    m_boa = [None] * NT
    for t in range(NKV):
        xt, dstT, slot_off = tile_info(t)
        mEvH = kv_tile_A(t, dstT, slot_off)
        if t + 1 < NKV:
            nxt = t + 1
            new_group = (nxt < NT and nxt % 4 == 0) or (
                nxt >= NT and (nxt - NT) % 2 == 0)
            if new_group:
                m_gs = tile_prep(nxt)
            emit_xln(nxt, tile_info(nxt)[0], m_gs)
        kv_tile_B(t, dstT, slot_off, mEvH)
        if t < NT:
            m_boa[t] = p.emit("vector", lambda e, xt=xt: e.tensor_add(
                xt, xt, bo_full[:]),
                waits=[(s_dve, mXdoneA[t]), (s_bo, 16)], inc=s_dve)
    wslot_free[0] = [(s_dve, mXdoneA[NKV - 1]), (s_dve, mXdoneV[NKV - 1])]
    wslot_free[1] = list(wslot_free[0])
    emit_kv(NKV - 1)

    def exch_local(e):
        pid = e.partition_id()
        slot = pid % 2
        e.wait_ge(s_nz.h, 16)
        e.wait_ge(s_dve.h, mAdd[NKV - 1])
        e.dma_start(out=kvx.ap()[bass.ts(slot, 128), :], in_=kv_sb[:]).then_inc(
            s_kvs.h, 16)
        e.wait_ge(s_kvs.h, 16)
        return e.dma_start(out=kvo_sb, in_=kvx.ap()[bass.ts(slot, 128), :])

    def exch(e):
        pid = e.partition_id()
        slot = pid % 2
        other = 1 - slot
        e.wait_ge(s_nz.h, 16)
        e.wait_ge(s_dve.h, mAdd[NKV - 1])
        e.dma_start(out=kvx.ap()[bass.ts(slot, 128), :], in_=kv_sb[:]).then_inc(
            s_kvs.h, 16)
        e.wait_ge(s_kvs.h, 16)
        e.dma_start(out=kvf.ap()[bass.ts(slot, 1), :], in_=snz[:]).then_inc(
            s_kvs.h, 16)
        e.wait_ge(s_kvs.h, 32)
        with (e.register("r_it") as r_it, e.register("r_tgt") as r_tgt):
            e.reg_mov(r_it, 0)
            e.reg_mov(r_tgt, 0)

            def cond():
                e.dma_start(out=sfl[:], in_=kvf.ap()[bass.ts(other, 1), :]
                            ).then_inc(s_pl.h, 16)
                e.reg_add(r_tgt, r_tgt, 16)
                e.wait_ge(s_pl.h, r_tgt)
                with (e.register("r_c") as r_c, e.register("r_v") as r_v,
                      e.register("r_q") as r_q, e.register("r_nd") as r_nd,
                      e.register("r_un") as r_un, e.register("r_go") as r_go):
                    e.reg_mov(r_c, 0)
                    for i in range(4):
                        e.load(r_v, sfl[0:1, i : i + 1])
                        e.load(r_q, snz[0:1, i : i + 1])
                        e.reg_alu(r_q, r_v, r_q, ALU.is_equal)
                        e.reg_alu(r_c, r_c, r_q, ALU.add)
                    e.reg_add(r_it, r_it, 1)
                    e.reg_alu(r_nd, r_c, 4, ALU.is_lt)
                    e.reg_alu(r_un, r_it, 200000, ALU.is_lt)
                    e.reg_alu(r_go, r_nd, r_un, ALU.logical_and)
                    return r_go

            with e.While(cond):
                pass
        return e.dma_start(out=kvo_sb, in_=kvx.ap()[bass.ts(other, 128), :])

    qf_sb = big_a[:].rearrange("p (c t) -> p c t", c=8)
    mQstt = [[None] * 8 for _ in range(4)]
    for g in range(4):
        for co in range(8):
            bank = qbanks[co % 4]
            w = []
            if g == 0 and co == 0:
                w = [(s_act, m_wq_done[0]), (s_pool, m_wq_done[1]),
                     (s_dve, mKstt[NKV - 1]), (s_act, mVev[NKV - 1])]
            gco = g * 8 + co
            if gco >= 4:
                pg, pc = divmod(gco - 4, 8)
                w = w + [(s_dve, mQstt[pg][pc])]
            for cidx in range(8):
                p.emit("tensor", lambda e, bank=bank, cidx=cidx, co=co, g=g: e.matmul(
                    bank[:],
                    wq_wo[:, cidx * 1024 + co * 128 : cidx * 1024 + co * 128 + 128],
                    xlnT[:, cidx, g * 512 : (g + 1) * 512],
                    start=(cidx == 0), stop=(cidx == 7)),
                    waits=w if cidx == 0 else [],
                    inc=s_pe if cidx == 7 else None)
            mQ = s_pe.n
            gco = g * 8 + co
            qw = [(s_pe, mQ)]
            if gco == 0:
                qw.append((s_dve, mKstt[NKV - 1]))
            else:
                qw.append((s_dve, mQstt[(gco - 1) // 8][(gco - 1) % 8]))
            mQstt[g][co] = emit_eluplus1(
                p, [(bank, qf_sb[:, co, g * 512 : g * 512 + 512], 0, qw)],
                S1, S2, None, s_dve, s_act)[-1]
    mQlast_pe = s_pe.n
    mQlast_dve = mQstt[3][7]

    do = stream_weight(wo_in, cast_consumer(
        lambda c: wq_wo[:, c * 1024 : (c + 1) * 1024],
        extra=[(s_pe, mQlast_pe)]))
    m_wo_done = (do[s_act], do[s_pool])

    def beta_consumer(c, src_ap, waits):
        m = p.emit("vector", lambda e, src_ap=src_ap, c=c: e.reduce_sum(
            bsum[:, c : c + 1], src_ap, axis=AXX, apply_absolute_value=True),
            waits=waits, inc=s_dve)
        return [(s_dve, m)]
    df1 = stream_weight(wf_in, beta_consumer)
    m_beta_parts = df1[s_dve]

    p.emit("sync", lambda e: e.dma_start(out=onesr, in_=ones_in),
           waits=[(s_dve, mXdoneA[NKV - 1]), (s_dve, mXdoneV[NKV - 1])],
           inc=s_ones, n=16)
    p.emit("sync", exch_local if sim_local_exchange else exch,
           inc=s_kvr, n=16)
    m_tot = p.emit("vector", lambda e: e.tensor_add(
        kv_sb[:], kv_sb[:], kvo_sb), waits=[(s_kvr, 16)], inc=s_dve)
    m_z = p.emit("gpsimd", lambda e: e.memset(R_bf, 0.0),
                 waits=[(s_xe[7], 16), (s_dve, mXdoneA[NKV - 1]),
                        (s_dve, mXdoneV[NKV - 1])])
    mR = None
    for cc in range(8):
        for pr in range(2):
            rs = slice(pr * 64, pr * 64 + 64)
            dcol = 128 * cc + pr * 64
            scol = 65 * cc
            p.emit("gpsimd", lambda e, rs=rs, dcol=dcol, scol=scol: e.tensor_copy(
                R_bf[rs, dcol : dcol + 64], kv_sb[rs, scol : scol + 64]),
                waits=[(s_dve, m_tot)] if (cc == 0 and pr == 0) else [])
            mR = p.emit("gpsimd", lambda e, rs=rs, cc=cc, pr=pr, scol=scol: e.tensor_copy(
                R_bf[rs, 1024 + 2 * cc + pr : 1024 + 2 * cc + pr + 1],
                kv_sb[rs, scol + 64 : scol + 65]))
    for i in range(2):
        p.emit("sync", lambda e, i=i: e.dma_start(
            out=selm_sb[i], in_=selm_in[16 * i : 16 * i + 16, :]),
            waits=[(s_pe, mKv[NKV - 1])] if i == 0 else [], inc=s_msk, n=16)
    m_msk = s_pool.n

    def xn_store(t):
        if t < 12:
            return xlnT[:, 2 + t // 2, (t % 2) * 1024 : (t % 2) * 1024 + 1024]
        if t < 14:
            return v_ext[t - 12][:, 0:1024]
        return oh_bf[t - 14][:]

    NPAIR = NT // 2
    mNum = [None] * NPAIR
    mDiv1 = [None] * NPAIR
    mDiv2 = [None] * NPAIR
    mWo = [None] * NT
    mRes = [None] * NT
    mEvW = [None] * NT
    mNe_r2_prev = [None]
    mSq2 = [None] * NT
    st3 = {"sums": l3sum, "sumsq": l3sumsq, "ex2": st["ex2"],
           "nvar": st["nvar"], "lnv": st["lnv"], "mean": mean3,
           "rstd": rstd3, "nmr": nm3r}
    m_xg_prev = None
    m_ga_prev = [None, None]
    num_banks = (T0, KA[0], T1, KA[1])

    def attn_pair(pp):
        buf = pp % 2
        t0 = 2 * pp
        g = pp // 2
        for tk in range(2):
            for ch in range(8):
                w = []
                if tk == 0 and ch == 0:
                    if pp == 0:
                        w = [(s_pool, mR), (s_pool, m_msk),
                             (s_dve, mQlast_dve)]
                    else:
                        w = [(s_dve, mDiv1[pp - 1])]
                    w.append((s_dve, mQstt[g][7]))
                p.emit("tensor", lambda e, tk=tk, ch=ch, t0=t0: e.matmul(
                    VA[0][:, 16 * tk + 2 * ch : 16 * tk + 2 * ch + 2],
                    qf_sb[:, ch, (t0 + tk) * 128 : (t0 + tk) * 128 + 128],
                    R_bf[:, 1024 + 2 * ch : 1024 + 2 * ch + 2],
                    start=True, stop=True),
                    waits=w,
                    inc=s_pe if (tk == 1 and ch == 7) else None)
        mDen = s_pe.n
        for cc in range(8):
            bank = num_banks[cc // 2]
            coff = (cc % 2) * 256
            w = []
            if cc == 0 and pp >= 1:
                w = [(s_dve, mDiv2[pp - 1])]
            if cc == 4 and pp >= 1:
                w = [(s_act, mNe_r2_prev[0])]
            p.emit("tensor", lambda e, bank=bank, coff=coff, cc=cc, t0=t0: e.matmul(
                bank[:, coff : coff + 256],
                R_bf[:, 128 * cc : 128 * cc + 128],
                qf_sb[:, cc, t0 * 128 : t0 * 128 + 256],
                start=True, stop=True),
                waits=w, inc=s_pe if cc == 7 else None)
        mNum[pp] = s_pe.n
        p.emit("vector", lambda e: e.tensor_scalar(
            tmp32, VA[0][:, 0:32], DEN_EPS, None, ALU.add),
            waits=[(s_pe, mDen)], inc=s_dve)
        m_rcs = p.emit("vector", lambda e: e.reciprocal(tmp32, tmp32),
                       inc=s_dve)
        if pp >= 1:
            wo_tile(pp - 1, 1)
        for tk in range(2):
            p.emit("tensor", lambda e, tk=tk: e.matmul(
                VA[1][0:16, 128 * tk : 128 * tk + 128],
                tmp32[:, 16 * tk : 16 * tk + 16], identf[:],
                start=True, stop=True),
                waits=[(s_dve, m_rcs), (s_idf, 16)] if tk == 0 else [],
                inc=s_pe if tk == 1 else None)
        m_tpd = s_pe.n
        m_cpT = p.emit("vector", lambda e: e.tensor_copy(
            dvT_sb, VA[1][0:16, 0:256]), waits=[(s_pe, m_tpd)], inc=s_dve)
        mNe_r1 = None
        for rnd in range(2):
            for j in range(2):
                nb = num_banks[2 * rnd + j]
                ew = []
                if j == 0:
                    ew = ([(s_pe, mNum[pp])] if rnd == 0
                          else [(s_dve, mDiv1[pp])])
                mNe = p.emit("scalar", lambda e, nb=nb, j=j: e.activation(
                    num_sb[:, j * 512 : j * 512 + 512], nb[:], AF.Identity,
                    bias=0.0, scale=1.0), waits=ew, inc=s_act)
            if rnd == 0:
                mNe_r1 = mNe
            else:
                mNe_r2_prev[0] = mNe
            for cc in range(4):
                ch = 4 * rnd + cc
                sv = selm_sb[ch // 4][0:16, 128 * (ch % 4) : 128 * (ch % 4) + 128]
                dbank = (VA[cc // 2] if rnd == 0 else num_banks[cc // 2])
                w = []
                if cc == 0:
                    w = [(s_dve, m_cpT), (s_msk, 32)]
                    if rnd == 1:
                        w.append((s_act, mNe_r1))
                p.emit("tensor", lambda e, ch=ch, cc=cc, sv=sv, dbank=dbank: e.matmul(
                    dbank[:, 256 * (ch % 2) : 256 * (ch % 2) + 256],
                    sv, dvT_sb,
                    start=True, stop=True),
                    waits=w, inc=s_pe if cc == 3 else None)
            mB = s_pe.n
            for j in range(2):
                dst = ohT[buf][:, 4 * rnd + 2 * j : 4 * rnd + 2 * j + 2,
                               :].rearrange("p c x -> p (c x)")
                db = (VA[j] if rnd == 0 else num_banks[j])
                dw = [(s_act, mNe)] if j == 0 else []
                if j == 0:
                    dw.append((s_pe, mB))
                if rnd == 0 and j == 0:
                    dw.append((s_pe, mQlast_pe) if pp <= 1
                              else (s_pe, mWo[2 * (pp - 2) + 1]))
                m = p.emit("vector", lambda e, dst=dst, j=j, db=db: e.tensor_tensor(
                    dst, num_sb[:, j * 512 : j * 512 + 512], db[:, 0:512],
                    ALU.mult), waits=dw, inc=s_dve)
            if rnd == 0:
                mDiv1[pp] = m
            else:
                mDiv2[pp] = m
        wo_tile(pp, 0)

    def wo_tile(pp, tk):
        buf = pp % 2
        t = 2 * pp + tk
        xt = x_sb[:, t, :]
        for half in range(2):
            bank = WB2[half]
            for cidx in range(8):
                w = []
                if half == 0 and cidx == 0:
                    w = [(s_act, m_wo_done[0]), (s_pool, m_wo_done[1])]
                    if tk == 0:
                        w.append((s_dve, mDiv1[pp]))
                    if t == 0:
                        w.append((s_dve, mAdd[NKV - 1]))
                    if t >= 1:
                        w.append((s_dve, mEvW[t - 1]))
                if half == 0 and cidx == 4 and tk == 0:
                    w = [(s_dve, mDiv2[pp])]
                p.emit("tensor", lambda e, bank=bank, cidx=cidx, half=half,
                       buf=buf, tk=tk: e.matmul(
                    bank[:], ohT[buf][:, cidx, tk * 128 : tk * 128 + 128],
                    wq_wo[:, cidx * 1024 + half * 512 :
                          cidx * 1024 + half * 512 + 512],
                    start=(cidx == 0), stop=(cidx == 7)),
                    waits=w, inc=s_pe if cidx == 7 else None)
            if half == 0:
                mWoH0 = s_pe.n
        mWo[t] = s_pe.n
        for half in range(2):
            bank = WB2[half]
            w = [(s_pe, mWoH0 if half == 0 else mWo[t])]
            if half == 0:
                w.append((s_dve, m_boa[t]))
            acc = (st["sums"] if half == 0 else sums_b)[:, t : t + 1]
            mRes[t] = p.emit("vector", lambda e, bank=bank, half=half, xt=xt,
                             acc=acc: e.scalar_tensor_tensor(
                xt[:, half * 512 : half * 512 + 512], bank[:], 1.0,
                xt[:, half * 512 : half * 512 + 512], ALU.mult, ALU.add,
                accum_out=acc), waits=w, inc=s_dve)
        mEvW[t] = mRes[t]
        mSq2[t] = p.emit("scalar", lambda e, xt=xt, t=t: e.activation(
            S0[:], xt, AF.Square, bias=0.0, scale=1.0,
            accum_out=st["sumsq"][:, t : t + 1]),
            waits=[(s_dve, mRes[t])], inc=s_act)

    def group_chain(g):
        nonlocal m_xg_prev
        c4 = slice(4 * g, 4 * g + 4)
        p.emit("vector", lambda e, c4=c4: e.tensor_add(
            st["sums"][:, c4], st["sums"][:, c4], sums_b[:, c4]), inc=s_dve)
        m_gs2 = emit_group_stats(p, c4, st, s_dve, s_act, EPS12, cw=const_w,
                                 sums_w=(s_act, mSq2[4 * g + 3]))
        m_x2t_last = None
        for slot in range(4):
            t = 4 * g + slot
            c = slice(t, t + 1)
            xw = [(s_dve, m_gs2)]
            if m_xg_prev is not None:
                xw.append((s_dve, m_xg_prev))
            m_x2t = p.emit("scalar", lambda e, t=t, c=c: e.activation(
                S2[:], x_sb[:, t, :], AF.Identity, bias=st["nmr"][:, c],
                scale=st["rstd"][:, c], accum_out=l3sum[:, c]),
                waits=xw, inc=s_act)
            m_x2t_last = p.emit("scalar", lambda e, c=c: e.activation(
                S2[:], S2[:], AF.Square, bias=0.0, scale=1.0,
                accum_out=l3sumsq[:, c]), inc=s_act)
        m_gs3 = emit_group_stats(p, c4, st3, s_dve, s_act, EPS3,
                                 sums_w=(s_act, m_x2t_last))
        p.emit("vector", lambda e, c4=c4: e.tensor_mul(
            ab[:, c4], st["rstd"][:, c4], rstd3[:, c4]),
            waits=[(s_dve, m_gs3)], inc=s_dve)
        p.emit("vector", lambda e, c4=c4: e.tensor_sub(
            u3[:, 0:4], st["nmr"][:, c4], mean3[:, c4]), inc=s_dve)
        m_ab = p.emit("vector", lambda e, c4=c4, g=g: e.tensor_mul(
            ab[:, NT + 4 * g : NT + 4 * g + 4], u3[:, 0:4], rstd3[:, c4]),
            inc=s_dve)
        return m_ab

    def group_gamma(g, m_ab):
        nonlocal m_xg_prev
        for slot in range(4):
            t = 4 * g + slot
            p.emit("vector", lambda e, t=t: e.tensor_scalar(
                S2[:], x_sb[:, t, :], ab[:, t : t + 1],
                ab[:, NT + t : NT + t + 1], ALU.mult, ALU.add),
                waits=[(s_dve, m_ab), (s_gf, 16)], inc=s_dve)
            p.emit("vector", lambda e: e.scalar_tensor_tensor(
                S2[:], S2[:], 1.0, g_full[:], ALU.mult, ALU.mult), inc=s_dve)
            p.emit("vector", lambda e, t=t: e.tensor_add(
                xn_store(t), S2[:], b_full[:]), waits=[(s_bf, 16)], inc=s_dve)
            m_xg_prev = p.emit("vector", lambda e, t=t: e.reduce_max(
                gm[:, t : t + 1], xn_store(t), axis=AXX,
                apply_absolute_value=True), inc=s_dve)

    m_abs = {}
    for step in (("p", 0), ("p", 1), ("p", 2), ("a", 0), ("p", 3), ("b", 0),
                 ("p", 4), ("a", 1), ("p", 5), ("b", 1), ("p", 6), ("a", 2),
                 ("p", 7), ("b", 2)):
        if step[0] == "p":
            attn_pair(step[1])
        elif step[0] == "a":
            m_abs[step[1]] = group_chain(step[1])
        else:
            group_gamma(step[1], m_abs[step[1]])
    wo_tile(NPAIR - 1, 1)

    m_bs1 = p.emit("vector", lambda e: e.reduce_sum(bsum1, bsum, axis=AXX),
                   waits=[(s_dve, m_beta_parts)], inc=s_dve)
    m_bmm = p.emit("tensor", lambda e: e.matmul(
        BC[0:1, 0:1], bsum1, cstb[:, 5:6], start=True, stop=True),
        waits=[(s_dve, m_bs1), const_w, (s_dve, mDiv1[NPAIR - 1])], inc=s_pe)
    p.emit("vector", lambda e: e.tensor_scalar(
        scal[0:1, 2:3], BC[0:1, 0:1], 1.0 / (D * D), None, ALU.mult),
        waits=[(s_pe, m_bmm)], inc=s_dve)
    p.emit("vector", lambda e: e.tensor_scalar(
        scal[0:1, 3:4], scal[0:1, 2:3], BETA_EPS, None, ALU.add), inc=s_dve)
    m_sbe_sc = p.emit("vector", lambda e: e.reciprocal(
        scal[0:1, 3:4], scal[0:1, 3:4]), inc=s_dve)
    m_sbe_mm = p.emit("tensor", lambda e: e.matmul(
        BC[:, 0:1], onesr, scal[0:1, 3:4], start=True, stop=True),
        waits=[(s_dve, m_sbe_sc), (s_ones, 16)], inc=s_pe)
    m_sbe_cp = p.emit("vector", lambda e: e.tensor_copy(
        sbe_b, BC[:, 0:1]), waits=[(s_pe, m_sbe_mm)], inc=s_dve)

    nsb_i = num_sb[:].bitcast(I32)

    q_prev = {}

    def quant_consumer(c, src_ap, waits):
        w = list(waits)
        if c == 0:
            w += [(s_dve, m_sbe_cp), (s_pe, mNum[NPAIR - 1]),
                  (s_dve, mDiv2[NPAIR - 1])]
        else:
            w += [(s_dve, q_prev["dve"]), (s_pool, q_prev["pool"])]
        m_sc = p.emit("scalar", lambda e, src_ap=src_ap: e.activation(
            num_sb[:], src_ap, AF.Identity, bias=0.0, scale=sbe_b),
            waits=w, inc=s_act)
        p.emit("vector", lambda e: e.tensor_copy(nsb_i, num_sb[:]),
               waits=[(s_act, m_sc)], inc=s_dve)
        m_cv2 = p.emit("vector", lambda e: e.tensor_copy(num_sb[:], nsb_i),
                       inc=s_dve)
        p.emit("gpsimd", lambda e, c=c: e.tensor_scalar(
            big_a[:, c * 1024 : (c + 1) * 1024], num_sb[:],
            1.0, -1.0, ALU.min, ALU.max), waits=[(s_dve, m_cv2)])
        q_prev["dve"] = m_cv2
        q_prev["pool"] = s_pool.n
        return [(s_pool, s_pool.n)]
    df2 = stream_weight(wf_in, quant_consumer)
    m_wq2 = df2[s_pool]

    group_gamma(3, group_chain(3))
    m_gm_all = s_dve.n


    p.emit("vector", lambda e: e.reduce_max(gmx, gm, axis=AXX), inc=s_dve)
    m_tp = p.emit("tensor", lambda e: e.matmul(
        T0[0:1, 0:128], gmx, identf[:], start=True, stop=True),
        waits=[(s_dve, s_dve.n), (s_idf, 16)],
        inc=s_pe)
    p.emit("vector", lambda e: e.reduce_max(
        scal[0:1, 0:1], T0[0:1, 0:128], axis=AXX), waits=[(s_pe, m_tp)],
        inc=s_dve)
    p.emit("vector", lambda e: e.reciprocal(scal[0:1, 1:2], scal[0:1, 0:1]),
           inc=s_dve)
    p.emit("vector", lambda e: e.tensor_scalar(
        scal[0:1, 1:2], scal[0:1, 1:2], 128.0, None, ALU.mult), inc=s_dve)
    p.emit("vector", lambda e: e.tensor_mul(
        scal[0:1, 4:5], scal[0:1, 0:1], scal[0:1, 2:3]), inc=s_dve)
    m_sy = p.emit("vector", lambda e: e.tensor_scalar(
        scal[0:1, 4:5], scal[0:1, 4:5], 1.0 / 128.0, None, ALU.mult), inc=s_dve)
    m_prev = m_sy
    for col, dstb in ((1, sxq_b), (4, sy_b)):
        m_pe = p.emit("tensor", lambda e, col=col: e.matmul(
            BC[:, 0:1], onesr, scal[0:1, col : col + 1],
            start=True, stop=True),
            waits=[(s_dve, m_prev), (s_ones, 16)], inc=s_pe)
        m_prev = p.emit("vector", lambda e, dstb=dstb: e.tensor_copy(
            dstb, BC[:, 0:1]), waits=[(s_pe, m_pe)], inc=s_dve)
    m_bcasts = m_prev

    if dbg:
        s_dbg = p.sem("s_dbg")
        for t in range(NT):
            p.emit("sync", lambda e, t=t: e.dma_start(
                out=x2_dbg[128 * t : 128 * (t + 1), :], in_=x_sb[:, t, :]),
                waits=[(s_dve, m_gm_all)] if t == 0 else [], inc=s_dbg, n=16)
        p.emit("sync", lambda e: e.dma_start(
            out=scr_dbg, in_=xe_sb[:, 0, :]), waits=[(s_dve, m_bcasts)],
            inc=s_dbg, n=16)
        for j, kk in enumerate(("sums", "sumsq", "mean", "rstd", "nmr")):
            p.emit("sync", lambda e, j=j, kk=kk: e.dma_start(
                out=st_dbg[:, 32 * j : 32 * (j + 1)], in_=st[kk][:]),
                waits=[(s_dve, m_bcasts)], inc=s_dbg, n=16)
        p.emit("sync", lambda e: e.dma_start(
            out=wq_dbg, in_=big_a[:, 0:8192]), waits=[(s_pool, m_wq2)],
            inc=s_dbg, n=16)

    w_q = big_a[:, 0:8192]
    mT3 = [None] * NT
    mEvT3 = [None] * NT
    mF = [None] * NT
    mOut = [None] * NT
    m_xq = [None] * NT
    for t in range(NT):
        w0 = [(s_dve, m_bcasts)]
        if t == 0 and dbg:
            w0.append((s_dbg, 16 * NT))
        m_clip = p.emit("vector", lambda e, t=t: e.tensor_scalar(
            S2[:], xn_store(t), sxq_b, -CLIP, ALU.mult, ALU.max),
            waits=w0 if t == 0 else [], inc=s_dve)
        m_xq[t] = p.emit("vector", lambda e, t=t: e.tensor_scalar(
            xn_store(t), S2[:], CLIP, None, ALU.min),
            waits=[(s_dve, m_clip)], inc=s_dve)
    for t in range(NT):
        par = t % 2
        xt = x_sb[:, t, :]
        tb3 = (T0, T1) if par == 0 else (KA[0], KA[1])
        fb3 = (KV[0], KV[1]) if par == 0 else (VA[0], VA[1])
        tw = [(s_dve, m_xq[t])]
        if t == 0:
            tw.append((s_pe, m_tp))
        if t >= 2:
            tw.append((s_act, mEvT3[t - 2]))
        mTh3 = [None, None]
        for cidx in range(8):
            bank = tb3[cidx // 4]
            col = (cidx % 4) * 128
            p.emit("tensor", lambda e, cidx=cidx, bank=bank, col=col, t=t: e.matmul(
                bank[:, col : col + 128],
                xn_store(t)[:, cidx * 128 : (cidx + 1) * 128],
                identb[:], start=True, stop=True),
                waits=tw if cidx == 0 else [],
                inc=s_pe if cidx % 4 == 3 else None)
            if cidx % 4 == 3:
                mTh3[cidx // 4] = s_pe.n
        mT3[t] = mTh3[1]
        mEv3H = [None, None]
        for b in range(2):
            src = tb3[b][:].rearrange("p (c x) -> p c x", c=4)
            dst = ohT[par][:, 4 * b : 4 * b + 4, 0:128]
            ew = [(s_pe, mTh3[b])]
            if b == 0 and t >= 2:
                ew.append((s_pe, mF[t - 2]))
            mEv3H[b] = p.emit("scalar", lambda e, src=src, dst=dst: e.activation(
                dst, src, AF.Identity, bias=0.0, scale=1.0),
                waits=ew, inc=s_act)
        mEvT3[t] = mEv3H[1]
        for half in range(2):
            bank = fb3[half]
            for cidx in range(8):
                w = []
                if half == 0 and cidx == 0:
                    w = [(s_act, mEv3H[0]), (s_pool, m_wq2)]
                    if t >= 2:
                        w.append((s_dve, mOut[t - 2]))
                if half == 0 and cidx == 4:
                    w = [(s_act, mEv3H[1])]
                p.emit("tensor", lambda e, bank=bank, cidx=cidx, half=half, par=par: e.matmul(
                    bank[:], ohT[par][:, cidx, 0:128],
                    w_q[:, cidx * 1024 + half * 512 :
                        cidx * 1024 + half * 512 + 512],
                    start=(cidx == 0), stop=(cidx == 7)),
                    waits=w, inc=s_pe if cidx == 7 else None)
            if half == 0:
                mFh0 = s_pe.n
        mF[t] = s_pe.n
        for half in range(2):
            bank = fb3[half]
            w = [(s_pe, mFh0 if half == 0 else mF[t])]
            mOut[t] = p.emit("vector", lambda e, bank=bank, half=half, xt=xt: e.scalar_tensor_tensor(
                xt[:, half * 512 : half * 512 + 512], bank[:], sy_b,
                xt[:, half * 512 : half * 512 + 512], ALU.mult, ALU.add),
                waits=w, inc=s_dve)
        dst = out_o[128 * t : 128 * (t + 1), :]
        p.emit("sync", lambda e, dst=dst, t=t: e.dma_start(
            out=dst, in_=x_sb[:, t, :]), waits=[(s_dve, mOut[t])],
            inc=s_out, n=16)
    p.emit("sync", lambda e: e.wait_ge(s_out.h, 16 * NT))
    if dbg:
        p.emit("sync", lambda e: e.wait_ge(s_dbg.h, 16 * (NT + 7)))
    return p



def _np(a, dt=np.float32):
    return np.ascontiguousarray(np.asarray(a, dtype=dt))


def kernel(L0_attn_x, L0_attn_wq, L0_attn_wk, L0_attn_wv, L0_attn_wo,
           L0_attn_bo, L0_ffn_weight, L0_ffn_ln_gamma, L0_ffn_ln_beta):
    x = _np(L0_attn_x).reshape(NCORES * TPC, D)
    wqT = _np(np.asarray(L0_attn_wq).T)
    wkT = _np(np.asarray(L0_attn_wk).T)
    wvT = _np(np.asarray(L0_attn_wv).T)
    woT = _np(np.asarray(L0_attn_wo).T)
    wf = _np(L0_ffn_weight)
    bo_full = np.ascontiguousarray(np.broadcast_to(_np(L0_attn_bo), (128, D)))
    g_full = np.ascontiguousarray(np.broadcast_to(_np(L0_ffn_ln_gamma), (128, D)))
    b_full = np.ascontiguousarray(np.broadcast_to(_np(L0_ffn_ln_beta), (128, D)))
    identb = np.eye(128, dtype=ml_dtypes.bfloat16)
    identf = np.eye(128, dtype=np.float32)
    ones128 = np.ones((1, 128), np.float32)
    cst = host_consts()
    nz = np.random.default_rng().integers(1, 2**31 - 1, size=(1, 4),
                                          dtype=np.int32)
    selm = np.zeros((32, 512), np.float32)
    for ch in range(8):
        for q in range(2):
            selm[16 * (ch // 4) + 2 * ch + q,
                 128 * (ch % 4) + 64 * q : 128 * (ch % 4) + 64 * q + 64] = 1.0
    xs = [np.ascontiguousarray(x[i * TPC : (i + 1) * TPC]) for i in range(NCORES)]

    nc = build_F().finalize()
    maps = []
    for i in range(NCORES):
        b = i // 4
        j = 4 * b + (i - 4 * b + 2) % 4
        maps.append(dict(x=xs[i], xe=xs[j], wqT=wqT, wkT=wkT, wvT=wvT,
                         woT=woT, wf=wf, bo_full=bo_full, g_full=g_full,
                         b_full=b_full, identb=identb, identf=identf,
                         ones128=ones128, cst=cst, selm=selm, nz=nz))
    res = run_bass_kernel_spmd(nc, maps, core_ids=list(range(NCORES))).results
    out = np.concatenate([res[i]["out"] for i in range(NCORES)], axis=0)
    return np.ascontiguousarray(out.reshape(2, NCORES * TPC // 2, D),
                                dtype=np.float32)
